# version 02
# baseline (speedup 1.0000x reference)
import numpy as np
import ml_dtypes
from contextlib import ExitStack

import concourse.bass as bass
import concourse.mybir as mybir
from concourse.bass_utils import run_bass_kernel_spmd
from concourse.alu_op_type import AluOpType as Op

F32 = mybir.dt.float32
BF16 = mybir.dt.bfloat16
I32 = mybir.dt.int32
AF = mybir.ActivationFunctionType
BF = ml_dtypes.bfloat16

ENGS = ("sync", "tensor", "vector", "scalar", "gpsimd")

D = 2048
TLOC = 2048
NDC = 16


class Prog:
    def __init__(self):
        self.mode = "count"
        self.marks = {}
        self.reset_counters()
        self.sems = {}

    def reset_counters(self):
        self.ccount = {e: 0 for e in ENGS}
        self.dcount = {e: 0 for e in ENGS}
        self.ncoll = 0
        self.watermark = {}


class EngCtx:
    def __init__(self, prog: Prog, name: str, eng):
        self.p, self.name, self.eng = prog, name, eng

    def run(self, fn):
        if self.p.mode == "emit":
            fn(self.eng)

    def sig(self, fn, name=None):
        p = self.p
        p.ccount[self.name] += 1
        cnt = p.ccount[self.name]
        if p.mode == "count":
            if name is not None:
                assert name not in p.marks, f"duplicate mark {name}"
                p.marks[name] = ("c", self.name, cnt)
        else:
            if name is not None:
                assert p.marks[name] == ("c", self.name, cnt), \
                    f"pass mismatch at {name}"
            fn(self.eng).then_inc(p.sems[("c", self.name)], 1)

    def dma(self, fn):
        p = self.p
        p.dcount[self.name] += 16
        if p.mode == "emit":
            fn(self.eng).then_inc(p.sems[("d", self.name)], 16)

    def dflush(self, name):
        p = self.p
        p.ccount[self.name] += 1
        cnt = p.ccount[self.name]
        if p.mode == "count":
            assert name not in p.marks, f"duplicate mark {name}"
            p.marks[name] = ("c", self.name, cnt)
        else:
            assert p.marks[name] == ("c", self.name, cnt), f"pass mismatch at {name}"
            if p.dcount[self.name] > 0:
                self.eng.wait_ge(p.sems[("d", self.name)], p.dcount[self.name])
            self.eng.nop().then_inc(p.sems[("c", self.name)], 1)

    def fence(self):
        p = self.p
        if p.mode == "emit" and p.ccount[self.name] > 0:
            key = (self.name, "c", self.name)
            cnt = p.ccount[self.name]
            if p.watermark.get(key, -1) < cnt:
                self.eng.wait_ge(p.sems[("c", self.name)], cnt)
                p.watermark[key] = cnt

    def need(self, name):
        p = self.p
        if p.mode == "count":
            assert True
            return
        kind, peng, cnt = p.marks[name]
        if peng == self.name and kind == "c":
            return
        key = (self.name, kind, peng)
        if p.watermark.get(key, -1) >= cnt:
            return
        self.eng.wait_ge(p.sems[(kind, peng)], cnt)
        p.watermark[key] = cnt

    def coll(self, fn, name):
        p = self.p
        p.ccount[self.name] += 1
        cnt = p.ccount[self.name]
        ci = p.ncoll
        p.ncoll += 1
        if p.mode == "count":
            assert name not in p.marks
            p.marks[name] = ("c", self.name, cnt)
        else:
            assert p.marks[name] == ("c", self.name, cnt)
            fn(self.eng).then_inc(p.sems[("cc", ci)], 1)
            self.eng.wait_ge(p.sems[("cc", ci)], 1)
            self.eng.nop().then_inc(p.sems[("c", self.name)], 1)

    def flush_all(self):
        p = self.p
        if p.mode == "emit" and p.dcount[self.name] > 0:
            key = (self.name, "d", self.name)
            cnt = p.dcount[self.name]
            if p.watermark.get(key, -1) < cnt:
                self.eng.wait_ge(p.sems[("d", self.name)], cnt)
                p.watermark[key] = cnt

    def wait_done(self):
        p = self.p
        if p.mode == "emit" and p.dcount[self.name] > 0:
            self.eng.wait_ge(p.sems[("d", self.name)], p.dcount[self.name])


def build_program(nc, emitters):
    p = Prog()
    for e in ENGS:
        if e in emitters:
            emitters[e](EngCtx(p, e, None))
    p.mode = "emit"
    p.reset_counters()
    with ExitStack() as st:
        for e in ENGS:
            p.sems[("c", e)] = st.enter_context(nc.semaphore(f"c_{e}"))
            p.sems[("d", e)] = st.enter_context(nc.semaphore(f"d_{e}"))
        for ci in range(4):
            p.sems[("cc", ci)] = st.enter_context(nc.semaphore(f"cc_{ci}"))
        with nc.Block() as blk:
            for ename in ENGS:
                if ename not in emitters:
                    continue
                getattr(blk, ename)(_mk_body(p, ename, emitters[ename]))
    return p


class PrefCtx:

    def __init__(self, e, pre):
        self.e, self.pre = e, pre

    def run(self, fn):
        self.e.run(fn)

    def sig(self, fn, name=None):
        self.e.sig(fn, (self.pre + name) if name else None)

    def dma(self, fn):
        self.e.dma(fn)

    def dflush(self, name):
        self.e.dflush(self.pre + name)

    def need(self, name):
        self.e.need(self.pre + name)

    def fence(self):
        self.e.fence()


class FusedIO:
    pass


def _shared_const(nc, fused, name, shape):
    if fused is not None:
        cache = getattr(fused, "_consts", None)
        if cache is None:
            cache = {}
            fused._consts = cache
        if name in cache:
            return cache[name]
    t = nc.dram_tensor(name, shape, F32, kind="ExternalInput").ap()
    if fused is not None:
        cache[name] = t
    return t


def _mk_body(p, ename, emitter):
    def body(engine):
        ec = EngCtx(p, ename, engine)
        emitter(ec)
        ec.wait_done()
    return body




HID = 5632
NHC = 44
FFN_G = 4
GHC = NHC // FFN_G
NT = 4
TT = 512


def build_ffn0(fused=None):
    nc = bass.Bass("TRN2", target_bir_lowering=False) if fused is None else fused.nc
    if fused is None:
        x1 = nc.dram_tensor("x1", [NDC, 128, TLOC], F32, kind="ExternalInput").ap()
    else:
        x1 = fused.x1
    wg = nc.dram_tensor("wg", [NHC, 128, NDC * 128], I32, kind="ExternalInput").ap()
    wu = nc.dram_tensor("wu", [NHC, 128, NDC * 128], I32, kind="ExternalInput").ap()
    wd = nc.dram_tensor("wd", [NHC, 128, NDC * 128], I32, kind="ExternalInput").ap()
    sgc = nc.dram_tensor("sgc", [128, NHC], F32, kind="ExternalInput").ap()
    suc = nc.dram_tensor("suc", [128, NHC], F32, kind="ExternalInput").ap()
    sdc = nc.dram_tensor("sdc", [128, NDC], F32, kind="ExternalInput").ap()
    onec = _shared_const(nc, fused, "onec", [128, 1])
    cst = _shared_const(nc, fused, "cst", [128, 4])
    oner = _shared_const(nc, fused, "oner", [1, 128])
    if fused is None:
        x2 = nc.dram_tensor("x2", [NDC, 128, TLOC], F32, kind="ExternalOutput").ap()
    else:
        x2 = nc.dram_tensor("x2", [NDC, 128, TLOC], F32).ap()
        fused.x2 = x2
    hscr = nc.dram_tensor("hscr", [NHC, NT, 128, TT], BF16).ap()

    with ExitStack() as st:
        def sb(name, shape, dt):
            return st.enter_context(nc.sbuf_tensor(name, shape, dt,
                                                   allow_name_mangling=True))

        def psb(name, shape):
            return st.enter_context(nc.psum_tensor(name, shape, F32,
                                                   allow_name_mangling=True))

        xn = sb("xn", [128, NDC * TLOC], BF16)
        x1ln = sb("x1ln", [128, NDC * TT], F32)
        sq = sb("sq", [128, 2 * TT], F32)
        rows = sb("rows", [1, 6 * TT], F32)
        onec_s = sb("onec_s", [128, 1], F32)
        cst_s = sb("cst_s", [128, 4], F32)
        oner_s = sb("oner_s", [1, 128], F32)
        sgc_s = sb("sgc_s", [128, NHC], F32)
        suc_s = sb("suc_s", [128, NHC], F32)
        sdc_s = sb("sdc_s", [128, NDC], F32)
        wint = sb("wint", [128, NDC * 128], I32)
        wgbf = sb("wgbf", [128, 2 * NDC * 128], BF16)
        wubf = sb("wubf", [128, 2 * NDC * 128], BF16)
        wdbf = sb("wdbf", [128, GHC * NDC * 128], BF16)
        gt = sb("gt", [128, 2 * TT], BF16)
        ut = sb("ut", [128, 2 * TT], BF16)
        hb = sb("hb", [128, 2 * TT], BF16)
        appt = sb("appt", [128, TT], F32)
        hrd = sb("hrd", [128, GHC * TT], BF16)

        ps_sum = psb("ps_sum", [1, TT])
        ps_ssq = psb("ps_ssq", [1, TT])
        ps_bc = [psb(f"ps_bc{i}", [128, TT]) for i in range(2)]
        ps_g = [psb(f"ps_g{i}", [128, TT]) for i in range(2)]
        ps_u = [psb(f"ps_u{i}", [128, TT]) for i in range(2)]

        def row(name):
            base = {"nmu": 0, "m2": 1, "t1": 2, "var": 3, "rstd": 4, "mrs": 5}[name]
            return rows[0:1, base * TT:(base + 1) * TT]

        def xns(dc, tau):
            return xn[:, (dc * NT + tau) * TT:(dc * NT + tau + 1) * TT]

        def x1s(dc):
            return x1ln[:, dc * TT:(dc + 1) * TT]

        def e_sync(e):
            for tau in range(NT):
                if tau > 0:
                    e.need(f"lnapply_{tau-1}")
                    e.need(f"stats_{tau-1}")
                    e.need(f"sq_{tau-1}_{NDC-1}")
                for dc in range(NDC):
                    e.dma(lambda g, dc=dc, tau=tau: g.dma_start(
                        out=x1s(dc), in_=x1[dc, :, tau * TT:(tau + 1) * TT]))
                e.dflush(f"x1ld_{tau}")
            for g2 in range(FFN_G):
                for tau in range(NT):
                    k = g2 * NT + tau
                    if k >= 1:
                        pg, ptau = divmod(k - 1, NT)
                        e.need(f"dmm_{pg}_{ptau}_{NDC-1}")
                    e.need(f"hstf_{g2}")
                    for ic in range(GHC):
                        e.dma(lambda g, g2=g2, tau=tau, ic=ic: g.dma_start(
                            out=hrd[:, ic * TT:(ic + 1) * TT],
                            in_=hscr[g2 * GHC + ic, tau]))
                    e.dflush(f"hld_{g2}_{tau}")
                    if g2 == 0:
                        if tau == 0:
                            e.need(f"lnapply_{NT-1}")
                            e.need(f"stats_{NT-1}")
                            e.need(f"sq_{NT-1}_{NDC-1}")
                        else:
                            e.need(f"x2f_{0}_{tau-1}")
                        for dc in range(NDC):
                            e.dma(lambda g, dc=dc, tau=tau: g.dma_start(
                                out=x1s(dc), in_=x1[dc, :, tau * TT:(tau + 1) * TT]))
                        e.dflush(f"x1r_{tau}")

        def e_scalar(e):
            e.need("const_ld")
            for tau in range(NT):
                e.need(f"x1ld_{tau}")
                for dc in range(NDC):
                    if dc >= 2:
                        e.need(f"ssqmm_{tau}_{dc-2}")
                    elif tau > 0:
                        e.need(f"stats_{tau-1}")
                    e.sig(lambda s, dc=dc: s.activation(
                        out=sq[:, (dc % 2) * TT:((dc % 2) + 1) * TT],
                        in_=x1s(dc), func=AF.Square), f"sq_{tau}_{dc}")
                e.need(f"var_{tau}")
                if tau > 0:
                    e.need(f"rstdb_{tau-1}")
                e.sig(lambda s: s.activation(
                    out=row("t1"), in_=row("var"), func=AF.Ln, bias=cst_s[0:1, 0:1]))
                e.fence()
                e.sig(lambda s: s.activation(
                    out=row("rstd"), in_=row("t1"), func=AF.Exp, scale=-0.5),
                    f"rstd_{tau}")
            for hc in range(NHC):
                for tau in range(NT):
                    k = hc * NT + tau
                    pg = k % 2
                    e.need(f"gmm_{hc}_{tau}")
                    if k >= 2:
                        e.need(f"hmul_{(k-2)//NT}_{(k-2)%NT}")
                    e.sig(lambda s, hc=hc, pg=pg: s.activation(
                        out=gt[:, pg * TT:(pg + 1) * TT], in_=ps_g[pg][:, :],
                        func=AF.Silu, scale=sgc_s[:, hc:hc + 1]), f"gevac_{hc}_{tau}")
                    e.need(f"umm_{hc}_{tau}")
                    e.sig(lambda s, hc=hc, pg=pg: s.activation(
                        out=ut[:, pg * TT:(pg + 1) * TT], in_=ps_u[pg][:, :],
                        func=AF.Identity, scale=suc_s[:, hc:hc + 1]), f"uevac_{hc}_{tau}")

        def e_tensor(e):
            e.need("const_ld")
            for tau in range(NT):
                e.need(f"x1ld_{tau}")
                if tau > 0:
                    e.need(f"rowrd_{tau-1}")
                for dc in range(NDC):
                    e.run(lambda t, dc=dc: t.matmul(
                        ps_sum[:, :], lhsT=onec_s[:, :], rhs=x1s(dc),
                        start=(dc == 0), stop=(dc == NDC - 1)))
                for dc in range(NDC):
                    e.need(f"sq_{tau}_{dc}")
                    e.sig(lambda t, dc=dc: t.matmul(
                        ps_ssq[:, :], lhsT=onec_s[:, :],
                        rhs=sq[:, (dc % 2) * TT:((dc % 2) + 1) * TT],
                        start=(dc == 0), stop=(dc == NDC - 1)),
                        f"ssqmm_{tau}_{dc}" if dc < NDC - 1 else f"stats_{tau}")
                e.need(f"rstd_{tau}")
                e.need(f"mrs_{tau}")
                if tau > 0:
                    e.need(f"lnapply_{tau-1}")
                e.sig(lambda t: t.matmul(
                    ps_bc[0][:, :], lhsT=oner_s[:, :], rhs=row("rstd"),
                    start=True, stop=True), f"rstdb_{tau}")
                e.sig(lambda t: t.matmul(
                    ps_bc[1][:, :], lhsT=oner_s[:, :], rhs=row("mrs"),
                    start=True, stop=True), f"mrsb_{tau}")
            for g2 in range(FFN_G):
                for ihc in range(GHC):
                    hc = g2 * GHC + ihc
                    ws = (hc % 2) * NDC * 128
                    e.need(f"wgdq_{hc}")
                    e.need(f"wudq_{hc}")
                    for tau in range(NT):
                        k = hc * NT + tau
                        pg = k % 2
                        e.need(f"lnapply_{tau}")
                        if k >= 2:
                            e.need(f"gevac_{(k-2)//NT}_{(k-2)%NT}")
                        for dc in range(NDC):
                            e.sig(lambda t, dc=dc, tau=tau, pg=pg, ws=ws: t.matmul(
                                ps_g[pg][:, :],
                                lhsT=wgbf[:, ws + dc * 128:ws + (dc + 1) * 128],
                                rhs=xns(dc, tau),
                                start=(dc == 0), stop=(dc == NDC - 1)),
                                f"gmm_{hc}_{tau}" if dc == NDC - 1 else None)
                        if k >= 2:
                            e.need(f"uevac_{(k-2)//NT}_{(k-2)%NT}")
                        for dc in range(NDC):
                            e.sig(lambda t, dc=dc, tau=tau, pg=pg, ws=ws: t.matmul(
                                ps_u[pg][:, :],
                                lhsT=wubf[:, ws + dc * 128:ws + (dc + 1) * 128],
                                rhs=xns(dc, tau),
                                start=(dc == 0), stop=(dc == NDC - 1)),
                                f"umm_{hc}_{tau}" if dc == NDC - 1 else None)
                if g2 == 0:
                    e.need(f"lnapply_{NT-1}")
                for tau in range(NT):
                    e.need(f"hld_{g2}_{tau}")
                    for dc in range(NDC):
                        k = g2 * NT * NDC + tau * NDC + dc
                        pd = k % 2
                        if k >= 2:
                            pk = k - 2
                            e.need(f"devac_{pk//(NT*NDC)}_{(pk%(NT*NDC))//NDC}_{pk%NDC}")
                        for ic in range(GHC):
                            e.need(f"wddq_{g2}_{ic}")
                            e.sig(lambda t, dc=dc, ic=ic, pd=pd: t.matmul(
                                ps_bc[pd][:, :],
                                lhsT=wdbf[:, (ic * NDC + dc) * 128:(ic * NDC + dc + 1) * 128],
                                rhs=hrd[:, ic * TT:(ic + 1) * TT],
                                start=(ic == 0), stop=(ic == GHC - 1)),
                                f"dmm_{g2}_{tau}_{dc}" if ic == GHC - 1 else None)

        def e_vector(e):
            for tau in range(NT):
                e.need(f"stats_{tau}")
                e.fence()
                e.sig(lambda v: v.tensor_scalar(
                    out=row("nmu"), in0=ps_sum[:, :], scalar1=-1.0 / D,
                    scalar2=None, op0=Op.mult))
                e.sig(lambda v: v.tensor_scalar(
                    out=row("m2"), in0=ps_ssq[:, :], scalar1=1.0 / D,
                    scalar2=None, op0=Op.mult), f"rowrd_{tau}")
                e.fence()
                e.sig(lambda v: v.tensor_tensor(
                    out=row("t1"), in0=row("nmu"), in1=row("nmu"), op=Op.mult))
                e.fence()
                e.sig(lambda v: v.tensor_tensor(
                    out=row("var"), in0=row("m2"), in1=row("t1"), op=Op.subtract),
                    f"var_{tau}")
                e.need(f"rstd_{tau}")
                if tau > 0:
                    e.need(f"mrsb_{tau-1}")
                e.fence()
                e.sig(lambda v: v.tensor_tensor(
                    out=row("mrs"), in0=row("nmu"), in1=row("rstd"), op=Op.mult),
                    f"mrs_{tau}")
                e.need(f"rstdb_{tau}")
                e.need(f"mrsb_{tau}")
                for dc in range(NDC):
                    e.fence()
                    e.sig(lambda v, dc=dc: v.tensor_tensor(
                        out=appt[:, :], in0=x1s(dc), in1=ps_bc[0][:, :], op=Op.mult))
                    e.fence()
                    e.sig(lambda v, dc=dc, tau=tau: v.tensor_tensor(
                        out=xns(dc, tau), in0=appt[:, :], in1=ps_bc[1][:, :], op=Op.add),
                        f"lnapply_{tau}" if dc == NDC - 1 else None)
            for g2 in range(FFN_G):
                for ihc in range(GHC):
                    hc = g2 * GHC + ihc
                    ws = (hc % 2) * NDC * 128
                    e.need(f"wgld_{hc}")
                    if hc >= 2:
                        e.need(f"gmm_{hc-2}_{NT-1}")
                    e.sig(lambda v, ws=ws: v.tensor_scalar(
                        out=wgbf[:, ws:ws + NDC * 128], in0=wint[:, :],
                        scalar1=-128.0, scalar2=None, op0=Op.add), f"wgdq_{hc}")
                    e.need(f"wuld_{hc}")
                    if hc >= 2:
                        e.need(f"umm_{hc-2}_{NT-1}")
                    e.sig(lambda v, ws=ws: v.tensor_scalar(
                        out=wubf[:, ws:ws + NDC * 128], in0=wint[:, :],
                        scalar1=-128.0, scalar2=None, op0=Op.add), f"wudq_{hc}")
                    for tau in range(NT):
                        k = hc * NT + tau
                        pg = k % 2
                        e.need(f"gevac_{hc}_{tau}")
                        e.need(f"uevac_{hc}_{tau}")
                        if k >= 2:
                            e.need(f"hstq_{(k-2)//NT}_{(k-2)%NT}")
                        e.sig(lambda v, pg=pg: v.tensor_tensor(
                            out=hb[:, pg * TT:(pg + 1) * TT],
                            in0=gt[:, pg * TT:(pg + 1) * TT],
                            in1=ut[:, pg * TT:(pg + 1) * TT], op=Op.mult),
                            f"hmul_{hc}_{tau}")
                for ic in range(GHC):
                    e.need(f"wdld_{g2}_{ic}")
                    if g2 > 0:
                        e.need(f"dmm_{g2-1}_{NT-1}_{NDC-1}")
                    e.sig(lambda v, ic=ic: v.tensor_scalar(
                        out=wdbf[:, ic * NDC * 128:(ic + 1) * NDC * 128],
                        in0=wint[:, :],
                        scalar1=-128.0, scalar2=None, op0=Op.add), f"wddq_{g2}_{ic}")
                for tau in range(NT):
                    for dc in range(NDC):
                        k = g2 * NT * NDC + tau * NDC + dc
                        pd = k % 2
                        e.need(f"dmm_{g2}_{tau}_{dc}")
                        if g2 == 0:
                            e.need(f"x1r_{tau}")
                            e.sig(lambda v, pd=pd, dc=dc: v.scalar_tensor_tensor(
                                out=x1s(dc), in0=ps_bc[pd][:, :],
                                scalar=sdc_s[:, dc:dc + 1], in1=x1s(dc),
                                op0=Op.mult, op1=Op.add), f"devac_{g2}_{tau}_{dc}")
                        else:
                            if tau > 0:
                                e.need(f"x2f_{g2}_{tau-1}")
                            else:
                                e.need(f"x2f_{g2-1}_{NT-1}")
                            e.sig(lambda v, pd=pd, dc=dc: v.tensor_scalar(
                                out=x1s(dc), in0=ps_bc[pd][:, :],
                                scalar1=sdc_s[:, dc:dc + 1],
                                scalar2=None, op0=Op.mult), f"devac_{g2}_{tau}_{dc}")

        def e_gpsimd(e):
            e.dma(lambda g: g.dma_start(out=onec_s[:, :], in_=onec))
            e.dma(lambda g: g.dma_start(out=cst_s[:, :], in_=cst))
            e.dma(lambda g: g.dma_start(out=oner_s[:, :], in_=oner))
            e.dma(lambda g: g.dma_start(out=sgc_s[:, :], in_=sgc))
            e.dma(lambda g: g.dma_start(out=suc_s[:, :], in_=suc))
            e.dma(lambda g: g.dma_start(out=sdc_s[:, :], in_=sdc))
            e.dflush("const_ld")
            for g2 in range(FFN_G):
                for ihc in range(GHC):
                    hc = g2 * GHC + ihc
                    if ihc == 0 and g2 > 0:
                        e.need(f"wddq_{g2-1}_{GHC-1}")
                    elif hc >= 1:
                        e.need(f"wudq_{hc-1}")
                    e.dma(lambda g, hc=hc: g.dma_start(out=wint[:, :], in_=wg[hc]))
                    e.dflush(f"wgld_{hc}")
                    e.need(f"wgdq_{hc}")
                    e.dma(lambda g, hc=hc: g.dma_start(out=wint[:, :], in_=wu[hc]))
                    e.dflush(f"wuld_{hc}")
                    for tau in range(NT):
                        pg = (hc * NT + tau) % 2
                        e.need(f"hmul_{hc}_{tau}")
                        e.dma(lambda g, hc=hc, tau=tau, pg=pg: g.dma_start(
                            out=hscr[hc, tau], in_=hb[:, pg * TT:(pg + 1) * TT]))
                        e.dflush(f"hstq_{hc}_{tau}")
                e.dflush(f"hstf_{g2}")
                for ic in range(GHC):
                    if ic == 0:
                        e.need(f"wudq_{(g2 + 1) * GHC - 1}")
                    else:
                        e.need(f"wddq_{g2}_{ic-1}")
                    e.dma(lambda g, g2=g2, ic=ic: g.dma_start(
                        out=wint[:, :], in_=wd[g2 * GHC + ic]))
                    e.dflush(f"wdld_{g2}_{ic}")
                for tau in range(NT):
                    for dc in range(NDC):
                        e.need(f"devac_{g2}_{tau}_{dc}")
                        if g2 == 0:
                            e.dma(lambda g, dc=dc, tau=tau: g.dma_start(
                                out=x2[dc, :, tau * TT:(tau + 1) * TT], in_=x1s(dc)))
                        else:
                            e.dma(lambda g, dc=dc, tau=tau: g.dma_start(
                                out=x2[dc, :, tau * TT:(tau + 1) * TT], in_=x1s(dc),
                                accum_op=Op.add))
                    e.dflush(f"x2f_{g2}_{tau}")

        ems = {"sync": e_sync, "tensor": e_tensor, "vector": e_vector,
               "scalar": e_scalar, "gpsimd": e_gpsimd}
        if fused is not None:
            return ems
        build_program(nc, ems)
    return nc


def _tile_lhsT(w, nkc, nmc):
    M, K = w.shape
    assert M == nmc * 128 and K == nkc * 128
    return np.ascontiguousarray(
        w.reshape(nmc, 128, nkc, 128).transpose(0, 3, 2, 1).reshape(nmc, 128, nkc * 128))


def ffn0_inputs(inp):
    return {
        "wg": _tile_lhsT(inp["L0_ffn_wq_gate"], NDC, NHC),
        "wu": _tile_lhsT(inp["L0_ffn_wq_up"], NDC, NHC),
        "wd": _tile_lhsT_down(inp["L0_ffn_wq_down"]),
        "sgc": np.ascontiguousarray(inp["L0_ffn_s_gate"].reshape(NHC, 128).T),
        "suc": np.ascontiguousarray(inp["L0_ffn_s_up"].reshape(NHC, 128).T),
        "sdc": np.ascontiguousarray(inp["L0_ffn_s_down"].reshape(NDC, 128).T),
        "onec": np.ones((128, 1), np.float32),
        "cst": np.tile(np.array([[1e-5, 1e-6, 0.0, 0.0]], np.float32), (128, 1)),
        "oner": np.ones((1, 128), np.float32),
    }


def _tile_lhsT_down(w):
    return np.ascontiguousarray(
        w.reshape(NDC, 128, NHC, 128).transpose(2, 3, 0, 1).reshape(NHC, 128, NDC * 128))


def fm(x_core):
    return np.ascontiguousarray(x_core.T.reshape(NDC, 128, -1))


def run_ffn0(x1_full, inp):
    nc = build_ffn0()
    shared = ffn0_inputs(inp)
    in_maps = []
    for c in range(8):
        b, h = divmod(c, 2)
        m = dict(shared)
        m["x1"] = fm(x1_full[b, h * TLOC:(h + 1) * TLOC])
        in_maps.append(m)
    res = run_bass_kernel_spmd(nc, in_maps, core_ids=list(range(8)))
    out = np.zeros((4, 4096, D), np.float32)
    for c in range(8):
        b, h = divmod(c, 2)
        out[b, h * TLOC:(h + 1) * TLOC] = res.results[c]["x2"].reshape(D, TLOC).T
    return out



TCTX = 4096
TM = 256
NTI = TCTX // TM
RT0 = NTI // 2
NST = 16


def build_mamba(fused=None):
    nc = bass.Bass("TRN2", target_bir_lowering=False) if fused is None else fused.nc
    x0 = nc.dram_tensor("x0", [NDC, 128, TCTX], F32, kind="ExternalInput").ap()
    cw = nc.dram_tensor("cw", [128, NDC * 4], F32, kind="ExternalInput").ap()
    cb = nc.dram_tensor("cb", [128, NDC], F32, kind="ExternalInput").ap()
    xpw = nc.dram_tensor("xpw", [128, NDC * 160], F32, kind="ExternalInput").ap()
    dtw = nc.dram_tensor("dtw", [128, NDC * 128], F32, kind="ExternalInput").ap()
    dtb = nc.dram_tensor("dtb", [128, NDC], F32, kind="ExternalInput").ap()
    alog = nc.dram_tensor("alog", [128, NDC * NST], F32, kind="ExternalInput").ap()
    dpar = nc.dram_tensor("dpar", [128, NDC], F32, kind="ExternalInput").ap()
    bmaskd = nc.dram_tensor("bmask", [1, TCTX], F32, kind="ExternalInput").ap()
    onec = _shared_const(nc, fused, "onec", [128, 1])
    oner = _shared_const(nc, fused, "oner", [1, 128])
    cst = _shared_const(nc, fused, "cst", [128, 4])
    if fused is None:
        x1 = nc.dram_tensor("x1", [NDC, 128, TLOC], F32, kind="ExternalOutput").ap()
    else:
        x1 = nc.dram_tensor("x1", [NDC, 128, TLOC], F32).ap()
        fused.x1 = x1
    bcscr = nc.dram_tensor("bcscr", [2, 32, TM], F32).ap()

    HTM = TM + 3

    with ExitStack() as st:
        def sb(name, shape, dt):
            return st.enter_context(nc.sbuf_tensor(name, shape, dt,
                                                   allow_name_mangling=True))

        def psb(name, shape):
            return st.enter_context(nc.psum_tensor(name, shape, F32,
                                                   allow_name_mangling=True))

        x0t = sb("x0t", [128, NDC * TM], F32)
        x0rt = sb("x0rt", [128, NDC * TM], F32)
        sq = sb("sq", [128, 2 * TM], F32)
        rows = sb("rows", [1, 6 * TM], F32)
        xnh = sb("xnh", [128, 2 * NDC * HTM], BF16)
        xcv = sb("xcv", [128, 2 * NDC * TM], BF16)
        acc = sb("acc", [128, 2 * TM], F32)
        dtin = sb("dtin", [128, 2 * TM], BF16)
        delta = sb("delta", [128, 2 * NDC * TM], BF16)
        sp1 = sb("sp1", [128, 2 * TM], F32)
        sp2 = sb("sp2", [128, 2 * TM], F32)
        spA = sb("spA", [128, 2 * TM], F32)
        bcb = sb("bcb", [128, 32 * TM], BF16)
        bcrow = sb("bcrow", [32, TM], F32)
        bmask = sb("bmask_s", [1, TCTX], F32)
        abuf = sb("abuf", [128, 2 * NST * TM], BF16)
        bbuf = sb("bbuf", [128, NST * TM], BF16)
        hdc = sb("hdc", [128, 2 * NST * TM], BF16)
        ubuf = sb("ubuf", [128, TM], BF16)
        stt = sb("stt", [128, NDC * NST], F32)
        ytmp = sb("ytmp", [128, NST * TM], BF16)
        yred = sb("yred", [128, TM], F32)
        yt2 = sb("yt2", [128, TM], F32)
        cw_s = sb("cw_s", [128, NDC * 4], F32)
        cb_s = sb("cb_s", [128, NDC], F32)
        xpw_s = sb("xpw_s", [128, NDC * 160], BF16)
        dtw_s = sb("dtw_s", [128, NDC * 128], BF16)
        dtb_s = sb("dtb_s", [128, NDC], F32)
        alog_s = sb("alog_s", [128, NDC * NST], F32)
        aneg = sb("aneg", [128, NDC * NST], F32)
        dpar_s = sb("dpar_s", [128, NDC], F32)
        onec_s = sb("onec_s", [128, 1], F32)
        oner_s = sb("oner_s", [1, 128], F32)
        cst_s = sb("cst_s", [128, 4], F32)

        ps_sum = psb("ps_sum", [1, TM])
        ps_ssq = psb("ps_ssq", [1, TM])
        ps_bc = [psb(f"ps_bc{i}", [128, TM]) for i in range(2)]
        ps_dt = psb("ps_dt", [128, TM])
        ps_misc = psb("ps_misc", [32, 2 * TM])
        ps_dl = [psb(f"ps_dl{i}", [128, TM]) for i in range(2)]

        def row(name):
            base = {"nmu": 0, "m2": 1, "t1": 2, "var": 3, "rstd": 4, "mrs": 5}[name]
            return rows[0:1, base * TM:(base + 1) * TM]

        def x0s(dc):
            return x0t[:, dc * TM:(dc + 1) * TM]

        def xnslot(i, dc, a, b):
            s = (i % 2) * NDC * HTM
            return xnh[:, s + dc * HTM + a:s + dc * HTM + b]

        def xcvs(i, dc):
            s = (i % 2) * NDC * TM
            return xcv[:, s + dc * TM:(s + dc * TM) + TM]

        def dls(i, dc):
            s = (i % 2) * NDC * TM
            return delta[:, s + dc * TM:(s + dc * TM) + TM]

        def e_sync(e):
            for i in range(NTI):
                if i > 0:
                    e.need(f"lnapply_{i-1}")
                    e.need(f"stats_{i-1}")
                    e.need(f"sq_{i-1}_{NDC-1}")
                for dc in range(NDC):
                    e.dma(lambda g, dc=dc, i=i: g.dma_start(
                        out=x0s(dc), in_=x0[dc, :, i * TM:(i + 1) * TM]))
                e.dflush(f"x0ld_{i}")
                if i >= RT0:
                    if i > RT0:
                        e.need(f"x1f_{i-1}")
                    for dc in range(NDC):
                        e.dma(lambda g, dc=dc, i=i: g.dma_start(
                            out=x0rt[:, dc * TM:(dc + 1) * TM],
                            in_=x0[dc, :, i * TM:(i + 1) * TM]))
                    e.dflush(f"x0rld_{i}")

        def e_gpsimd(e):
            e.dma(lambda g: g.dma_start(out=cw_s[:, :], in_=cw))
            e.dma(lambda g: g.dma_start(out=cb_s[:, :], in_=cb))
            e.dma(lambda g: g.dma_start(out=xpw_s[:, :], in_=xpw))
            e.dma(lambda g: g.dma_start(out=dtw_s[:, :], in_=dtw))
            e.dma(lambda g: g.dma_start(out=dtb_s[:, :], in_=dtb))
            e.dma(lambda g: g.dma_start(out=alog_s[:, :], in_=alog))
            e.dma(lambda g: g.dma_start(out=dpar_s[:, :], in_=dpar))
            e.dma(lambda g: g.dma_start(out=bmask[:, :], in_=bmaskd))
            e.dma(lambda g: g.dma_start(out=onec_s[:, :], in_=onec))
            e.dma(lambda g: g.dma_start(out=oner_s[:, :], in_=oner))
            e.dma(lambda g: g.dma_start(out=cst_s[:, :], in_=cst))
            e.dflush("const_ld")
            for i in range(NTI):
                e.need(f"bcmask_{i}")
                e.dma(lambda g, i=i: g.dma_start(out=bcscr[i % 2], in_=bcrow[:, :]))
                e.dflush(f"bcw_{i}")
                if i > 0:
                    e.need(f"bfin_{i-1}")
                e.dma(lambda g, i=i: g.dma_start(
                    out=bcb[:, :].rearrange("p (n t) -> p n t", n=32),
                    in_=bcscr[i % 2].partition_broadcast(128)))
                e.dflush(f"bcb_{i}")
                if i >= RT0:
                    for dc in range(NDC):
                        e.need(f"x1a_{i}_{dc}")
                        e.dma(lambda g, dc=dc, i=i: g.dma_start(
                            out=x1[dc, :, (i - RT0) * TM:(i - RT0 + 1) * TM],
                            in_=x0rt[:, dc * TM:(dc + 1) * TM]))
                    e.dflush(f"x1f_{i}")

        def e_tensor(e):
            e.need("const_ld")
            for i in range(NTI):
                e.need(f"x0ld_{i}")
                if i > 0:
                    e.need(f"rowrd_{i-1}")
                for dc in range(NDC):
                    e.run(lambda t, dc=dc: t.matmul(
                        ps_sum[:, :], lhsT=onec_s[:, :], rhs=x0s(dc),
                        start=(dc == 0), stop=(dc == NDC - 1)))
                for dc in range(NDC):
                    e.need(f"sq_{i}_{dc}")
                    e.sig(lambda t, dc=dc: t.matmul(
                        ps_ssq[:, :], lhsT=onec_s[:, :],
                        rhs=sq[:, (dc % 2) * TM:((dc % 2) + 1) * TM],
                        start=(dc == 0), stop=(dc == NDC - 1)),
                        f"ssqmm_{i}_{dc}" if dc < NDC - 1 else f"stats_{i}")
                e.need(f"rstd_{i}")
                e.need(f"mrs_{i}")
                if i > 0:
                    e.need(f"lnapply_{i-1}")
                e.sig(lambda t: t.matmul(
                    ps_bc[0][:, :], lhsT=oner_s[:, :], rhs=row("rstd"),
                    start=True, stop=True), f"rstdb_{i}")
                e.sig(lambda t: t.matmul(
                    ps_bc[1][:, :], lhsT=oner_s[:, :], rhs=row("mrs"),
                    start=True, stop=True), f"mrsb_{i}")
                e.need(f"silucv_{i}_{NDC-1}")
                if i > 0:
                    e.need(f"dtin_{i-1}")
                for dc in range(NDC):
                    e.sig(lambda t, dc=dc, i=i: t.matmul(
                        ps_dt[:, :], lhsT=xpw_s[:, dc * 160:dc * 160 + 128],
                        rhs=xcvs(i, dc),
                        start=(dc == 0), stop=(dc == NDC - 1)),
                        f"dtp_{i}" if dc == NDC - 1 else None)
                if i > 0:
                    e.need(f"bcmask_{i-1}")
                for dc in range(NDC):
                    e.sig(lambda t, dc=dc, i=i: t.matmul(
                        ps_misc[0:32, 0:TM], lhsT=xpw_s[:, dc * 160 + 128:dc * 160 + 160],
                        rhs=xcvs(i, dc),
                        start=(dc == 0), stop=(dc == NDC - 1)),
                        f"bcp_{i}" if dc == NDC - 1 else None)
                e.sig(lambda t, i=i: t.matmul(
                    ps_misc[0:16, TM:2 * TM], lhsT=oner_s[0:1, 0:16],
                    rhs=bmask[0:1, i * TM:(i + 1) * TM],
                    start=True, stop=True), f"mbp_{i}")
                e.need(f"dtin_{i}")
                for dc in range(NDC):
                    k = i * NDC + dc
                    pd = k % 2
                    if k >= 2:
                        e.need(f"praw_{((k-2)//NDC)}_{(k-2)%NDC}")
                    e.sig(lambda t, dc=dc, i=i, pd=pd: t.matmul(
                        ps_dl[pd][:, :], lhsT=dtw_s[:, dc * 128:(dc + 1) * 128],
                        rhs=dtin[:, (i % 2) * TM:((i % 2) + 1) * TM],
                        start=True, stop=True), f"dtmm_{i}_{dc}")

        def e_scalar(e):
            e.need("const_ld")
            e.sig(lambda s: s.activation(
                out=aneg[:, :], in_=alog_s[:, :], func=AF.Exp), "aexp_pre")
            for i in range(NTI):
                e.need(f"x0ld_{i}")
                for dc in range(NDC):
                    if dc >= 2:
                        e.need(f"ssqmm_{i}_{dc-2}")
                    elif i > 0:
                        e.need(f"stats_{i-1}")
                    e.sig(lambda s, dc=dc: s.activation(
                        out=sq[:, (dc % 2) * TM:((dc % 2) + 1) * TM],
                        in_=x0s(dc), func=AF.Square), f"sq_{i}_{dc}")
                e.need(f"var_{i}")
                if i > 0:
                    e.need(f"rstdb_{i-1}")
                e.sig(lambda s: s.activation(
                    out=row("t1"), in_=row("var"), func=AF.Ln, bias=cst_s[0:1, 0:1]))
                e.fence()
                e.sig(lambda s: s.activation(
                    out=row("rstd"), in_=row("t1"), func=AF.Exp, scale=-0.5),
                    f"rstd_{i}")
                for dc in range(NDC):
                    e.need(f"conv_{i}_{dc}")
                    e.sig(lambda s, dc=dc, i=i: s.activation(
                        out=xcvs(i, dc), in_=acc[:, (dc % 2) * TM:((dc % 2) + 1) * TM],
                        func=AF.Silu, bias=cb_s[:, dc:dc + 1]), f"silucv_{i}_{dc}")
                for dc in range(NDC):
                    pd2 = dc % 2
                    e.need(f"praw_{i}_{dc}")
                    e.sig(lambda s, pd2=pd2: s.activation(
                        out=spA[:, pd2 * TM:(pd2 + 1) * TM],
                        in_=sp1[:, pd2 * TM:(pd2 + 1) * TM], func=AF.Abs))
                    e.fence()
                    e.sig(lambda s, pd2=pd2: s.activation(
                        out=spA[:, pd2 * TM:(pd2 + 1) * TM],
                        in_=spA[:, pd2 * TM:(pd2 + 1) * TM], func=AF.Exp, scale=-1.0))
                    e.fence()
                    e.sig(lambda s, pd2=pd2: s.activation(
                        out=spA[:, pd2 * TM:(pd2 + 1) * TM],
                        in_=spA[:, pd2 * TM:(pd2 + 1) * TM], func=AF.Ln, bias=1.0),
                        f"spl_{i}_{dc}")
                for dc in range(NDC):
                    k = i * NDC + dc
                    sa = (dc % 2) * NST * TM
                    e.need("aneg_neg")
                    e.need(f"delta_{i}_{dc}")
                    if k >= 2:
                        e.need(f"scn_{((k-2)//NDC)}_{(k-2)%NDC}")
                    for n in range(NST):
                        e.sig(lambda s, dc=dc, i=i, n=n, sa=sa: s.activation(
                            out=abuf[:, sa + n * TM:sa + (n + 1) * TM],
                            in_=dls(i, dc), func=AF.Exp,
                            scale=aneg[:, dc * NST + n:dc * NST + n + 1]),
                            f"aexp_{i}_{dc}" if n == NST - 1 else None)

        def e_vector(e):
            e.sig(lambda v: v.memset(xnh[:, :], 0.0), "xnh0")
            e.sig(lambda v: v.memset(stt[:, :], 0.0), "st0")
            e.need("aexp_pre")
            e.sig(lambda v: v.tensor_scalar(
                out=aneg[:, :], in0=aneg[:, :], scalar1=-1.0, scalar2=None,
                op0=Op.mult), "aneg_neg")
            for i in range(NTI):
                e.need(f"stats_{i}")
                e.fence()
                e.sig(lambda v: v.tensor_scalar(
                    out=row("nmu"), in0=ps_sum[:, :], scalar1=-1.0 / D,
                    scalar2=None, op0=Op.mult))
                e.sig(lambda v: v.tensor_scalar(
                    out=row("m2"), in0=ps_ssq[:, :], scalar1=1.0 / D,
                    scalar2=None, op0=Op.mult), f"rowrd_{i}")
                e.fence()
                e.sig(lambda v: v.tensor_tensor(
                    out=row("t1"), in0=row("nmu"), in1=row("nmu"), op=Op.mult))
                e.fence()
                e.sig(lambda v: v.tensor_tensor(
                    out=row("var"), in0=row("m2"), in1=row("t1"), op=Op.subtract),
                    f"var_{i}")
                e.need(f"rstd_{i}")
                if i > 0:
                    e.need(f"mrsb_{i-1}")
                e.fence()
                e.sig(lambda v: v.tensor_tensor(
                    out=row("mrs"), in0=row("nmu"), in1=row("rstd"), op=Op.mult),
                    f"mrs_{i}")
                e.need(f"rstdb_{i}")
                e.need(f"mrsb_{i}")
                if i == 0:
                    e.need("xnh0")
                for dc in range(NDC):
                    e.fence()
                    e.sig(lambda v, dc=dc: v.tensor_tensor(
                        out=sp1[:, 0:TM], in0=x0s(dc), in1=ps_bc[0][:, :], op=Op.mult))
                    e.fence()
                    e.sig(lambda v, dc=dc, i=i: v.tensor_tensor(
                        out=xnslot(i, dc, 3, 3 + TM), in0=sp1[:, 0:TM],
                        in1=ps_bc[1][:, :], op=Op.add),
                        f"lnapply_{i}" if dc == NDC - 1 else None)
                if i < NTI - 1:
                    e.fence()
                    for dc in range(NDC):
                        e.sig(lambda v, dc=dc, i=i: v.tensor_copy(
                            xnslot(i + 1, dc, 0, 3), xnslot(i, dc, TM, TM + 3)),
                            f"halo_{i}" if dc == NDC - 1 else None)
                for dc in range(NDC):
                    if i * NDC + dc >= 2:
                        pk = i * NDC + dc - 2
                        e.need(f"silucv_{pk//NDC}_{pk%NDC}")
                    e.fence()
                    e.sig(lambda v, dc=dc, i=i: v.tensor_scalar(
                        out=acc[:, (dc % 2) * TM:((dc % 2) + 1) * TM],
                        in0=xnslot(i, dc, 0, TM), scalar1=cw_s[:, dc * 4:dc * 4 + 1],
                        scalar2=None, op0=Op.mult))
                    for j in range(1, 4):
                        e.fence()
                        e.sig(lambda v, dc=dc, i=i, j=j: v.scalar_tensor_tensor(
                            out=acc[:, (dc % 2) * TM:((dc % 2) + 1) * TM],
                            in0=xnslot(i, dc, j, j + TM),
                            scalar=cw_s[:, dc * 4 + j:dc * 4 + j + 1],
                            in1=acc[:, (dc % 2) * TM:((dc % 2) + 1) * TM],
                            op0=Op.mult, op1=Op.add),
                            f"conv_{i}_{dc}" if j == 3 else None)
                e.need(f"dtp_{i}")
                e.fence()
                e.sig(lambda v, i=i: v.tensor_copy(
                    dtin[:, (i % 2) * TM:((i % 2) + 1) * TM], ps_dt[:, :]),
                    f"dtin_{i}")
                e.need(f"bcp_{i}")
                e.need(f"mbp_{i}")
                if i > 0:
                    e.need(f"bcw_{i-1}")
                e.fence()
                e.sig(lambda v: v.tensor_copy(bcrow[:, :], ps_misc[0:32, 0:TM]))
                e.fence()
                e.sig(lambda v: v.tensor_tensor(
                    out=bcrow[0:16, :], in0=bcrow[0:16, :],
                    in1=ps_misc[0:16, TM:2 * TM], op=Op.mult), f"bcmask_{i}")
                for dc in range(NDC):
                    k = i * NDC + dc
                    pd = k % 2
                    pd2 = dc % 2
                    e.need(f"dtmm_{i}_{dc}")
                    e.fence()
                    e.sig(lambda v, pd=pd, pd2=pd2, dc=dc: v.tensor_scalar(
                        out=sp1[:, pd2 * TM:(pd2 + 1) * TM], in0=ps_dl[pd][:, :],
                        scalar1=dtb_s[:, dc:dc + 1], scalar2=None, op0=Op.add),
                        f"praw_{i}_{dc}")
                    e.fence()
                    e.sig(lambda v, pd2=pd2: v.tensor_scalar(
                        out=sp2[:, pd2 * TM:(pd2 + 1) * TM],
                        in0=sp1[:, pd2 * TM:(pd2 + 1) * TM],
                        scalar1=0.0, scalar2=None, op0=Op.max))
                    e.need(f"spl_{i}_{dc}")
                    e.fence()
                    e.sig(lambda v, pd2=pd2, i=i, dc=dc: v.tensor_tensor(
                        out=dls(i, dc), in0=spA[:, pd2 * TM:(pd2 + 1) * TM],
                        in1=sp2[:, pd2 * TM:(pd2 + 1) * TM], op=Op.add),
                        f"delta_{i}_{dc}")
                e.need(f"bcb_{i}")
                for dc in range(NDC):
                    sa = (dc % 2) * NST * TM
                    sh = (dc % 2) * NST * TM
                    e.fence()
                    e.sig(lambda v, i=i, dc=dc: v.tensor_tensor(
                        out=ubuf[:, :], in0=dls(i, dc), in1=xcvs(i, dc), op=Op.mult))
                    e.fence()
                    for n in range(NST):
                        e.sig(lambda v, n=n: v.tensor_tensor(
                            out=bbuf[:, n * TM:(n + 1) * TM], in0=ubuf[:, :],
                            in1=bcb[:, n * TM:(n + 1) * TM], op=Op.mult),
                            f"bfin_{i}" if (n == NST - 1 and dc == NDC - 1 and i < RT0) else None)
                    e.need(f"aexp_{i}_{dc}")
                    e.fence()
                    for n in range(NST):
                        e.sig(lambda v, n=n, sa=sa, sh=sh, dc=dc: v.tensor_tensor_scan(
                            out=hdc[:, sh + n * TM:sh + (n + 1) * TM],
                            data0=abuf[:, sa + n * TM:sa + (n + 1) * TM],
                            data1=bbuf[:, n * TM:(n + 1) * TM],
                            initial=stt[:, dc * NST + n:dc * NST + n + 1],
                            op0=Op.mult, op1=Op.add))
                    e.fence()
                    e.sig(lambda v, sh=sh, dc=dc: v.tensor_copy(
                        stt[:, dc * NST:(dc + 1) * NST],
                        hdc[:, sh:sh + NST * TM].rearrange(
                            "p (n t) -> p n t", n=NST)[:, :, TM - 1:TM].squeeze(2)),
                        f"scn_{i}_{dc}")
                    if i >= RT0:
                        e.fence()
                        e.sig(lambda v, sh=sh: v.tensor_tensor(
                            out=ytmp[:, :], in0=hdc[:, sh:sh + NST * TM],
                            in1=bcb[:, NST * TM:2 * NST * TM], op=Op.mult),
                            f"bfin_{i}" if dc == NDC - 1 else None)
                        e.fence()
                        e.sig(lambda v: v.tensor_reduce(
                            out=yred[:, :],
                            in_=ytmp[:, :].rearrange("p (n t) -> p t n", n=NST),
                            axis=mybir.AxisListType.X, op=Op.add))
                        e.fence()
                        e.sig(lambda v, i=i, dc=dc: v.scalar_tensor_tensor(
                            out=yt2[:, :], in0=xnslot(i, dc, 3, 3 + TM),
                            scalar=dpar_s[:, dc:dc + 1], in1=yred[:, :],
                            op0=Op.mult, op1=Op.add))
                        if dc == 0:
                            e.need(f"x0rld_{i}")
                        e.fence()
                        e.sig(lambda v, dc=dc: v.tensor_tensor(
                            out=x0rt[:, dc * TM:(dc + 1) * TM], in0=yt2[:, :],
                            in1=x0rt[:, dc * TM:(dc + 1) * TM], op=Op.add),
                            f"x1a_{i}_{dc}")

        ems = {"sync": e_sync, "tensor": e_tensor, "vector": e_vector,
               "scalar": e_scalar, "gpsimd": e_gpsimd}
        if fused is not None:
            return ems
        build_program(nc, ems)
    return nc


def mamba_inputs(inp):
    cw = inp["L0_attn_conv_w"][:, 0, :]
    xp = inp["L0_attn_xproj_w"]
    dtwm = inp["L0_attn_dtproj_w"]
    xpw = np.ascontiguousarray(xp.reshape(160, NDC, 128).transpose(1, 2, 0)).reshape(NDC, 128, 160)
    xpw = np.ascontiguousarray(xpw.transpose(1, 0, 2).reshape(128, NDC * 160))
    dtw = np.ascontiguousarray(dtwm.reshape(NDC, 128, 128).transpose(2, 0, 1).reshape(128, NDC * 128))
    return {
        "cw": np.ascontiguousarray(cw.reshape(NDC, 128, 4).transpose(1, 0, 2).reshape(128, NDC * 4)),
        "cb": np.ascontiguousarray(inp["L0_attn_conv_b"].reshape(NDC, 128).T),
        "xpw": xpw,
        "dtw": dtw,
        "dtb": np.ascontiguousarray(inp["L0_attn_dtproj_b"].reshape(NDC, 128).T),
        "alog": np.ascontiguousarray(
            inp["L0_attn_A_log"].reshape(NDC, 128, NST).transpose(1, 0, 2).reshape(128, NDC * NST)),
        "dpar": np.ascontiguousarray(inp["L0_attn_D_param"].reshape(NDC, 128).T),
        "onec": np.ones((128, 1), np.float32),
        "oner": np.ones((1, 128), np.float32),
        "cst": np.tile(np.array([[1e-5, 1e-6, 0.0, 0.0]], np.float32), (128, 1)),
    }


def mamba_core_inputs(x_full, b, h):
    if h == 0:
        ctx = np.concatenate([np.zeros((TLOC, D), np.float32), x_full[b, :TLOC]], 0)
        mask = np.concatenate([np.zeros(TLOC, np.float32), np.ones(TLOC, np.float32)])
    else:
        ctx = np.asarray(x_full[b], np.float32)
        mask = np.ones(TCTX, np.float32)
    return {"x0": fm(ctx), "bmask": mask.reshape(1, TCTX)}


def run_mamba(x_full, inp):
    nc = build_mamba()
    shared = mamba_inputs(inp)
    in_maps = []
    for c in range(8):
        b, h = divmod(c, 2)
        m = dict(shared)
        m.update(mamba_core_inputs(x_full, b, h))
        in_maps.append(m)
    res = run_bass_kernel_spmd(nc, in_maps, core_ids=list(range(8)))
    out = np.zeros((4, 4096, D), np.float32)
    for c in range(8):
        b, h = divmod(c, 2)
        out[b, h * TLOC:(h + 1) * TLOC] = res.results[c]["x1"].reshape(D, TLOC).T
    return out



NBLK = 16
BLK = 256
NHD = 8
CT = 512
NCT = TCTX // CT


def build_lightning(fused=None):
    nc = bass.Bass("TRN2", target_bir_lowering=False) if fused is None else fused.nc
    if fused is None:
        x2f = nc.dram_tensor("x2f", [NDC, 128, TCTX], F32, kind="ExternalInput").ap()
    else:
        x2f = None
    wq = nc.dram_tensor("wq", [4 * NHD, 128, NDC * 128], F32, kind="ExternalInput").ap()
    slopes = nc.dram_tensor("slopes", [1, NHD], F32, kind="ExternalInput").ap()
    onec = _shared_const(nc, fused, "onec", [128, 1])
    oner = _shared_const(nc, fused, "oner", [1, 128])
    cst = _shared_const(nc, fused, "cst", [128, 4])
    if fused is None:
        oo = nc.dram_tensor("oo", [NHD, 128, TCTX], F32, kind="ExternalOutput").ap()
        uo = nc.dram_tensor("uo", [NHD, 128, TCTX], F32, kind="ExternalOutput").ap()
        oub = None
    else:
        oo = uo = None
        oub = nc.dram_tensor("oub", [2 * 2048, TLOC], BF16).ap()
        fused.oub = oub
    qs = nc.dram_tensor("qs", [NHD, 128, TCTX], BF16).ap()
    xns = nc.dram_tensor("xns", [NDC, 128, TCTX], BF16).ap()
    ks = nc.dram_tensor("ks", [NHD, 128, TCTX], BF16).ap()
    vs = nc.dram_tensor("vs", [NHD, 128, TCTX], BF16).ap()

    with ExitStack() as st:
        def sb(name, shape, dt):
            return st.enter_context(nc.sbuf_tensor(name, shape, dt,
                                                   allow_name_mangling=True))

        def psb(name, shape, dt=F32):
            return st.enter_context(nc.psum_tensor(name, shape, dt,
                                                   allow_name_mangling=True))

        sq = sb("sq", [128, 2 * CT], F32)
        rows = sb("rows", [1, 6 * CT], F32)
        _ln_stack = ExitStack()
        x2t = _ln_stack.enter_context(nc.sbuf_tensor(
            "x2t", [128, NDC * CT], F32 if fused is None else BF16, allow_name_mangling=True))
        xn2 = _ln_stack.enter_context(nc.sbuf_tensor("xn2", [128, 2 * NDC * CT], BF16, allow_name_mangling=True))
        _ln_stack.close()
        wqs = sb("wqs", [128, 8 * NDC * 128], BF16)
        xnt = sb("xnt", [128, 2 * NDC * CT], BF16)
        evb = sb("evb", [128, 4 * CT], BF16)
        evf = sb("evf", [128, 4 * CT], F32)
        onec_s = sb("onec_s", [128, 1], F32)
        onec_b = sb("onec_b", [128, 1], BF16)
        oner_s = sb("oner_s", [1, 128], F32)
        cst_s = sb("cst_s", [128, 4], F32)
        slp = sb("slp", [1, NHD], F32)
        slpc = sb("slpc", [128, NHD], F32)
        nslpc = sb("nslpc", [128, NHD], F32)
        idxt = sb("idxt", [128, 2 * BLK], I32)
        idxc = sb("idxc", [128, 2 * BLK], F32)
        mk01 = sb("mk01", [128, 2 * BLK], F32)
        ddsc = sb("ddsc", [128, 2 * BLK], F32)
        ddt = sb("ddt", [128, NHD * 2 * BLK], BF16)
        qdb = sb("qdb", [128, NHD * BLK], BF16)
        qdi = sb("qdi", [128, BLK], I32)
        qdf = sb("qdf", [128, BLK], F32)
        kdi = sb("kdi", [128, 2], I32)
        kdc = sb("kdc", [128, 2 * NHD], F32)
        bdc = sb("bdc", [128, NHD], F32)
        qkvb = sb("qkvb", [128, 2 * NHD * 3 * BLK], BF16)
        kvsb = sb("kvsb", [128, NHD * 128], F32)
        kvbf = sb("kvbf", [128, 128], BF16)
        qpb = sb("qpb", [128, BLK], BF16)
        ktm = sb("ktm", [128, 2 * 128], BF16)
        vtm = sb("vtm", [128, 2 * 128], BF16)
        qkd = sb("qkd", [128, 2 * BLK], BF16)
        oev = sb("oev", [128, 2 * NHD * BLK], F32)

        ps_sum = psb("ps_sum", [1, CT])
        ps_ssq = psb("ps_ssq", [1, CT])
        ps_bc = [psb(f"ps_bc{i}", [128, CT]) for i in range(2)]
        ps_qk = [psb(f"ps_qk{i}", [128, CT]) for i in range(2)]
        ps_tpc = psb("ps_tpc", [128, 512], BF16)
        ps_tp = ps_tpc[:, 0:256]
        ps_tp2 = ps_tpc[:, 256:512]
        ps_kv = None
        ps_o = None

        def row(name):
            base = {"nmu": 0, "m2": 1, "t1": 2, "var": 3, "rstd": 4, "mrs": 5}[name]
            return rows[0:1, base * CT:(base + 1) * CT]

        def x2s(dc):
            return x2t[:, dc * CT:(dc + 1) * CT]

        def x2src(dc, tau):
            if fused is None:
                return x2f[dc, :, tau * CT:(tau + 1) * CT]
            slot, tp = divmod(tau * CT, TLOC)
            base = slot * 2048 + dc * 128
            return fused.xg[base:base + 128, tp:tp + CT]

        def osink(i, hd):
            if fused is None:
                return oo[hd, :, i * BLK:(i + 1) * BLK]
            r = (i // 8) * 2048 + hd * 128
            c = (i % 8) * BLK
            return oub[r:r + 128, c:c + BLK]

        def usink(hd, tau):
            if fused is None:
                return uo[hd, :, tau * CT:(tau + 1) * CT]
            r = (tau // 4) * 2048 + 1024 + hd * 128
            c = (tau % 4) * CT
            return oub[r:r + 128, c:c + CT]

        def xn2s(tau, dc):
            s = (tau % 2) * NDC * CT
            return xn2[:, s + dc * CT:s + dc * CT + CT]

        def qkvslot(i, hd, which):
            s = (i % 2) * NHD * 3 * BLK
            return qkvb[:, s + (hd * 3 + which) * BLK: s + (hd * 3 + which) * BLK + BLK]

        def e_sync(e):
            for tau in range(NCT):
                if tau > 0:
                    e.need(f"lnapply_{tau-1}")
                    e.need(f"stats_{tau-1}")
                    e.need(f"sq_{tau-1}_{NDC-1}")
                for dc in range(NDC):
                    e.dma(lambda g, dc=dc, tau=tau: g.dma_start(
                        out=x2s(dc), in_=x2src(dc, tau)))
                e.dflush(f"x2ld_{tau}")
            for fcg in range(4):
                for tau in range(NCT):
                    kk = fcg * NCT + tau
                    e.need(f"stats_{NCT-1}")
                    e.need(f"sq_{NCT-1}_{NDC-1}")
                    e.need(f"lnapply_{NCT-1}")
                    e.need(f"xnst_{NCT-1}")
                    if kk >= 2:
                        pkk = kk - 2
                        e.need(f"pmm_{(pkk//NCT)*8+7}_{pkk%NCT}")
                    for dc in range(NDC):
                        e.dma(lambda g, dc=dc, tau=tau, kk=kk: g.dma_start(
                            out=xnt[:, ((kk % 2) * NDC + dc) * CT:((kk % 2) * NDC + dc + 1) * CT],
                            in_=xns[dc, :, tau * CT:(tau + 1) * CT]))
                    e.dflush(f"xnld_{fcg}_{tau}")
            e.need("p1done")
            for i in range(NBLK):
                if i >= 2:
                    e.need(f"oev_{i-2}_{NHD-1}")
                for hd in range(NHD):
                    e.dma(lambda g, i=i, hd=hd: g.dma_start(
                        out=qkvslot(i, hd, 0), in_=qs[hd, :, i * BLK:(i + 1) * BLK]))
                    e.dma(lambda g, i=i, hd=hd: g.dma_start(
                        out=qkvslot(i, hd, 1), in_=ks[hd, :, i * BLK:(i + 1) * BLK]))
                    e.dma(lambda g, i=i, hd=hd: g.dma_start(
                        out=qkvslot(i, hd, 2), in_=vs[hd, :, i * BLK:(i + 1) * BLK]))
                e.dflush(f"qkvld_{i}")

        def e_gpsimd(e):
            e.dma(lambda g: g.dma_start(out=onec_s[:, :], in_=onec))
            e.dma(lambda g: g.dma_start(out=oner_s[:, :], in_=oner))
            e.dma(lambda g: g.dma_start(out=cst_s[:, :], in_=cst))
            e.dma(lambda g: g.dma_start(out=slp[:, :], in_=slopes))
            e.dflush("const_ld")
            for tau in range(NCT):
                e.need(f"lnapply_{tau}")
                for dc in range(NDC):
                    e.dma(lambda g, dc=dc, tau=tau: g.dma_start(
                        out=xns[dc, :, tau * CT:(tau + 1) * CT],
                        in_=xn2s(tau, dc)))
                e.dflush(f"xnst_{tau}")
            for fcg in range(4):
                e.need(f"stats_{NCT-1}")
                e.need(f"sq_{NCT-1}_{NDC-1}")
                e.need(f"lnapply_{NCT-1}")
                e.need(f"xnst_{NCT-1}")
                if fcg > 0:
                    e.need(f"pmm_{(fcg-1)*8+7}_{NCT-1}")
                for fc8 in range(8):
                    e.dma(lambda g, fcg=fcg, fc8=fc8: g.dma_start(
                        out=wqs[:, fc8 * NDC * 128:(fc8 + 1) * NDC * 128],
                        in_=wq[fcg * 8 + fc8]))
                e.dflush(f"wqg_{fcg}")
                for tau in range(NCT):
                    kk = fcg * NCT + tau
                    for fc8 in range(8):
                        fc = fcg * 8 + fc8
                        hd, quarter = divmod(fc, 4)
                        k = kk * 8 + fc8
                        sl = k % 4
                        e.need(f"pev_{fc}_{tau}")
                        tgt = [qs, ks, vs, None][quarter]
                        if quarter < 3:
                            e.dma(lambda g, hd=hd, tau=tau, sl=sl, tgt=tgt: g.dma_start(
                                out=tgt[hd, :, tau * CT:(tau + 1) * CT],
                                in_=evb[:, sl * CT:(sl + 1) * CT]))
                        else:
                            e.dma(lambda g, hd=hd, tau=tau, sl=sl: g.dma_start(
                                out=usink(hd, tau),
                                in_=evf[:, sl * CT:(sl + 1) * CT]))
                        if k % 2 == 1:
                            e.dflush(f"pst_{fc}_{tau}")
                        else:
                            e.dflush(f"pst2_{fc}_{tau}")
            e.dflush("p1done")
            for i in range(NBLK):
                for hd in range(NHD):
                    e.need(f"oev_{i}_{hd}")
                    e.dma(lambda g, i=i, hd=hd: g.dma_start(
                        out=osink(i, hd),
                        in_=oev[:, ((i % 2) * NHD + hd) * BLK:((i % 2) * NHD + hd + 1) * BLK]))
                e.dflush(f"ost_{i}")

        def e_tensor(e):
            e.need("const_ld")
            e.sig(lambda t: t.matmul(
                ps_bc[0][:, 0:NHD], lhsT=oner_s[:, :], rhs=slp[0:1, :],
                start=True, stop=True), "slpb")
            for tau in range(NCT):
                e.need(f"x2ld_{tau}")
                if tau > 0:
                    e.need(f"rowrd_{tau-1}")
                lhs_x = onec_s if fused is None else onec_b
                if fused is not None:
                    e.need("onecb")
                for dc in range(NDC):
                    e.run(lambda t, dc=dc, lhs_x=lhs_x: t.matmul(
                        ps_sum[:, :], lhsT=lhs_x[:, :], rhs=x2s(dc),
                        start=(dc == 0), stop=(dc == NDC - 1)))
                for dc in range(NDC):
                    e.need(f"sq_{tau}_{dc}")
                    e.sig(lambda t, dc=dc: t.matmul(
                        ps_ssq[:, :], lhsT=onec_s[:, :],
                        rhs=sq[:, (dc % 2) * CT:((dc % 2) + 1) * CT],
                        start=(dc == 0), stop=(dc == NDC - 1)),
                        f"ssqmm_{tau}_{dc}" if dc < NDC - 1 else f"stats_{tau}")
                e.need(f"rstd_{tau}")
                e.need(f"mrs_{tau}")
                if tau > 0:
                    e.need(f"lnapply_{tau-1}")
                if tau == 0:
                    e.need("ddtpre")
                e.sig(lambda t: t.matmul(
                    ps_bc[0][:, :], lhsT=oner_s[:, :], rhs=row("rstd"),
                    start=True, stop=True), f"rstdb_{tau}")
                e.sig(lambda t: t.matmul(
                    ps_bc[1][:, :], lhsT=oner_s[:, :], rhs=row("mrs"),
                    start=True, stop=True), f"mrsb_{tau}")
            for fcg in range(4):
                e.need(f"wqg_{fcg}")
                for tau in range(NCT):
                    kk = fcg * NCT + tau
                    e.need(f"xnld_{fcg}_{tau}")
                    for fc8 in range(8):
                        fc = fcg * 8 + fc8
                        k = kk * 8 + fc8
                        pg = k % 2
                        if k >= 2:
                            pk = k - 2
                            pfc = (pk // NCT // 8) * 8 + pk % 8 if False else None
                            e.need(f"pev_{(pk//8//NCT)*8+pk%8}_{(pk//8)%NCT}")
                        for dc in range(NDC):
                            e.sig(lambda t, dc=dc, kk=kk, pg=pg, fc8=fc8: t.matmul(
                                ps_qk[pg][:, :],
                                lhsT=wqs[:, (fc8 * NDC + dc) * 128:(fc8 * NDC + dc + 1) * 128],
                                rhs=xnt[:, ((kk % 2) * NDC + dc) * CT:((kk % 2) * NDC + dc + 1) * CT],
                                start=(dc == 0), stop=(dc == NDC - 1)),
                                f"pmm_{fc}_{tau}" if dc == NDC - 1 else None)
            for i in range(NBLK):
                e.need(f"qkvld_{i}")
                for hd in range(NHD):
                    if i * NHD + hd >= 1:
                        pk = i * NHD + hd - 1
                        e.need(f"ktmev_{pk//NHD}_{pk%NHD}")
                        e.need(f"vtmev_{pk//NHD}_{pk%NHD}")
                    for half in range(2):
                        e.sig(lambda t, i=i, hd=hd, half=half: t.transpose(
                            ps_tp[:, half * 128:(half + 1) * 128],
                            in_=qkvslot(i, hd, 1)[:, half * 128:(half + 1) * 128],
                            identity=kvbf[:, :]) if False else t.transpose(
                            ps_tp[:, half * 128:(half + 1) * 128],
                            in_=qkvslot(i, hd, 1)[:, half * 128:(half + 1) * 128],
                            identity=identb[:, :]),
                            f"ktp_{i}_{hd}" if half == 1 else None)
                    for half in range(2):
                        e.sig(lambda t, i=i, hd=hd, half=half: t.transpose(
                            ps_tp2[:, half * 128:(half + 1) * 128],
                            in_=qkvslot(i, hd, 2)[:, half * 128:(half + 1) * 128],
                            identity=identb[:, :]),
                            f"vtp_{i}_{hd}" if half == 1 else None)
                    if i * NHD + hd >= 2:
                        pk = i * NHD + hd - 2
                        e.need(f"qkdm_{pk//NHD}_{pk%NHD}")
                    for half in range(2):
                        e.sig(lambda t, i=i, hd=hd, half=half: t.matmul(
                            ps_qk[half][:, 0:BLK],
                            lhsT=qkvslot(i, hd, 1)[:, half * 128:(half + 1) * 128],
                            rhs=qkvslot(i, hd, 0),
                            start=True, stop=True),
                            f"qkmm_{i}_{hd}" if half == 1 else None)
                    e.need(f"ktmev_{i}_{hd}")
                    e.need(f"vtmev_{i}_{hd}")
                    if i * NHD + hd >= 1:
                        pk = i * NHD + hd - 1
                        e.need(f"kvup_{pk//NHD}_{pk%NHD}")
                    for half in range(2):
                        e.sig(lambda t, half=half: t.matmul(
                            ps_bc[1][:, 0:128], lhsT=ktm[:, half * 128:(half + 1) * 128],
                            rhs=vtm[:, half * 128:(half + 1) * 128],
                            start=(half == 0), stop=(half == 1)),
                            f"kvmm_{i}_{hd}" if half == 1 else None)
                    e.need(f"kvbf_{i}_{hd}")
                    e.need(f"qpb_{i}_{hd}")
                    if i * NHD + hd >= 1:
                        pk = i * NHD + hd - 1
                        e.need(f"oev_{pk//NHD}_{pk%NHD}")
                    e.sig(lambda t: t.matmul(
                        ps_bc[0][:, 0:BLK], lhsT=kvbf[:, :], rhs=qpb[:, :],
                        start=True, stop=False))
                    e.need(f"qkdm_{i}_{hd}")
                    for half in range(2):
                        e.sig(lambda t, half=half: t.matmul(
                            ps_bc[0][:, 0:BLK], lhsT=vtm[:, half * 128:(half + 1) * 128],
                            rhs=qkd[:, half * BLK:(half + 1) * BLK],
                            start=False, stop=(half == 1)),
                            f"odone_{i}_{hd}" if half == 1 else None)

        def e_scalar(e):
            e.need("const_ld")
            e.need("idx_pre")
            e.need("slpneg")
            for hd in range(NHD):
                if hd >= 1:
                    e.need(f"ddtw_{hd-1}_1")
                for half in range(2):
                    e.sig(lambda s, hd=hd, half=half: s.activation(
                        out=ddsc[:, half * BLK:(half + 1) * BLK],
                        in_=idxc[:, half * BLK:(half + 1) * BLK], func=AF.Exp,
                        scale=nslpc[:, hd:hd + 1]),
                        f"ddexp_{hd}_{half}")
            e.need("qdi_pre")
            for hd in range(NHD):
                e.sig(lambda s, hd=hd: s.activation(
                    out=qdf[:, :], in_=qdi[:, :], func=AF.Exp,
                    scale=nslpc[:, hd:hd + 1]), f"qdexp_{hd}")
                e.fence()
                e.need(f"qdbw_{hd}")
            e.need("kdi_pre")
            for hd in range(NHD):
                e.sig(lambda s, hd=hd: s.activation(
                    out=kdc[:, hd * 2:hd * 2 + 2], in_=kdi[:, :], func=AF.Exp,
                    scale=nslpc[:, hd:hd + 1]), f"kdexp_{hd}")
            e.sig(lambda s: s.activation(
                out=bdc[:, :], in_=slpc[:, :], func=AF.Exp, scale=-256.0), "bdexp")
            for tau in range(NCT):
                e.need(f"x2ld_{tau}")
                for dc in range(NDC):
                    if dc >= 2:
                        e.need(f"ssqmm_{tau}_{dc-2}")
                    elif tau > 0:
                        e.need(f"stats_{tau-1}")
                    e.sig(lambda s, dc=dc: s.activation(
                        out=sq[:, (dc % 2) * CT:((dc % 2) + 1) * CT],
                        in_=x2s(dc), func=AF.Square), f"sq_{tau}_{dc}")
                e.need(f"var_{tau}")
                if tau > 0:
                    e.need(f"rstdb_{tau-1}")
                e.sig(lambda s: s.activation(
                    out=row("t1"), in_=row("var"), func=AF.Ln, bias=cst_s[0:1, 1:2]))
                e.fence()
                e.sig(lambda s: s.activation(
                    out=row("rstd"), in_=row("t1"), func=AF.Exp, scale=-0.5),
                    f"rstd_{tau}")
            for fcg in range(4):
                for tau in range(NCT):
                    kk = fcg * NCT + tau
                    for fc8 in range(8):
                        fc = fcg * 8 + fc8
                        quarter = fc % 4
                        k = kk * 8 + fc8
                        pg = k % 2
                        sl = k % 4
                        e.need(f"pmm_{fc}_{tau}")
                        if k >= 4:
                            pk = k - 4
                            pfc = (pk // 8 // NCT) * 8 + pk % 8
                            ptau = (pk // 8) % NCT
                            if (pk % 2) == 1:
                                e.need(f"pst_{pfc}_{ptau}")
                            else:
                                e.need(f"pst2_{pfc}_{ptau}")
                        if quarter in (0, 1):
                            e.sig(lambda s, pg=pg, sl=sl: s.activation(
                                out=evb[:, sl * CT:(sl + 1) * CT], in_=ps_qk[pg][:, :],
                                func=AF.Silu), f"pev_{fc}_{tau}")
                        elif quarter == 2:
                            e.sig(lambda s, pg=pg, sl=sl: s.activation(
                                out=evb[:, sl * CT:(sl + 1) * CT], in_=ps_qk[pg][:, :],
                                func=AF.Identity), f"pev_{fc}_{tau}")
                        else:
                            e.sig(lambda s, pg=pg, sl=sl: s.activation(
                                out=evf[:, sl * CT:(sl + 1) * CT], in_=ps_qk[pg][:, :],
                                func=AF.Identity), f"pev_{fc}_{tau}")

        def e_vector(e):
            e.need("const_ld")
            e.sig(lambda v: v.tensor_copy(onec_b[:, :], onec_s[:, :]), "onecb")
            e.need("identload")
            e.need("slpb")
            e.fence()
            e.sig(lambda v: v.tensor_copy(slpc[:, :], ps_bc[0][:, 0:NHD]), "slpcp")
            e.fence()
            e.sig(lambda v: v.tensor_scalar(
                out=nslpc[:, :], in0=slpc[:, :], scalar1=-1.0, scalar2=None,
                op0=Op.mult), "slpneg")
            e.sig(lambda v: v.iota(idxt[:, 0:BLK], pattern=[[1, BLK]], base=0,
                                   channel_multiplier=-1) if False else v.engine_nop(), "iota_nop")
            e.need("iota_done")
            e.fence()
            e.sig(lambda v: v.tensor_scalar(
                out=idxc[:, :], in0=idxt[:, :], scalar1=0.0, scalar2=None,
                op0=Op.max), "idxclamp")
            e.sig(lambda v: v.tensor_scalar(
                out=mk01[:, :], in0=idxt[:, :], scalar1=0.0, scalar2=None,
                op0=Op.is_ge), "mk01_done")
            e.sig(lambda v: v.engine_nop(), "idx_pre")
            for hd in range(NHD):
                for half in range(2):
                    e.need(f"ddexp_{hd}_{half}")
                    e.fence()
                    e.sig(lambda v, hd=hd, half=half: v.tensor_tensor(
                        out=ddt[:, (hd * 2 + half) * BLK:(hd * 2 + half + 1) * BLK],
                        in0=ddsc[:, half * BLK:(half + 1) * BLK],
                        in1=mk01[:, half * BLK:(half + 1) * BLK], op=Op.mult),
                        f"ddtw_{hd}_{half}" if half == 1 else None)
            e.sig(lambda v: v.engine_nop(), "ddtpre")
            for hd in range(NHD):
                e.need(f"qdexp_{hd}")
                e.fence()
                e.sig(lambda v, hd=hd: v.tensor_copy(
                    qdb[:, hd * BLK:(hd + 1) * BLK], qdf[:, :]), f"qdbw_{hd}")
            e.sig(lambda v: v.memset(kvsb[:, :], 0.0), "kv0")
            for tau in range(NCT):
                e.need(f"stats_{tau}")
                e.fence()
                e.sig(lambda v: v.tensor_scalar(
                    out=row("nmu"), in0=ps_sum[:, :], scalar1=-1.0 / D,
                    scalar2=None, op0=Op.mult))
                e.sig(lambda v: v.tensor_scalar(
                    out=row("m2"), in0=ps_ssq[:, :], scalar1=1.0 / D,
                    scalar2=None, op0=Op.mult), f"rowrd_{tau}")
                e.fence()
                e.sig(lambda v: v.tensor_tensor(
                    out=row("t1"), in0=row("nmu"), in1=row("nmu"), op=Op.mult))
                e.fence()
                e.sig(lambda v: v.tensor_tensor(
                    out=row("var"), in0=row("m2"), in1=row("t1"), op=Op.subtract),
                    f"var_{tau}")
                e.need(f"rstd_{tau}")
                if tau > 0:
                    e.need(f"mrsb_{tau-1}")
                e.fence()
                e.sig(lambda v: v.tensor_tensor(
                    out=row("mrs"), in0=row("nmu"), in1=row("rstd"), op=Op.mult),
                    f"mrs_{tau}")
                e.need(f"rstdb_{tau}")
                e.need(f"mrsb_{tau}")
                if tau >= 2:
                    e.need(f"xnst_{tau-2}")
                for dc in range(NDC):
                    e.fence()
                    e.sig(lambda v, dc=dc: v.tensor_tensor(
                        out=sq[:, 0:CT], in0=x2s(dc), in1=ps_bc[0][:, :], op=Op.mult))
                    e.fence()
                    e.sig(lambda v, dc=dc, tau=tau: v.tensor_tensor(
                        out=xn2s(tau, dc), in0=sq[:, 0:CT], in1=ps_bc[1][:, :],
                        op=Op.add), f"lnapply_{tau}" if dc == NDC - 1 else None)
            for i in range(NBLK):
                for hd in range(NHD):
                    e.need(f"qkvld_{i}")
                    e.fence()
                    e.sig(lambda v, i=i, hd=hd: v.tensor_tensor(
                        out=qpb[:, :], in0=qkvslot(i, hd, 0),
                        in1=qdb[:, hd * BLK:(hd + 1) * BLK], op=Op.mult),
                        f"qpb_{i}_{hd}")
                    e.need(f"ktp_{i}_{hd}")
                    e.fence()
                    for half in range(2):
                        e.sig(lambda v, hd=hd, half=half: v.tensor_scalar(
                            out=ktm[:, half * 128:(half + 1) * 128],
                            in0=ps_tp[:, half * 128:(half + 1) * 128],
                            scalar1=kdc[:, hd * 2 + half:hd * 2 + half + 1],
                            scalar2=None, op0=Op.mult),
                            f"ktmev_{i}_{hd}" if half == 1 else None)
                    e.need(f"vtp_{i}_{hd}")
                    e.fence()
                    for half in range(2):
                        e.sig(lambda v, half=half: v.tensor_copy(
                            vtm[:, half * 128:(half + 1) * 128],
                            ps_tp2[:, half * 128:(half + 1) * 128]),
                            f"vtmev_{i}_{hd}" if half == 1 else None)
                    e.need(f"qkmm_{i}_{hd}")
                    e.fence()
                    for half in range(2):
                        e.sig(lambda v, hd=hd, half=half: v.tensor_tensor(
                            out=qkd[:, half * BLK:(half + 1) * BLK],
                            in0=ps_qk[half][:, 0:BLK],
                            in1=ddt[:, (hd * 2 + half) * BLK:(hd * 2 + half + 1) * BLK],
                            op=Op.mult), f"qkdm_{i}_{hd}" if half == 1 else None)
                    if i == 0 and hd == 0:
                        e.need("kv0")
                    e.fence()
                    e.sig(lambda v, hd=hd: v.tensor_copy(
                        kvbf[:, :], kvsb[:, hd * 128:(hd + 1) * 128]), f"kvbf_{i}_{hd}")
                    e.need(f"kvmm_{i}_{hd}")
                    e.fence()
                    e.sig(lambda v, hd=hd: v.scalar_tensor_tensor(
                        out=kvsb[:, hd * 128:(hd + 1) * 128],
                        in0=kvsb[:, hd * 128:(hd + 1) * 128],
                        scalar=bdc[:, hd:hd + 1], in1=ps_bc[1][:, 0:128],
                        op0=Op.mult, op1=Op.add), f"kvup_{i}_{hd}")
                    e.need(f"odone_{i}_{hd}")
                    if i >= 2:
                        e.need(f"ost_{i-2}")
                    e.fence()
                    e.sig(lambda v, i=i, hd=hd: v.tensor_copy(
                        oev[:, ((i % 2) * NHD + hd) * BLK:((i % 2) * NHD + hd + 1) * BLK],
                        ps_bc[0][:, 0:BLK]), f"oev_{i}_{hd}")

        identb = sb("identb", [128, 128], BF16)
        identd = nc.dram_tensor("identd", [128, 128], F32, kind="ExternalInput").ap()

        def e_gpsimd2(e):
            e.dma(lambda g: g.dma_start(out=identb[:, :], in_=identd))
            e.dflush("identload")
            e.sig(lambda g: g.iota(idxt[:, 0:BLK], pattern=[[1, BLK]], base=0,
                                   channel_multiplier=-1))
            e.sig(lambda g: g.iota(idxt[:, BLK:2 * BLK], pattern=[[1, BLK]], base=-128,
                                   channel_multiplier=-1))
            e.sig(lambda g: g.iota(qdi[:, :], pattern=[[1, BLK]], base=1,
                                   channel_multiplier=0))
            e.sig(lambda g: g.engine_nop(), "qdi_pre")
            e.sig(lambda g: g.iota(kdi[:, 0:1], pattern=[[0, 1]], base=255,
                                   channel_multiplier=-1))
            e.sig(lambda g: g.iota(kdi[:, 1:2], pattern=[[0, 1]], base=127,
                                   channel_multiplier=-1))
            e.sig(lambda g: g.engine_nop(), "kdi_pre")
            e.sig(lambda g: g.engine_nop(), "iota_done")
            e_gpsimd(e)

        ems = {"sync": e_sync, "tensor": e_tensor, "vector": e_vector,
               "scalar": e_scalar, "gpsimd": e_gpsimd2}
        if fused is not None:
            return ems
        build_program(nc, ems)
    return nc


def lightning_inputs(inp, hh):
    W = inp["L1_attn_W_qkvu"]
    rowsel = []
    for hd in range(NHD):
        g = hh * NHD + hd
        for quarter in range(4):
            base = quarter * 2048 + g * 128
            rowsel.append(W[base:base + 128])
    Wsel = np.stack(rowsel, 0)
    strips = np.ascontiguousarray(
        Wsel.reshape(4 * NHD, 128, NDC, 128).transpose(0, 3, 2, 1).reshape(4 * NHD, 128, NDC * 128))
    slopes = inp["L1_attn_slope_rate"].reshape(16)[hh * NHD:(hh + 1) * NHD]
    return {
        "wq": strips,
        "slopes": slopes.reshape(1, NHD).astype(np.float32),
        "onec": np.ones((128, 1), np.float32),
        "oner": np.ones((1, 128), np.float32),
        "cst": np.tile(np.array([[1e-5, 1e-6, 0.0, 0.0]], np.float32), (128, 1)),
        "identd": np.eye(128, dtype=np.float32),
    }


def run_lightning(x2_full, inp):
    nc = build_lightning()
    shared = [lightning_inputs(inp, hh) for hh in range(2)]
    in_maps = []
    for c in range(8):
        b, hh = divmod(c, 2)
        m = dict(shared[hh])
        m["x2f"] = fm(x2_full[b])
        in_maps.append(m)
    res = run_bass_kernel_spmd(nc, in_maps, core_ids=list(range(8)))
    o = np.zeros((4, 4096, D), np.float32)
    u = np.zeros((4, 4096, D), np.float32)
    for c in range(8):
        b, hh = divmod(c, 2)
        ofm = res.results[c]["oo"].reshape(NHD * 128, TCTX)
        ufm = res.results[c]["uo"].reshape(NHD * 128, TCTX)
        o[b, :, hh * 1024:(hh + 1) * 1024] = ofm.T
        u[b, :, hh * 1024:(hh + 1) * 1024] = ufm.T
    return o, u



NEX = 10
EGHC = 11


def build_tail(fused=None):
    nc = bass.Bass("TRN2", target_bir_lowering=False) if fused is None else fused.nc
    if fused is None:
        x2 = nc.dram_tensor("x2", [NDC, 128, TLOC], F32, kind="ExternalInput").ap()
        oo = nc.dram_tensor("oo", [NDC, 128, TLOC], F32, kind="ExternalInput").ap()
        uo = nc.dram_tensor("uo", [NDC, 128, TLOC], F32, kind="ExternalInput").ap()
        oidx = uidx = None
    else:
        x2 = fused.x2
        oo = uo = None
        oidx = nc.dram_tensor("oidx", [128, NDC], I32, kind="ExternalInput").ap()
        uidx = nc.dram_tensor("uidx", [128, NDC], I32, kind="ExternalInput").ap()
    wo = nc.dram_tensor("wo", [NDC, 128, NDC * 128], F32, kind="ExternalInput").ap()
    gw = nc.dram_tensor("gw", [128, NDC * 8], F32, kind="ExternalInput").ap()
    w13 = nc.dram_tensor("w13", [NEX, EGHC, 2, 128, NDC * 128], F32, kind="ExternalInput").ap()
    w2s = nc.dram_tensor("w2s", [NEX, EGHC, 128, NDC * 128], F32, kind="ExternalInput").ap()
    onec = _shared_const(nc, fused, "onec", [128, 1])
    oner = _shared_const(nc, fused, "oner", [1, 128])
    cst = _shared_const(nc, fused, "cst", [128, 4])
    x4 = nc.dram_tensor("x4", [NDC, 128, TLOC], F32, kind="ExternalOutput").ap()
    oug = fused.oug if fused is not None else None
    x3s = nc.dram_tensor("x3s", [NDC, 128, TLOC], F32).ap()
    ogd = nc.dram_tensor("ogd", [NDC, 128, TLOC], BF16).ap()
    hscr = nc.dram_tensor("hscr_d", [EGHC, NT, 128, TT], BF16).ap()

    with ExitStack() as st:
        def sb(name, shape, dt):
            return st.enter_context(nc.sbuf_tensor(name, shape, dt,
                                                   allow_name_mangling=True))

        def psb(name, shape):
            return st.enter_context(nc.psum_tensor(name, shape, F32,
                                                   allow_name_mangling=True))

        xn3 = sb("xn3", [128, NDC * TLOC], BF16)
        x3t = sb("x3t", [128, NDC * TT], F32)
        dwsb = sb("dwsb", [8, TLOC], F32)
        gw_s = sb("gw_s", [128, NDC * 8], F32)
        gwsumc = sb("gwsumc", [8, 1], F32)
        rawsb = sb("rawsb", [8, TT], F32)
        lgt = sb("lgt", [8, TT], F32)
        exs = sb("exs", [8, TT], F32)
        srow = sb("srow", [1, 3 * TT], F32)
        scod = sb("scod", [8, TT], F32)
        eqt = sb("eqt", [8, TT], F32)
        onec_s = sb("onec_s", [128, 1], F32)
        oner_s = sb("oner_s", [1, 128], F32)
        cst_s = sb("cst_s", [128, 4], F32)
        OUDT = F32 if fused is None else BF16
        idx_s = sb("idx_s", [128, 2 * NDC], I32)
        _p1 = ExitStack()
        rows = _p1.enter_context(nc.sbuf_tensor("rows", [1, 6 * TT], F32, allow_name_mangling=True))
        sq = _p1.enter_context(nc.sbuf_tensor("sq", [128, 2 * TT], F32, allow_name_mangling=True))
        old = _p1.enter_context(nc.sbuf_tensor("old", [128, 2 * TT], OUDT, allow_name_mangling=True))
        uld = _p1.enter_context(nc.sbuf_tensor("uld", [128, 2 * TT], OUDT, allow_name_mangling=True))
        og = _p1.enter_context(nc.sbuf_tensor("og", [128, 2 * TT], BF16, allow_name_mangling=True))
        wos = _p1.enter_context(nc.sbuf_tensor("wos", [128, 8 * NDC * 128], BF16, allow_name_mangling=True))
        oglt = _p1.enter_context(nc.sbuf_tensor("oglt", [128, 2 * 8 * TT], BF16, allow_name_mangling=True))
        x2t = _p1.enter_context(nc.sbuf_tensor("x2t", [128, 2 * TT], F32, allow_name_mangling=True))
        x3lo = _p1.enter_context(nc.sbuf_tensor("x3lo", [128, 2 * TT], F32, allow_name_mangling=True))
        _p1.close()
        wgbf = sb("wgbf", [128, 2 * NDC * 128], BF16)
        wubf = sb("wubf", [128, 2 * NDC * 128], BF16)
        wdbf = sb("wdbf", [128, EGHC * NDC * 128], BF16)
        hrd = sb("hrd", [128, EGHC * TT], BF16)
        gt = sb("gt", [128, 2 * TT], BF16)
        ut = sb("ut", [128, 2 * TT], BF16)
        hb = sb("hb", [128, 2 * TT], BF16)
        dwrow = sb("dwrow", [1, NT * TT], F32)

        ps_sum = psb("ps_sum", [1, TT])
        ps_ssq = psb("ps_ssq", [1, TT])
        ps_bc = [psb(f"ps_bc{i}", [128, TT]) for i in range(2)]
        ps_w = [psb(f"ps_w{i}", [128, TT]) for i in range(2)]
        ps_r8 = psb("ps_r8", [128, TT])
        ps_rms = psb("ps_rms", [1, TT])

        def row(name):
            base = {"nmu": 0, "m2": 1, "t1": 2, "var": 3, "rstd": 4, "mrs": 5}[name]
            return rows[0:1, base * TT:(base + 1) * TT]

        def x3sl(dc):
            return x3t[:, dc * TT:(dc + 1) * TT]

        def xn3s(dc, tau):
            return xn3[:, (dc * NT + tau) * TT:(dc * NT + tau + 1) * TT]

        def ogs(k):
            return og[:, (k % 2) * TT:((k % 2) + 1) * TT]

        def emit_ou_loads_tau(e, tau, store_cb=None):
            if True:
                if tau > 0:
                    e.need(f"ogm_{tau-1}_{NDC-1}")
                for ec in range(NDC):
                    k = tau * NDC + ec
                    if k >= 2:
                        e.need(f"osq_{(k-2)//NDC}_{(k-2)%NDC}")
                    if fused is None:
                        e.dma(lambda g, ec=ec, tau=tau, k=k: g.dma_start(
                            out=old[:, (k % 2) * TT:((k % 2) + 1) * TT],
                            in_=oo[ec, :, tau * TT:(tau + 1) * TT]))
                    else:
                        e.dma(lambda g, ec=ec, tau=tau, k=k: g.indirect_dma_start(
                            out=old[:, (k % 2) * TT:((k % 2) + 1) * TT],
                            out_offset=None,
                            in_=oug, element_offset=tau * TT,
                            in_offset=bass.IndirectOffsetOnAxis(
                                ap=idx_s[:, ec:ec + 1], axis=0)))
                    e.dflush(f"old_{tau}_{ec}")
                e.need(f"osq_{tau}_{NDC-1}")
                for ec in range(NDC):
                    k = tau * NDC + ec
                    if k >= 1:
                        e.need(f"ogm_{(k-1)//NDC}_{(k-1)%NDC}")
                    if fused is None:
                        e.dma(lambda g, ec=ec, tau=tau, k=k: g.dma_start(
                            out=old[:, (k % 2) * TT:((k % 2) + 1) * TT],
                            in_=oo[ec, :, tau * TT:(tau + 1) * TT]))
                        e.dma(lambda g, ec=ec, tau=tau, k=k: g.dma_start(
                            out=uld[:, (k % 2) * TT:((k % 2) + 1) * TT],
                            in_=uo[ec, :, tau * TT:(tau + 1) * TT]))
                    else:
                        e.dma(lambda g, ec=ec, tau=tau, k=k: g.indirect_dma_start(
                            out=old[:, (k % 2) * TT:((k % 2) + 1) * TT],
                            out_offset=None,
                            in_=oug, element_offset=tau * TT,
                            in_offset=bass.IndirectOffsetOnAxis(
                                ap=idx_s[:, ec:ec + 1], axis=0)))
                        e.dma(lambda g, ec=ec, tau=tau, k=k: g.indirect_dma_start(
                            out=uld[:, (k % 2) * TT:((k % 2) + 1) * TT],
                            out_offset=None,
                            in_=oug, element_offset=tau * TT,
                            in_offset=bass.IndirectOffsetOnAxis(
                                ap=idx_s[:, NDC + ec:NDC + ec + 1], axis=0)))
                    e.dflush(f"ould_{tau}_{ec}")
                    if store_cb is not None:
                        store_cb(e, tau, ec)

        def emit_ou_loads(e):
            for tau in range(NT):
                emit_ou_loads_tau(e, tau)

        def e_sync(e):
            if fused is None:
                emit_ou_loads(e)
            for tau in range(NT):
                e.need(f"ogfin_{tau}")
                if tau >= 2:
                    e.need(f"wolomm_{tau-2}_{NDC-1}")
                for e8 in range(8):
                    e.dma(lambda g, e8=e8, tau=tau: g.dma_start(
                        out=oglt[:, ((tau % 2) * 8 + e8) * TT:((tau % 2) * 8 + e8 + 1) * TT],
                        in_=ogd[e8, :, tau * TT:(tau + 1) * TT]))
                e.dflush(f"oglod_{tau}")
                for dc in range(NDC):
                    k = tau * NDC + dc
                    if k >= 2:
                        e.need(f"x3evlo_{(k-2)//NDC}_{(k-2)%NDC}")
                    e.dma(lambda g, dc=dc, tau=tau, k=k: g.dma_start(
                        out=x2t[:, (k % 2) * TT:((k % 2) + 1) * TT],
                        in_=x2[dc, :, tau * TT:(tau + 1) * TT]))
                    e.dflush(f"x2ld_{tau}_{dc}")
            for tau in range(NT):
                e.need(f"x3lost_{tau}")
                if tau >= 2:
                    e.need(f"wohimm_{tau-2}_{NDC-1}")
                for e8 in range(8):
                    e.dma(lambda g, e8=e8, tau=tau: g.dma_start(
                        out=oglt[:, ((tau % 2) * 8 + e8) * TT:((tau % 2) * 8 + e8 + 1) * TT],
                        in_=ogd[8 + e8, :, tau * TT:(tau + 1) * TT]))
                e.dflush(f"oghid_{tau}")
                for dc in range(NDC):
                    k = tau * NDC + dc
                    if k >= 2:
                        e.need(f"x3ev_{(k-2)//NDC}_{(k-2)%NDC}")
                    e.dma(lambda g, dc=dc, tau=tau, k=k: g.dma_start(
                        out=x3lo[:, (k % 2) * TT:((k % 2) + 1) * TT],
                        in_=x3s[dc, :, tau * TT:(tau + 1) * TT]))
                    e.dflush(f"x3lold_{tau}_{dc}")
            for ex in range(NEX):
                for tau in range(NT):
                    k = ex * NT + tau
                    if k >= 1:
                        pe, ptau = divmod(k - 1, NT)
                        e.need(f"dmm_{pe}_{ptau}_{NDC-1}")
                    e.need(f"hstf_{ex}")
                    for ic in range(EGHC):
                        e.dma(lambda g, ex=ex, tau=tau, ic=ic: g.dma_start(
                            out=hrd[:, ic * TT:(ic + 1) * TT],
                            in_=hscr[ic, tau]))
                    e.dflush(f"hld_{ex}_{tau}")
                    if ex == 0:
                        if tau == 0:
                            e.need(f"lnapply_{NT-1}")
                            e.need(f"rawmm_{NT-1}")
                            e.need(f"x3st_{NT-1}")
                        else:
                            e.need(f"x4f_{0}_{tau-1}")
                        for dc in range(NDC):
                            e.dma(lambda g, dc=dc, tau=tau: g.dma_start(
                                out=x3sl(dc), in_=x3s[dc, :, tau * TT:(tau + 1) * TT]))
                        e.dflush(f"x3r_{tau}")

        def e_gpsimd(e):
            e.dma(lambda g: g.dma_start(out=onec_s[:, :], in_=onec))
            e.dma(lambda g: g.dma_start(out=oner_s[:, :], in_=oner))
            e.dma(lambda g: g.dma_start(out=cst_s[:, :], in_=cst))
            e.dma(lambda g: g.dma_start(out=gw_s[:, :], in_=gw))
            if fused is not None:
                e.dma(lambda g: g.dma_start(out=idx_s[:, 0:NDC], in_=oidx))
                e.dma(lambda g: g.dma_start(out=idx_s[:, NDC:2 * NDC], in_=uidx))
            e.dflush("const_ld")
            if fused is not None:
                e.need("const_ld")
            def og_store(e2, tau, ec):
                k = tau * NDC + ec
                e2.need(f"ogm_{tau}_{ec}")
                e2.dma(lambda g, ec=ec, tau=tau, k=k: g.dma_start(
                    out=ogd[ec, :, tau * TT:(tau + 1) * TT],
                    in_=og[:, (k % 2) * TT:((k % 2) + 1) * TT]))
                e2.dflush(f"ogstq_{tau}_{ec}")

            for tau in range(NT):
                if fused is not None:
                    emit_ou_loads_tau(e, tau, store_cb=og_store)
                else:
                    for ec in range(NDC):
                        og_store(e, tau, ec)
                e.dflush(f"ogfin_{tau}")
            for e8 in range(8):
                e.dma(lambda g, e8=e8: g.dma_start(
                    out=wos[:, e8 * NDC * 128:(e8 + 1) * NDC * 128], in_=wo[e8]))
            e.dflush("wolo")
            for tau in range(NT):
                for dc in range(NDC):
                    e.need(f"x3evlo_{tau}_{dc}")
                    e.dma(lambda g, dc=dc, tau=tau: g.dma_start(
                        out=x3s[dc, :, tau * TT:(tau + 1) * TT],
                        in_=x3t[:, dc * TT:(dc + 1) * TT]))
                e.dflush(f"x3lost_{tau}")
            e.need(f"wolomm_{NT-1}_{NDC-1}")
            for e8 in range(8):
                e.dma(lambda g, e8=e8: g.dma_start(
                    out=wos[:, e8 * NDC * 128:(e8 + 1) * NDC * 128], in_=wo[8 + e8]))
            e.dflush("wohi")
            for tau in range(NT):
                for dc in range(NDC):
                    e.need(f"x3ev_{tau}_{dc}")
                    e.need(f"x3sq_{tau}_{dc}")
                    e.dma(lambda g, dc=dc, tau=tau: g.dma_start(
                        out=x3s[dc, :, tau * TT:(tau + 1) * TT],
                        in_=x3t[:, dc * TT:(dc + 1) * TT]))
                e.dflush(f"x3st_{tau}")
                e.need(f"lgt_{tau}")
                e.sig(lambda g: g.tensor_reduce(
                    out=srow[0:1, 0:TT], in_=lgt[:, :], axis=mybir.AxisListType.C,
                    op=Op.max), f"m1_{tau}")
                e.need(f"exs_{tau}")
                e.sig(lambda g: g.tensor_reduce(
                    out=srow[0:1, TT:2 * TT], in_=exs[:, :], axis=mybir.AxisListType.C,
                    op=Op.add), f"sume_{tau}")
                e.need(f"mskt_{tau}")
                e.sig(lambda g: g.tensor_reduce(
                    out=srow[0:1, 2 * TT:3 * TT], in_=eqt[:, :], axis=mybir.AxisListType.C,
                    op=Op.max), f"m2_{tau}")
            for ex in range(NEX):
                if ex < 8:
                    if ex > 0:
                        e.need(f"dwb_{(ex-1)*EGHC+EGHC-1}_{NT-1}")
                    for tau in range(NT):
                        e.need(f"dw_{tau}")
                        e.dma(lambda g, ex=ex, tau=tau: g.dma_start(
                            out=dwrow[:, tau * TT:(tau + 1) * TT],
                            in_=dwsb[ex:ex + 1, tau * TT:(tau + 1) * TT]))
                    e.dflush(f"dwr_{ex}")
                for ihc in range(EGHC):
                    hc = ex * EGHC + ihc
                    ws = (hc % 2) * NDC * 128
                    if hc >= 2:
                        e.need(f"gmm_{hc-2}_{NT-1}")
                    e.dma(lambda g, ex=ex, ihc=ihc, ws=ws: g.dma_start(
                        out=wgbf[:, ws:ws + NDC * 128], in_=w13[ex, ihc, 0]))
                    e.dflush(f"wgld_{hc}")
                    if hc >= 2:
                        e.need(f"umm_{hc-2}_{NT-1}")
                    e.dma(lambda g, ex=ex, ihc=ihc, ws=ws: g.dma_start(
                        out=wubf[:, ws:ws + NDC * 128], in_=w13[ex, ihc, 1]))
                    e.dflush(f"wuld_{hc}")
                    for tau in range(NT):
                        pg = (hc * NT + tau) % 2
                        e.need(f"hmul_{hc}_{tau}")
                        if ex > 0:
                            e.need(f"hld_{ex-1}_{tau}")
                        e.dma(lambda g, ihc=ihc, tau=tau, pg=pg: g.dma_start(
                            out=hscr[ihc, tau], in_=hb[:, pg * TT:(pg + 1) * TT]))
                        e.dflush(f"hstq_{hc}_{tau}")
                e.dflush(f"hstf_{ex}")
                if ex > 0:
                    e.need(f"dmm_{ex-1}_{NT-1}_{NDC-1}")
                for ic in range(EGHC):
                    e.dma(lambda g, ex=ex, ic=ic: g.dma_start(
                        out=wdbf[:, ic * NDC * 128:(ic + 1) * NDC * 128],
                        in_=w2s[ex, ic]))
                    e.dflush(f"wdld_{ex}_{ic}")
                for tau in range(NT):
                    for dc in range(NDC):
                        e.need(f"devac_{ex}_{tau}_{dc}")
                        if ex == 0:
                            e.dma(lambda g, dc=dc, tau=tau: g.dma_start(
                                out=x4[dc, :, tau * TT:(tau + 1) * TT],
                                in_=x3sl(dc)))
                        else:
                            e.dma(lambda g, dc=dc, tau=tau: g.dma_start(
                                out=x4[dc, :, tau * TT:(tau + 1) * TT],
                                in_=x3sl(dc), accum_op=Op.add))
                    e.dflush(f"x4f_{ex}_{tau}")

        def e_tensor(e):
            e.need("const_ld")
            for dc in range(NDC):
                e.sig(lambda t, dc=dc: t.matmul(
                    ps_r8[0:8, 0:1], lhsT=gw_s[:, dc * 8:(dc + 1) * 8], rhs=onec_s[:, :],
                    start=(dc == 0), stop=(dc == NDC - 1)),
                    "gwsum_mm" if dc == NDC - 1 else None)
            for tau in range(NT):
                if tau > 0:
                    e.need(f"rmsrd_{tau-1}")
                for ec in range(NDC):
                    e.need(f"osq_{tau}_{ec}")
                    e.sig(lambda t, ec=ec: t.matmul(
                        ps_rms[:, :], lhsT=onec_s[:, :],
                        rhs=sq[:, (ec % 2) * TT:((ec % 2) + 1) * TT],
                        start=(ec == 0), stop=(ec == NDC - 1)),
                        f"rmsmm_{tau}_{ec}" if ec < NDC - 1 else f"rmsstat_{tau}")
                e.need(f"rrow_{tau}")
                if tau > 0:
                    e.need(f"ogm_{tau-1}_{NDC-1}")
                e.sig(lambda t: t.matmul(
                    ps_bc[0][:, :], lhsT=oner_s[:, :], rhs=row("rstd"),
                    start=True, stop=True), f"rbb_{tau}")
            e.need("wolo")
            for tau in range(NT):
                e.need(f"oglod_{tau}")
                for dc in range(NDC):
                    k = tau * NDC + dc
                    pw = k % 2
                    if k >= 2:
                        e.need(f"x3evlo_{(k-2)//NDC}_{(k-2)%NDC}")
                    for e8 in range(8):
                        e.sig(lambda t, dc=dc, tau=tau, e8=e8, pw=pw: t.matmul(
                            ps_w[pw][:, :],
                            lhsT=wos[:, (e8 * NDC + dc) * 128:(e8 * NDC + dc + 1) * 128],
                            rhs=oglt[:, ((tau % 2) * 8 + e8) * TT:((tau % 2) * 8 + e8 + 1) * TT],
                            start=(e8 == 0), stop=(e8 == 7)),
                            f"wolomm_{tau}_{dc}" if e8 == 7 else None)
            e.need("wohi")
            for tau in range(NT):
                e.need(f"oghid_{tau}")
                if tau > 0:
                    e.need(f"rowrd_{tau-1}")
                    e.need(f"rawrd_{tau-1}")
                for dc in range(NDC):
                    k = tau * NDC + dc
                    pw = k % 2
                    if k >= 2:
                        e.need(f"x3ev_{(k-2)//NDC}_{(k-2)%NDC}")
                    for e8 in range(8):
                        e.sig(lambda t, dc=dc, tau=tau, e8=e8, pw=pw: t.matmul(
                            ps_w[pw][:, :],
                            lhsT=wos[:, (e8 * NDC + dc) * 128:(e8 * NDC + dc + 1) * 128],
                            rhs=oglt[:, ((tau % 2) * 8 + e8) * TT:((tau % 2) * 8 + e8 + 1) * TT],
                            start=(e8 == 0), stop=(e8 == 7)),
                            f"wohimm_{tau}_{dc}" if e8 == 7 else None)
                for dc in range(NDC):
                    e.need(f"x3ev_{tau}_{dc}")
                    e.run(lambda t, dc=dc: t.matmul(
                        ps_sum[:, :], lhsT=onec_s[:, :], rhs=x3sl(dc),
                        start=(dc == 0), stop=(dc == NDC - 1)))
                    e.need(f"x3sq_{tau}_{dc}")
                    e.sig(lambda t, dc=dc: t.matmul(
                        ps_ssq[:, :], lhsT=onec_s[:, :],
                        rhs=sq[:, (dc % 2) * TT:((dc % 2) + 1) * TT],
                        start=(dc == 0), stop=(dc == NDC - 1)),
                        f"ssqmm_{tau}_{dc}" if dc < NDC - 1 else f"stats_{tau}")
                    e.sig(lambda t, dc=dc: t.matmul(
                        ps_r8[0:8, 0:TT], lhsT=gw_s[:, dc * 8:(dc + 1) * 8], rhs=x3sl(dc),
                        start=(dc == 0), stop=(dc == NDC - 1)),
                        f"rawmm_{tau}" if dc == NDC - 1 else None)
                e.need(f"rstd_{tau}")
                e.need(f"mrs_{tau}")
                if tau > 0:
                    e.need(f"lnapply_{tau-1}")
                e.sig(lambda t: t.matmul(
                    ps_bc[0][:, :], lhsT=oner_s[:, :], rhs=row("rstd"),
                    start=True, stop=True), f"rstdb_{tau}")
                e.sig(lambda t: t.matmul(
                    ps_bc[1][:, :], lhsT=oner_s[:, :], rhs=row("mrs"),
                    start=True, stop=True), f"mrsb_{tau}")
                e.need(f"rawrd_{tau}")
                e.need(f"m1_{tau}")
                e.sig(lambda t: t.matmul(
                    ps_r8[0:8, 0:TT], lhsT=oner_s[0:1, 0:8], rhs=srow[0:1, 0:TT],
                    start=True, stop=True), f"m1b_{tau}")
                e.need(f"recip_{tau}")
                e.need(f"exsub_{tau}")
                e.sig(lambda t: t.matmul(
                    ps_r8[0:8, 0:TT], lhsT=oner_s[0:1, 0:8], rhs=srow[0:1, TT:2 * TT],
                    start=True, stop=True), f"recb_{tau}")
                e.need(f"scomul_{tau}")
                e.sig(lambda t: t.matmul(
                    ps_r8[0:8, 0:TT], lhsT=oner_s[0:1, 0:8], rhs=srow[0:1, 0:TT],
                    start=True, stop=True), f"m1sb_{tau}")
                e.need(f"eqm_{tau}")
                e.need(f"m2_{tau}")
                e.sig(lambda t: t.matmul(
                    ps_r8[0:8, 0:TT], lhsT=oner_s[0:1, 0:8], rhs=srow[0:1, 2 * TT:3 * TT],
                    start=True, stop=True), f"m2b_{tau}")
            for ex in range(NEX):
                for ihc in range(EGHC):
                    hc = ex * EGHC + ihc
                    ws = (hc % 2) * NDC * 128
                    e.need(f"wgld_{hc}")
                    e.need(f"wuld_{hc}")
                    for tau in range(NT):
                        k = hc * NT + tau
                        e.need(f"lnapply_{tau}")
                        if ex < 8:
                            e.need(f"dwr_{ex}")
                            if k >= 1:
                                pk = k - 1
                                e.need(f"hmul_{pk//NT}_{pk%NT}")
                            e.sig(lambda t, tau=tau: t.matmul(
                                ps_r8[:, 0:TT], lhsT=oner_s[:, :],
                                rhs=dwrow[0:1, tau * TT:(tau + 1) * TT],
                                start=True, stop=True), f"dwb_{hc}_{tau}")
                        if k >= 1:
                            pk = k - 1
                            e.need(f"gevac_{pk//NT}_{pk%NT}")
                        for dc in range(NDC):
                            e.sig(lambda t, dc=dc, tau=tau, ws=ws: t.matmul(
                                ps_bc[0][:, :],
                                lhsT=wgbf[:, ws + dc * 128:ws + (dc + 1) * 128],
                                rhs=xn3s(dc, tau),
                                start=(dc == 0), stop=(dc == NDC - 1)),
                                f"gmm_{hc}_{tau}" if dc == NDC - 1 else None)
                        if k >= 1:
                            pk = k - 1
                            e.need(f"uevac_{pk//NT}_{pk%NT}")
                        for dc in range(NDC):
                            e.sig(lambda t, dc=dc, tau=tau, ws=ws: t.matmul(
                                ps_bc[1][:, :],
                                lhsT=wubf[:, ws + dc * 128:ws + (dc + 1) * 128],
                                rhs=xn3s(dc, tau),
                                start=(dc == 0), stop=(dc == NDC - 1)),
                                f"umm_{hc}_{tau}" if dc == NDC - 1 else None)
                for tau in range(NT):
                    e.need(f"hld_{ex}_{tau}")
                    for dc in range(NDC):
                        k = ex * NT * NDC + tau * NDC + dc
                        pd = k % 2
                        if k >= 2:
                            pk = k - 2
                            e.need(f"devac_{pk//(NT*NDC)}_{(pk%(NT*NDC))//NDC}_{pk%NDC}")
                        for ic in range(EGHC):
                            e.need(f"wdld_{ex}_{ic}")
                            e.sig(lambda t, dc=dc, ic=ic, pd=pd: t.matmul(
                                ps_w[pd][:, :],
                                lhsT=wdbf[:, (ic * NDC + dc) * 128:(ic * NDC + dc + 1) * 128],
                                rhs=hrd[:, ic * TT:(ic + 1) * TT],
                                start=(ic == 0), stop=(ic == EGHC - 1)),
                                f"dmm_{ex}_{tau}_{dc}" if ic == EGHC - 1 else None)

        def e_scalar(e):
            e.need("const_ld")
            for tau in range(NT):
                for ec in range(NDC):
                    k = tau * NDC + ec
                    e.need(f"old_{tau}_{ec}")
                    if k >= 2:
                        pk = k - 2
                        e.need(f"rmsmm_{pk//NDC}_{pk%NDC}" if pk % NDC < NDC - 1
                               else f"rmsstat_{pk//NDC}")
                    e.sig(lambda s, k=k: s.activation(
                        out=sq[:, (k % 2) * TT:((k % 2) + 1) * TT],
                        in_=old[:, (k % 2) * TT:((k % 2) + 1) * TT], func=AF.Square),
                        f"osq_{tau}_{ec}")
                e.need(f"rmsrow_{tau}")
                if tau > 0:
                    e.need(f"rbb_{tau-1}")
                e.sig(lambda s: s.activation(
                    out=row("t1"), in_=row("var"), func=AF.Ln, bias=cst_s[0:1, 1:2]))
                e.fence()
                e.sig(lambda s: s.activation(
                    out=row("rstd"), in_=row("t1"), func=AF.Exp, scale=-0.5),
                    f"rrow_{tau}")
            for tau in range(NT):
                for dc in range(NDC):
                    k = tau * NDC + dc
                    e.need(f"x3ev_{tau}_{dc}")
                    if k >= 2:
                        pk = k - 2
                        e.need(f"ssqmm_{pk//NDC}_{pk%NDC}" if pk % NDC < NDC - 1
                               else f"stats_{pk//NDC}")
                    e.sig(lambda s, dc=dc, k=k: s.activation(
                        out=sq[:, (k % 2) * TT:((k % 2) + 1) * TT],
                        in_=x3sl(dc), func=AF.Square), f"x3sq_{tau}_{dc}")
                e.need(f"var_{tau}")
                if tau > 0:
                    e.need(f"rstdb_{tau-1}")
                e.sig(lambda s: s.activation(
                    out=row("t1"), in_=row("var"), func=AF.Ln, bias=cst_s[0:1, 0:1]))
                e.fence()
                e.sig(lambda s: s.activation(
                    out=row("rstd"), in_=row("t1"), func=AF.Exp, scale=-0.5),
                    f"rstd_{tau}")
                e.need(f"exsub_{tau}")
                e.fence()
                e.sig(lambda s: s.activation(
                    out=exs[:, :], in_=scod[:, :], func=AF.Exp), f"exs_{tau}")
            for ex in range(NEX):
                for ihc in range(EGHC):
                    hc = ex * EGHC + ihc
                    for tau in range(NT):
                        k = hc * NT + tau
                        pg = k % 2
                        e.need(f"gmm_{hc}_{tau}")
                        if k >= 2:
                            e.need(f"hmul_{(k-2)//NT}_{(k-2)%NT}")
                        e.sig(lambda s, pg=pg: s.activation(
                            out=gt[:, pg * TT:(pg + 1) * TT], in_=ps_bc[0][:, :],
                            func=AF.Silu), f"gevac_{hc}_{tau}")
                        e.need(f"umm_{hc}_{tau}")
                        e.sig(lambda s, pg=pg: s.activation(
                            out=ut[:, pg * TT:(pg + 1) * TT], in_=ps_bc[1][:, :],
                            func=AF.Identity), f"uevac_{hc}_{tau}")

        def e_vector(e):
            e.need("gwsum_mm")
            e.fence()
            e.sig(lambda v: v.tensor_copy(gwsumc[:, :], ps_r8[0:8, 0:1]), "gwsum_rd")
            for tau in range(NT):
                e.need(f"rmsstat_{tau}")
                e.fence()
                e.sig(lambda v: v.tensor_scalar(
                    out=row("var"), in0=ps_rms[:, :], scalar1=1.0 / D, scalar2=None,
                    op0=Op.mult), f"rmsrow_{tau}")
                e.sig(lambda v: v.engine_nop(), f"rmsrd_{tau}")
                e.need(f"rbb_{tau}")
                for ec in range(NDC):
                    k = tau * NDC + ec
                    e.need(f"ould_{tau}_{ec}")
                    if k >= 2:
                        e.need(f"ogstq_{(k-2)//NDC}_{(k-2)%NDC}")
                    e.fence()
                    e.sig(lambda v, k=k: v.tensor_tensor(
                        out=uld[:, (k % 2) * TT:((k % 2) + 1) * TT],
                        in0=old[:, (k % 2) * TT:((k % 2) + 1) * TT],
                        in1=uld[:, (k % 2) * TT:((k % 2) + 1) * TT], op=Op.mult))
                    e.fence()
                    e.sig(lambda v, k=k: v.tensor_tensor(
                        out=og[:, (k % 2) * TT:((k % 2) + 1) * TT],
                        in0=uld[:, (k % 2) * TT:((k % 2) + 1) * TT],
                        in1=ps_bc[0][:, :], op=Op.mult), f"ogm_{tau}_{ec}")
            for tau in range(NT):
                if tau > 0:
                    e.need(f"x3lost_{tau-1}")
                for dc in range(NDC):
                    k = tau * NDC + dc
                    pw = k % 2
                    e.need(f"wolomm_{tau}_{dc}")
                    e.need(f"x2ld_{tau}_{dc}")
                    e.fence()
                    e.sig(lambda v, dc=dc, pw=pw, k=k: v.tensor_tensor(
                        out=x3sl(dc), in0=ps_w[pw][:, :],
                        in1=x2t[:, (k % 2) * TT:((k % 2) + 1) * TT], op=Op.add),
                        f"x3evlo_{tau}_{dc}")
            for tau in range(NT):
                if tau == 0:
                    e.need(f"x3lost_{NT-1}")
                else:
                    e.need(f"x3st_{tau-1}")
                for dc in range(NDC):
                    k = tau * NDC + dc
                    pw = k % 2
                    e.need(f"wohimm_{tau}_{dc}")
                    e.need(f"x3lold_{tau}_{dc}")
                    e.fence()
                    e.sig(lambda v, dc=dc, pw=pw, k=k: v.tensor_tensor(
                        out=x3sl(dc), in0=ps_w[pw][:, :],
                        in1=x3lo[:, (k % 2) * TT:((k % 2) + 1) * TT], op=Op.add),
                        f"x3ev_{tau}_{dc}")
                e.need(f"stats_{tau}")
                e.fence()
                e.sig(lambda v: v.tensor_scalar(
                    out=row("nmu"), in0=ps_sum[:, :], scalar1=-1.0 / D,
                    scalar2=None, op0=Op.mult))
                e.sig(lambda v: v.tensor_scalar(
                    out=row("m2"), in0=ps_ssq[:, :], scalar1=1.0 / D,
                    scalar2=None, op0=Op.mult), f"rowrd_{tau}")
                e.fence()
                e.sig(lambda v: v.tensor_tensor(
                    out=row("t1"), in0=row("nmu"), in1=row("nmu"), op=Op.mult))
                e.fence()
                e.sig(lambda v: v.tensor_tensor(
                    out=row("var"), in0=row("m2"), in1=row("t1"), op=Op.subtract),
                    f"var_{tau}")
                e.need(f"rstd_{tau}")
                if tau > 0:
                    e.need(f"mrsb_{tau-1}")
                e.fence()
                e.sig(lambda v: v.tensor_tensor(
                    out=row("mrs"), in0=row("nmu"), in1=row("rstd"), op=Op.mult),
                    f"mrs_{tau}")
                e.need(f"rstdb_{tau}")
                e.need(f"mrsb_{tau}")
                for dc in range(NDC):
                    e.fence()
                    e.sig(lambda v, dc=dc: v.tensor_tensor(
                        out=sq[:, 0:TT], in0=x3sl(dc), in1=ps_bc[0][:, :], op=Op.mult))
                    e.fence()
                    e.sig(lambda v, dc=dc, tau=tau: v.tensor_tensor(
                        out=xn3s(dc, tau), in0=sq[:, 0:TT], in1=ps_bc[1][:, :],
                        op=Op.add), f"lnapply_{tau}" if dc == NDC - 1 else None)
                e.need(f"rawmm_{tau}")
                e.fence()
                e.sig(lambda v: v.tensor_copy(rawsb[:, :], ps_r8[0:8, 0:TT]),
                      f"rawrd_{tau}")
                e.fence()
                e.sig(lambda v: v.tensor_tensor(
                    out=rawsb[:, :], in0=rawsb[:, :], in1=ps_bc[0][0:8, :], op=Op.mult))
                e.fence()
                e.sig(lambda v: v.scalar_tensor_tensor(
                    out=lgt[:, :], in0=ps_bc[1][0:8, :], scalar=gwsumc[:, :],
                    in1=rawsb[:, :], op0=Op.mult, op1=Op.add), f"lgt_{tau}")
                e.need(f"m1b_{tau}")
                e.fence()
                e.sig(lambda v: v.tensor_tensor(
                    out=scod[:, :], in0=lgt[:, :], in1=ps_r8[0:8, 0:TT], op=Op.subtract),
                    f"exsub_{tau}")
                e.need(f"sume_{tau}")
                e.fence()
                e.sig(lambda v: v.reciprocal(
                    out=srow[0:1, TT:2 * TT], in_=srow[0:1, TT:2 * TT]), f"recip_{tau}")
                e.need(f"recb_{tau}")
                e.need(f"exs_{tau}")
                e.fence()
                e.sig(lambda v: v.tensor_tensor(
                    out=scod[:, :], in0=exs[:, :], in1=ps_r8[0:8, 0:TT], op=Op.mult),
                    f"scomul_{tau}")
                e.need(f"m1sb_{tau}")
                e.fence()
                e.sig(lambda v: v.tensor_tensor(
                    out=eqt[:, :], in0=lgt[:, :], in1=ps_r8[0:8, 0:TT], op=Op.is_equal),
                    f"eqm_{tau}")
                e.fence()
                e.sig(lambda v: v.scalar_tensor_tensor(
                    out=eqt[:, :], in0=eqt[:, :], scalar=-1e30, in1=scod[:, :],
                    op0=Op.mult, op1=Op.add), f"mskt_{tau}")
                e.need(f"m2b_{tau}")
                e.fence()
                e.sig(lambda v: v.tensor_tensor(
                    out=eqt[:, :], in0=scod[:, :], in1=ps_r8[0:8, 0:TT], op=Op.is_ge))
                e.fence()
                e.sig(lambda v, tau=tau: v.tensor_tensor(
                    out=dwsb[:, tau * TT:(tau + 1) * TT], in0=scod[:, :], in1=eqt[:, :],
                    op=Op.mult), f"dw_{tau}")
            for ex in range(NEX):
                for ihc in range(EGHC):
                    hc = ex * EGHC + ihc
                    for tau in range(NT):
                        k = hc * NT + tau
                        pg = k % 2
                        e.need(f"gevac_{hc}_{tau}")
                        e.need(f"uevac_{hc}_{tau}")
                        if k >= 2:
                            e.need(f"hstq_{(k-2)//NT}_{(k-2)%NT}")
                        if ex < 8:
                            e.need(f"dwb_{hc}_{tau}")
                            e.fence()
                            e.sig(lambda v, pg=pg: v.tensor_tensor(
                                out=gt[:, pg * TT:(pg + 1) * TT],
                                in0=gt[:, pg * TT:(pg + 1) * TT],
                                in1=ut[:, pg * TT:(pg + 1) * TT], op=Op.mult))
                            e.fence()
                            e.sig(lambda v, pg=pg: v.tensor_tensor(
                                out=hb[:, pg * TT:(pg + 1) * TT],
                                in0=gt[:, pg * TT:(pg + 1) * TT],
                                in1=ps_r8[:, 0:TT],
                                op=Op.mult), f"hmul_{hc}_{tau}")
                        else:
                            e.fence()
                            e.sig(lambda v, pg=pg: v.tensor_tensor(
                                out=hb[:, pg * TT:(pg + 1) * TT],
                                in0=gt[:, pg * TT:(pg + 1) * TT],
                                in1=ut[:, pg * TT:(pg + 1) * TT], op=Op.mult),
                                f"hmul_{hc}_{tau}")
                for tau in range(NT):
                    for dc in range(NDC):
                        k = ex * NT * NDC + tau * NDC + dc
                        pd = k % 2
                        e.need(f"dmm_{ex}_{tau}_{dc}")
                        if ex == 0:
                            e.need(f"x3r_{tau}")
                            e.fence()
                            e.sig(lambda v, pd=pd, dc=dc: v.tensor_tensor(
                                out=x3sl(dc), in0=ps_w[pd][:, :], in1=x3sl(dc),
                                op=Op.add), f"devac_{ex}_{tau}_{dc}")
                        else:
                            if tau > 0:
                                e.need(f"x4f_{ex}_{tau-1}")
                            else:
                                e.need(f"x4f_{ex-1}_{NT-1}")
                            e.fence()
                            e.sig(lambda v, pd=pd, dc=dc: v.tensor_copy(
                                x3sl(dc), ps_w[pd][:, :]), f"devac_{ex}_{tau}_{dc}")

        ems = {"sync": e_sync, "tensor": e_tensor, "vector": e_vector,
               "scalar": e_scalar, "gpsimd": e_gpsimd}
        if fused is not None:
            return ems
        build_program(nc, ems)
    return nc


def tail_inputs(inp):
    Wout = inp["L1_attn_W_out"]
    wo = np.ascontiguousarray(
        Wout.reshape(NDC, 128, NDC, 128).transpose(2, 3, 0, 1).reshape(NDC, 128, NDC * 128))
    gwm = inp["L1_ffn_gate_w"]
    gws = np.ascontiguousarray(gwm.reshape(8, NDC, 128).transpose(1, 2, 0).reshape(128 * NDC, 8))
    gws = np.ascontiguousarray(gwm.T.reshape(NDC, 128, 8).transpose(1, 0, 2).reshape(128, NDC * 8))
    def lhsT_strips(w, nmc):
        return np.ascontiguousarray(
            w.reshape(nmc, 128, NDC, 128).transpose(0, 3, 2, 1).reshape(nmc, 128, NDC * 128))
    w13 = np.zeros((NEX, EGHC, 2, 128, NDC * 128), np.float32)
    w2 = np.zeros((NEX, EGHC, 128, NDC * 128), np.float32)
    for ex in range(8):
        w13[ex, :, 0] = lhsT_strips(inp["L1_ffn_w1"][ex], EGHC)
        w13[ex, :, 1] = lhsT_strips(inp["L1_ffn_w3"][ex], EGHC)
        w2[ex] = np.ascontiguousarray(
            inp["L1_ffn_w2"][ex].reshape(NDC, 128, EGHC, 128).transpose(2, 3, 0, 1)
            .reshape(EGHC, 128, NDC * 128))
    for half in range(2):
        sl = slice(half * 1408, (half + 1) * 1408)
        w13[8 + half, :, 0] = lhsT_strips(inp["L1_ffn_sw1"][sl], EGHC)
        w13[8 + half, :, 1] = lhsT_strips(inp["L1_ffn_sw3"][sl], EGHC)
        w2[8 + half] = np.ascontiguousarray(
            inp["L1_ffn_sw2"][:, sl].reshape(NDC, 128, EGHC, 128).transpose(2, 3, 0, 1)
            .reshape(EGHC, 128, NDC * 128))
    return {
        "wo": wo, "gw": gws, "w13": w13, "w2s": w2,
        "onec": np.ones((128, 1), np.float32),
        "oner": np.ones((1, 128), np.float32),
        "cst": np.tile(np.array([[1e-5, 1e-6, 0.0, 0.0]], np.float32), (128, 1)),
    }


def run_tail(x2_full, o_full, u_full, inp):
    nc = build_tail()
    shared = tail_inputs(inp)
    in_maps = []
    for c in range(8):
        b, h = divmod(c, 2)
        m = dict(shared)
        m["x2"] = fm(x2_full[b, h * TLOC:(h + 1) * TLOC])
        m["oo"] = fm(o_full[b, h * TLOC:(h + 1) * TLOC])
        m["uo"] = fm(u_full[b, h * TLOC:(h + 1) * TLOC])
        in_maps.append(m)
    res = run_bass_kernel_spmd(nc, in_maps, core_ids=list(range(8)))
    out = np.zeros((4, 4096, D), np.float32)
    for c in range(8):
        b, h = divmod(c, 2)
        out[b, h * TLOC:(h + 1) * TLOC] = res.results[c]["x4"].reshape(D, TLOC).T
    return out



PAIRS = [[0, 1], [2, 3], [4, 5], [6, 7]]


def build_fused():
    from concourse.aot_env import allow_ici_all_to_all
    nc = bass.Bass("TRN2", target_bir_lowering=False)
    fio = FusedIO()
    fio.nc = nc
    A = build_mamba(fused=fio)
    B = build_ffn0(fused=fio)
    x2b = nc.dram_tensor("x2b", [NDC * 128, TLOC], BF16).ap()
    xg = nc.dram_tensor("xg", [2 * NDC * 128, TLOC], BF16).ap()
    fio.xg = xg
    C = build_lightning(fused=fio)
    oug = nc.dram_tensor("oug", [2 * 2 * TLOC, TLOC], BF16).ap()
    fio.oug = oug
    D = build_tail(fused=fio)
    stages = {"a": A, "b": B, "c": C, "d": D}

    def bar_end(e, s):
        e.fence()
        e.flush_all()
        e.sig(lambda x: x.nop(), f"{s}fin_{e.name}")

    def bar_start(e, s):
        for eng2 in ENGS:
            e.need(f"{s}fin_{eng2}")

    def glue_x2(e):
        e.dma(lambda g: g.dma_start(
            out=x2b.rearrange("(c p) t -> c p t", p=128), in_=fio.x2))
        e.dflush("x2cast")
        e.coll(lambda g: g.collective_compute(
            "AllGather", Op.bypass, replica_groups=PAIRS,
            ins=[x2b.opt()], outs=[xg.opt()]), "agx2")

    def glue_ou(e):
        e.coll(lambda g: g.collective_compute(
            "AllGather", Op.bypass, replica_groups=PAIRS,
            ins=[fio.oub.opt()], outs=[oug.opt()]), "agou")

    def mk(eng):
        def em(e):
            stages["a"][eng](PrefCtx(e, "a_"))
            bar_end(e, "a")
            bar_start(e, "a")
            stages["b"][eng](PrefCtx(e, "b_"))
            bar_end(e, "b")
            if eng == "gpsimd":
                glue_x2(e)
            bar_start(e, "b")
            e.need("agx2")
            stages["c"][eng](PrefCtx(e, "c_"))
            bar_end(e, "c")
            if eng == "gpsimd":
                glue_ou(e)
            bar_start(e, "c")
            e.need("agou")
            stages["d"][eng](PrefCtx(e, "d_"))
        return em

    build_program(nc, {eng: mk(eng) for eng in ENGS})
    return nc


def fused_inputs(inp):
    shared = {}
    shared.update(mamba_inputs(inp))
    shared.update(ffn0_inputs(inp))
    shared.update(tail_inputs(inp))
    lw = [lightning_inputs(inp, hh) for hh in range(2)]
    return shared, lw


def ou_indices(h):
    oidx = np.zeros((128, NDC), np.int32)
    uidx = np.zeros((128, NDC), np.int32)
    p = np.arange(128)
    for ec in range(NDC):
        slot, hdl = divmod(ec, 8)
        base = slot * 4096 + h * 2048 + hdl * 128
        oidx[:, ec] = base + p
        uidx[:, ec] = base + 1024 + p
    return oidx, uidx


def run_fused(inputs):
    nc = build_fused()
    shared, lw = fused_inputs(inputs)
    x0 = np.asarray(inputs["L0_attn_x"], np.float32)
    in_maps = []
    for c in range(8):
        b, h = divmod(c, 2)
        m = dict(shared)
        m.update(lw[h])
        m.update(mamba_core_inputs(x0, b, h))
        oidx, uidx = ou_indices(h)
        m["oidx"] = oidx
        m["uidx"] = uidx
        in_maps.append(m)
    res = run_bass_kernel_spmd(nc, in_maps, core_ids=list(range(8)))
    out = np.zeros((4, 4096, D), np.float32)
    for c in range(8):
        b, h = divmod(c, 2)
        out[b, h * TLOC:(h + 1) * TLOC] = res.results[c]["x4"].reshape(D, TLOC).T
    return out


def kernel(**inputs):
    return run_fused(inputs)


# version 03 (final)
# speedup vs baseline: 1.0711x; 1.0711x over previous
import numpy as np
import ml_dtypes
from contextlib import ExitStack

import concourse.bass as bass
import concourse.mybir as mybir
from concourse.bass_utils import run_bass_kernel_spmd
from concourse.alu_op_type import AluOpType as Op

F32 = mybir.dt.float32
BF16 = mybir.dt.bfloat16
I32 = mybir.dt.int32
AF = mybir.ActivationFunctionType
BF = ml_dtypes.bfloat16

ENGS = ("sync", "tensor", "vector", "scalar", "gpsimd")

D = 2048
TLOC = 2048
NDC = 16


class Prog:
    def __init__(self):
        self.mode = "count"
        self.marks = {}
        self.reset_counters()
        self.sems = {}

    def reset_counters(self):
        self.ccount = {e: 0 for e in ENGS}
        self.dcount = {e: 0 for e in ENGS}
        self.ncoll = 0
        self.watermark = {}


class EngCtx:
    def __init__(self, prog: Prog, name: str, eng):
        self.p, self.name, self.eng = prog, name, eng

    def run(self, fn):
        if self.p.mode == "emit":
            fn(self.eng)

    def sig(self, fn, name=None):
        p = self.p
        p.ccount[self.name] += 1
        cnt = p.ccount[self.name]
        if p.mode == "count":
            if name is not None:
                assert name not in p.marks, f"duplicate mark {name}"
                p.marks[name] = ("c", self.name, cnt)
        else:
            if name is not None:
                assert p.marks[name] == ("c", self.name, cnt), \
                    f"pass mismatch at {name}"
            fn(self.eng).then_inc(p.sems[("c", self.name)], 1)

    def dma(self, fn):
        p = self.p
        p.dcount[self.name] += 16
        if p.mode == "emit":
            fn(self.eng).then_inc(p.sems[("d", self.name)], 16)

    def dflush(self, name):
        p = self.p
        p.ccount[self.name] += 1
        cnt = p.ccount[self.name]
        if p.mode == "count":
            assert name not in p.marks, f"duplicate mark {name}"
            p.marks[name] = ("c", self.name, cnt)
        else:
            assert p.marks[name] == ("c", self.name, cnt), f"pass mismatch at {name}"
            if p.dcount[self.name] > 0:
                self.eng.wait_ge(p.sems[("d", self.name)], p.dcount[self.name])
            self.eng.nop().then_inc(p.sems[("c", self.name)], 1)

    def fence(self):
        p = self.p
        if p.mode == "emit" and p.ccount[self.name] > 0:
            key = (self.name, "c", self.name)
            cnt = p.ccount[self.name]
            if p.watermark.get(key, -1) < cnt:
                self.eng.wait_ge(p.sems[("c", self.name)], cnt)
                p.watermark[key] = cnt

    def need(self, name):
        p = self.p
        if p.mode == "count":
            assert True
            return
        kind, peng, cnt = p.marks[name]
        if peng == self.name and kind == "c":
            return
        key = (self.name, kind, peng)
        if p.watermark.get(key, -1) >= cnt:
            return
        self.eng.wait_ge(p.sems[(kind, peng)], cnt)
        p.watermark[key] = cnt

    def coll(self, fn, name):
        p = self.p
        p.ccount[self.name] += 1
        cnt = p.ccount[self.name]
        ci = p.ncoll
        p.ncoll += 1
        if p.mode == "count":
            assert name not in p.marks
            p.marks[name] = ("c", self.name, cnt)
        else:
            assert p.marks[name] == ("c", self.name, cnt)
            fn(self.eng).then_inc(p.sems[("cc", ci)], 1)
            self.eng.wait_ge(p.sems[("cc", ci)], 1)
            self.eng.nop().then_inc(p.sems[("c", self.name)], 1)

    def flush_all(self):
        p = self.p
        if p.mode == "emit" and p.dcount[self.name] > 0:
            key = (self.name, "d", self.name)
            cnt = p.dcount[self.name]
            if p.watermark.get(key, -1) < cnt:
                self.eng.wait_ge(p.sems[("d", self.name)], cnt)
                p.watermark[key] = cnt

    def wait_done(self):
        p = self.p
        if p.mode == "emit" and p.dcount[self.name] > 0:
            self.eng.wait_ge(p.sems[("d", self.name)], p.dcount[self.name])


def build_program(nc, emitters):
    p = Prog()
    for e in ENGS:
        if e in emitters:
            emitters[e](EngCtx(p, e, None))
    p.mode = "emit"
    p.reset_counters()
    with ExitStack() as st:
        for e in ENGS:
            p.sems[("c", e)] = st.enter_context(nc.semaphore(f"c_{e}"))
            p.sems[("d", e)] = st.enter_context(nc.semaphore(f"d_{e}"))
        for ci in range(4):
            p.sems[("cc", ci)] = st.enter_context(nc.semaphore(f"cc_{ci}"))
        with nc.Block() as blk:
            for ename in ENGS:
                if ename not in emitters:
                    continue
                getattr(blk, ename)(_mk_body(p, ename, emitters[ename]))
    return p


class PrefCtx:

    def __init__(self, e, pre):
        self.e, self.pre = e, pre

    def run(self, fn):
        self.e.run(fn)

    def sig(self, fn, name=None):
        self.e.sig(fn, (self.pre + name) if name else None)

    def dma(self, fn):
        self.e.dma(fn)

    def dflush(self, name):
        self.e.dflush(self.pre + name)

    def need(self, name):
        self.e.need(self.pre + name)

    def fence(self):
        self.e.fence()


class FusedIO:
    pass


def _shared_const(nc, fused, name, shape):
    if fused is not None:
        cache = getattr(fused, "_consts", None)
        if cache is None:
            cache = {}
            fused._consts = cache
        if name in cache:
            return cache[name]
    t = nc.dram_tensor(name, shape, F32, kind="ExternalInput").ap()
    if fused is not None:
        cache[name] = t
    return t


def _mk_body(p, ename, emitter):
    def body(engine):
        ec = EngCtx(p, ename, engine)
        emitter(ec)
        ec.wait_done()
    return body




HID = 5632
NHC = 44
FFN_G = 4
GHC = NHC // FFN_G
NT = 4
TT = 512


def build_ffn0(fused=None):
    nc = bass.Bass("TRN2", target_bir_lowering=False) if fused is None else fused.nc
    if fused is None:
        x1 = nc.dram_tensor("x1", [NDC, 128, TLOC], F32, kind="ExternalInput").ap()
    else:
        x1 = fused.x1
    wg = nc.dram_tensor("wg", [NHC, 128, NDC * 128], I32, kind="ExternalInput").ap()
    wu = nc.dram_tensor("wu", [NHC, 128, NDC * 128], I32, kind="ExternalInput").ap()
    wd = nc.dram_tensor("wd", [NHC, 128, NDC * 128], I32, kind="ExternalInput").ap()
    sgc = nc.dram_tensor("sgc", [128, NHC], F32, kind="ExternalInput").ap()
    suc = nc.dram_tensor("suc", [128, NHC], F32, kind="ExternalInput").ap()
    sdc = nc.dram_tensor("sdc", [128, NDC], F32, kind="ExternalInput").ap()
    onec = _shared_const(nc, fused, "onec", [128, 1])
    cst = _shared_const(nc, fused, "cst", [128, 4])
    oner = _shared_const(nc, fused, "oner", [1, 128])
    if fused is None:
        x2 = nc.dram_tensor("x2", [NDC, 128, TLOC], F32, kind="ExternalOutput").ap()
    else:
        x2 = nc.dram_tensor("x2", [NDC, 128, TLOC], F32).ap()
        fused.x2 = x2
    hscr = nc.dram_tensor("hscr", [NHC, NT, 128, TT], BF16).ap()

    with ExitStack() as st:
        def sb(name, shape, dt):
            return st.enter_context(nc.sbuf_tensor(name, shape, dt,
                                                   allow_name_mangling=True))

        def psb(name, shape):
            return st.enter_context(nc.psum_tensor(name, shape, F32,
                                                   allow_name_mangling=True))

        xn = sb("xn", [128, NDC * TLOC], BF16)
        x1ln = sb("x1ln", [128, NDC * TT], F32)
        sq = sb("sq", [128, 2 * TT], F32)
        rows = sb("rows", [1, 6 * TT], F32)
        onec_s = sb("onec_s", [128, 1], F32)
        cst_s = sb("cst_s", [128, 4], F32)
        oner_s = sb("oner_s", [1, 128], F32)
        sgc_s = sb("sgc_s", [128, NHC], F32)
        suc_s = sb("suc_s", [128, NHC], F32)
        sdc_s = sb("sdc_s", [128, NDC], F32)
        wint = sb("wint", [128, NDC * 128], I32)
        wgbf = sb("wgbf", [128, 2 * NDC * 128], BF16)
        wubf = sb("wubf", [128, 2 * NDC * 128], BF16)
        wdbf = sb("wdbf", [128, GHC * NDC * 128], BF16)
        gt = sb("gt", [128, 2 * TT], BF16)
        ut = sb("ut", [128, 2 * TT], BF16)
        hb = sb("hb", [128, 2 * TT], BF16)
        appt = sb("appt", [128, TT], F32)
        hrd = sb("hrd", [128, GHC * TT], BF16)

        ps_sum = psb("ps_sum", [1, TT])
        ps_ssq = psb("ps_ssq", [1, TT])
        ps_bc = [psb(f"ps_bc{i}", [128, TT]) for i in range(2)]
        ps_g = [psb(f"ps_g{i}", [128, TT]) for i in range(2)]
        ps_u = [psb(f"ps_u{i}", [128, TT]) for i in range(2)]

        def row(name):
            base = {"nmu": 0, "m2": 1, "t1": 2, "var": 3, "rstd": 4, "mrs": 5}[name]
            return rows[0:1, base * TT:(base + 1) * TT]

        def xns(dc, tau):
            return xn[:, (dc * NT + tau) * TT:(dc * NT + tau + 1) * TT]

        def x1s(dc):
            return x1ln[:, dc * TT:(dc + 1) * TT]

        def e_sync(e):
            for tau in range(NT):
                if tau > 0:
                    e.need(f"lnapply_{tau-1}")
                    e.need(f"stats_{tau-1}")
                    e.need(f"sq_{tau-1}_{NDC-1}")
                for dc in range(NDC):
                    e.dma(lambda g, dc=dc, tau=tau: g.dma_start(
                        out=x1s(dc), in_=x1[dc, :, tau * TT:(tau + 1) * TT]))
                e.dflush(f"x1ld_{tau}")
            for g2 in range(FFN_G):
                for tau in range(NT):
                    k = g2 * NT + tau
                    if k >= 1:
                        pg, ptau = divmod(k - 1, NT)
                        e.need(f"dmm_{pg}_{ptau}_{NDC-1}")
                    e.need(f"hstf_{g2}")
                    for ic in range(GHC):
                        e.dma(lambda g, g2=g2, tau=tau, ic=ic: g.dma_start(
                            out=hrd[:, ic * TT:(ic + 1) * TT],
                            in_=hscr[g2 * GHC + ic, tau]))
                    e.dflush(f"hld_{g2}_{tau}")
                    if g2 == 0:
                        if tau == 0:
                            e.need(f"lnapply_{NT-1}")
                            e.need(f"stats_{NT-1}")
                            e.need(f"sq_{NT-1}_{NDC-1}")
                        else:
                            e.need(f"x2f_{0}_{tau-1}")
                        for dc in range(NDC):
                            e.dma(lambda g, dc=dc, tau=tau: g.dma_start(
                                out=x1s(dc), in_=x1[dc, :, tau * TT:(tau + 1) * TT]))
                        e.dflush(f"x1r_{tau}")

        def e_scalar(e):
            e.need("const_ld")
            for tau in range(NT):
                e.need(f"x1ld_{tau}")
                for dc in range(NDC):
                    if dc >= 2:
                        e.need(f"ssqmm_{tau}_{dc-2}")
                    elif tau > 0:
                        e.need(f"stats_{tau-1}")
                    e.sig(lambda s, dc=dc: s.activation(
                        out=sq[:, (dc % 2) * TT:((dc % 2) + 1) * TT],
                        in_=x1s(dc), func=AF.Square), f"sq_{tau}_{dc}")
                e.need(f"var_{tau}")
                if tau > 0:
                    e.need(f"rstdb_{tau-1}")
                e.sig(lambda s: s.activation(
                    out=row("t1"), in_=row("var"), func=AF.Ln, bias=cst_s[0:1, 0:1]))
                e.fence()
                e.sig(lambda s: s.activation(
                    out=row("rstd"), in_=row("t1"), func=AF.Exp, scale=-0.5),
                    f"rstd_{tau}")
            for hc in range(NHC):
                for tau in range(NT):
                    k = hc * NT + tau
                    pg = k % 2
                    e.need(f"gmm_{hc}_{tau}")
                    if k >= 2:
                        e.need(f"hmul_{(k-2)//NT}_{(k-2)%NT}")
                    e.sig(lambda s, hc=hc, pg=pg: s.activation(
                        out=gt[:, pg * TT:(pg + 1) * TT], in_=ps_g[pg][:, :],
                        func=AF.Silu, scale=sgc_s[:, hc:hc + 1]), f"gevac_{hc}_{tau}")
                    e.need(f"umm_{hc}_{tau}")
                    e.sig(lambda s, hc=hc, pg=pg: s.activation(
                        out=ut[:, pg * TT:(pg + 1) * TT], in_=ps_u[pg][:, :],
                        func=AF.Identity, scale=suc_s[:, hc:hc + 1]), f"uevac_{hc}_{tau}")

        def e_tensor(e):
            e.need("const_ld")
            for tau in range(NT):
                e.need(f"x1ld_{tau}")
                if tau > 0:
                    e.need(f"rowrd_{tau-1}")
                for dc in range(NDC):
                    e.run(lambda t, dc=dc: t.matmul(
                        ps_sum[:, :], lhsT=onec_s[:, :], rhs=x1s(dc),
                        start=(dc == 0), stop=(dc == NDC - 1)))
                for dc in range(NDC):
                    e.need(f"sq_{tau}_{dc}")
                    e.sig(lambda t, dc=dc: t.matmul(
                        ps_ssq[:, :], lhsT=onec_s[:, :],
                        rhs=sq[:, (dc % 2) * TT:((dc % 2) + 1) * TT],
                        start=(dc == 0), stop=(dc == NDC - 1)),
                        f"ssqmm_{tau}_{dc}" if dc < NDC - 1 else f"stats_{tau}")
                e.need(f"rstd_{tau}")
                e.need(f"mrs_{tau}")
                if tau > 0:
                    e.need(f"lnapply_{tau-1}")
                e.sig(lambda t: t.matmul(
                    ps_bc[0][:, :], lhsT=oner_s[:, :], rhs=row("rstd"),
                    start=True, stop=True), f"rstdb_{tau}")
                e.sig(lambda t: t.matmul(
                    ps_bc[1][:, :], lhsT=oner_s[:, :], rhs=row("mrs"),
                    start=True, stop=True), f"mrsb_{tau}")
            for g2 in range(FFN_G):
                for ihc in range(GHC):
                    hc = g2 * GHC + ihc
                    ws = (hc % 2) * NDC * 128
                    e.need(f"wgdq_{hc}")
                    e.need(f"wudq_{hc}")
                    for tau in range(NT):
                        k = hc * NT + tau
                        pg = k % 2
                        e.need(f"lnapply_{tau}")
                        if k >= 2:
                            e.need(f"gevac_{(k-2)//NT}_{(k-2)%NT}")
                        for dc in range(NDC):
                            e.sig(lambda t, dc=dc, tau=tau, pg=pg, ws=ws: t.matmul(
                                ps_g[pg][:, :],
                                lhsT=wgbf[:, ws + dc * 128:ws + (dc + 1) * 128],
                                rhs=xns(dc, tau),
                                start=(dc == 0), stop=(dc == NDC - 1)),
                                f"gmm_{hc}_{tau}" if dc == NDC - 1 else None)
                        if k >= 2:
                            e.need(f"uevac_{(k-2)//NT}_{(k-2)%NT}")
                        for dc in range(NDC):
                            e.sig(lambda t, dc=dc, tau=tau, pg=pg, ws=ws: t.matmul(
                                ps_u[pg][:, :],
                                lhsT=wubf[:, ws + dc * 128:ws + (dc + 1) * 128],
                                rhs=xns(dc, tau),
                                start=(dc == 0), stop=(dc == NDC - 1)),
                                f"umm_{hc}_{tau}" if dc == NDC - 1 else None)
                if g2 == 0:
                    e.need(f"lnapply_{NT-1}")
                for tau in range(NT):
                    e.need(f"hld_{g2}_{tau}")
                    for dc in range(NDC):
                        k = g2 * NT * NDC + tau * NDC + dc
                        pd = k % 2
                        if k >= 2:
                            pk = k - 2
                            e.need(f"devac_{pk//(NT*NDC)}_{(pk%(NT*NDC))//NDC}_{pk%NDC}")
                        for ic in range(GHC):
                            e.need(f"wddq_{g2}_{ic}")
                            e.sig(lambda t, dc=dc, ic=ic, pd=pd: t.matmul(
                                ps_bc[pd][:, :],
                                lhsT=wdbf[:, (ic * NDC + dc) * 128:(ic * NDC + dc + 1) * 128],
                                rhs=hrd[:, ic * TT:(ic + 1) * TT],
                                start=(ic == 0), stop=(ic == GHC - 1)),
                                f"dmm_{g2}_{tau}_{dc}" if ic == GHC - 1 else None)

        def e_vector(e):
            for tau in range(NT):
                e.need(f"stats_{tau}")
                e.fence()
                e.sig(lambda v: v.tensor_scalar(
                    out=row("nmu"), in0=ps_sum[:, :], scalar1=-1.0 / D,
                    scalar2=None, op0=Op.mult))
                e.sig(lambda v: v.tensor_scalar(
                    out=row("m2"), in0=ps_ssq[:, :], scalar1=1.0 / D,
                    scalar2=None, op0=Op.mult), f"rowrd_{tau}")
                e.fence()
                e.sig(lambda v: v.tensor_tensor(
                    out=row("t1"), in0=row("nmu"), in1=row("nmu"), op=Op.mult))
                e.fence()
                e.sig(lambda v: v.tensor_tensor(
                    out=row("var"), in0=row("m2"), in1=row("t1"), op=Op.subtract),
                    f"var_{tau}")
                e.need(f"rstd_{tau}")
                if tau > 0:
                    e.need(f"mrsb_{tau-1}")
                e.fence()
                e.sig(lambda v: v.tensor_tensor(
                    out=row("mrs"), in0=row("nmu"), in1=row("rstd"), op=Op.mult),
                    f"mrs_{tau}")
                e.need(f"rstdb_{tau}")
                e.need(f"mrsb_{tau}")
                for dc in range(NDC):
                    e.fence()
                    e.sig(lambda v, dc=dc: v.tensor_tensor(
                        out=appt[:, :], in0=x1s(dc), in1=ps_bc[0][:, :], op=Op.mult))
                    e.fence()
                    e.sig(lambda v, dc=dc, tau=tau: v.tensor_tensor(
                        out=xns(dc, tau), in0=appt[:, :], in1=ps_bc[1][:, :], op=Op.add),
                        f"lnapply_{tau}" if dc == NDC - 1 else None)
            for g2 in range(FFN_G):
                for ihc in range(GHC):
                    hc = g2 * GHC + ihc
                    ws = (hc % 2) * NDC * 128
                    e.need(f"wgld_{hc}")
                    if hc >= 2:
                        e.need(f"gmm_{hc-2}_{NT-1}")
                    e.sig(lambda v, ws=ws: v.tensor_scalar(
                        out=wgbf[:, ws:ws + NDC * 128], in0=wint[:, :],
                        scalar1=-128.0, scalar2=None, op0=Op.add), f"wgdq_{hc}")
                    e.need(f"wuld_{hc}")
                    if hc >= 2:
                        e.need(f"umm_{hc-2}_{NT-1}")
                    e.sig(lambda v, ws=ws: v.tensor_scalar(
                        out=wubf[:, ws:ws + NDC * 128], in0=wint[:, :],
                        scalar1=-128.0, scalar2=None, op0=Op.add), f"wudq_{hc}")
                    for tau in range(NT):
                        k = hc * NT + tau
                        pg = k % 2
                        e.need(f"gevac_{hc}_{tau}")
                        e.need(f"uevac_{hc}_{tau}")
                        if k >= 2:
                            e.need(f"hstq_{(k-2)//NT}_{(k-2)%NT}")
                        e.sig(lambda v, pg=pg: v.tensor_tensor(
                            out=hb[:, pg * TT:(pg + 1) * TT],
                            in0=gt[:, pg * TT:(pg + 1) * TT],
                            in1=ut[:, pg * TT:(pg + 1) * TT], op=Op.mult),
                            f"hmul_{hc}_{tau}")
                for ic in range(GHC):
                    e.need(f"wdld_{g2}_{ic}")
                    if g2 > 0:
                        e.need(f"dmm_{g2-1}_{NT-1}_{NDC-1}")
                    e.sig(lambda v, ic=ic: v.tensor_scalar(
                        out=wdbf[:, ic * NDC * 128:(ic + 1) * NDC * 128],
                        in0=wint[:, :],
                        scalar1=-128.0, scalar2=None, op0=Op.add), f"wddq_{g2}_{ic}")
                for tau in range(NT):
                    for dc in range(NDC):
                        k = g2 * NT * NDC + tau * NDC + dc
                        pd = k % 2
                        e.need(f"dmm_{g2}_{tau}_{dc}")
                        if g2 == 0:
                            e.need(f"x1r_{tau}")
                            e.sig(lambda v, pd=pd, dc=dc: v.scalar_tensor_tensor(
                                out=x1s(dc), in0=ps_bc[pd][:, :],
                                scalar=sdc_s[:, dc:dc + 1], in1=x1s(dc),
                                op0=Op.mult, op1=Op.add), f"devac_{g2}_{tau}_{dc}")
                        else:
                            if tau > 0:
                                e.need(f"x2f_{g2}_{tau-1}")
                            else:
                                e.need(f"x2f_{g2-1}_{NT-1}")
                            e.sig(lambda v, pd=pd, dc=dc: v.tensor_scalar(
                                out=x1s(dc), in0=ps_bc[pd][:, :],
                                scalar1=sdc_s[:, dc:dc + 1],
                                scalar2=None, op0=Op.mult), f"devac_{g2}_{tau}_{dc}")

        def e_gpsimd(e):
            e.dma(lambda g: g.dma_start(out=onec_s[:, :], in_=onec))
            e.dma(lambda g: g.dma_start(out=cst_s[:, :], in_=cst))
            e.dma(lambda g: g.dma_start(out=oner_s[:, :], in_=oner))
            e.dma(lambda g: g.dma_start(out=sgc_s[:, :], in_=sgc))
            e.dma(lambda g: g.dma_start(out=suc_s[:, :], in_=suc))
            e.dma(lambda g: g.dma_start(out=sdc_s[:, :], in_=sdc))
            e.dflush("const_ld")
            for g2 in range(FFN_G):
                for ihc in range(GHC):
                    hc = g2 * GHC + ihc
                    if ihc == 0 and g2 > 0:
                        e.need(f"wddq_{g2-1}_{GHC-1}")
                    elif hc >= 1:
                        e.need(f"wudq_{hc-1}")
                    e.dma(lambda g, hc=hc: g.dma_start(out=wint[:, :], in_=wg[hc]))
                    e.dflush(f"wgld_{hc}")
                    e.need(f"wgdq_{hc}")
                    e.dma(lambda g, hc=hc: g.dma_start(out=wint[:, :], in_=wu[hc]))
                    e.dflush(f"wuld_{hc}")
                    for tau in range(NT):
                        pg = (hc * NT + tau) % 2
                        e.need(f"hmul_{hc}_{tau}")
                        e.dma(lambda g, hc=hc, tau=tau, pg=pg: g.dma_start(
                            out=hscr[hc, tau], in_=hb[:, pg * TT:(pg + 1) * TT]))
                        e.dflush(f"hstq_{hc}_{tau}")
                e.dflush(f"hstf_{g2}")
                for ic in range(GHC):
                    if ic == 0:
                        e.need(f"wudq_{(g2 + 1) * GHC - 1}")
                    else:
                        e.need(f"wddq_{g2}_{ic-1}")
                    e.dma(lambda g, g2=g2, ic=ic: g.dma_start(
                        out=wint[:, :], in_=wd[g2 * GHC + ic]))
                    e.dflush(f"wdld_{g2}_{ic}")
                for tau in range(NT):
                    for dc in range(NDC):
                        e.need(f"devac_{g2}_{tau}_{dc}")
                        if g2 == 0:
                            e.dma(lambda g, dc=dc, tau=tau: g.dma_start(
                                out=x2[dc, :, tau * TT:(tau + 1) * TT], in_=x1s(dc)))
                        else:
                            e.dma(lambda g, dc=dc, tau=tau: g.dma_start(
                                out=x2[dc, :, tau * TT:(tau + 1) * TT], in_=x1s(dc),
                                accum_op=Op.add))
                    e.dflush(f"x2f_{g2}_{tau}")

        ems = {"sync": e_sync, "tensor": e_tensor, "vector": e_vector,
               "scalar": e_scalar, "gpsimd": e_gpsimd}
        if fused is not None:
            return ems
        build_program(nc, ems)
    return nc


def _tile_lhsT(w, nkc, nmc):
    M, K = w.shape
    assert M == nmc * 128 and K == nkc * 128
    return np.ascontiguousarray(
        w.reshape(nmc, 128, nkc, 128).transpose(0, 3, 2, 1).reshape(nmc, 128, nkc * 128))


def ffn0_inputs(inp):
    return {
        "wg": _tile_lhsT(inp["L0_ffn_wq_gate"], NDC, NHC),
        "wu": _tile_lhsT(inp["L0_ffn_wq_up"], NDC, NHC),
        "wd": _tile_lhsT_down(inp["L0_ffn_wq_down"]),
        "sgc": np.ascontiguousarray(inp["L0_ffn_s_gate"].reshape(NHC, 128).T),
        "suc": np.ascontiguousarray(inp["L0_ffn_s_up"].reshape(NHC, 128).T),
        "sdc": np.ascontiguousarray(inp["L0_ffn_s_down"].reshape(NDC, 128).T),
        "onec": np.ones((128, 1), np.float32),
        "cst": np.tile(np.array([[1e-5, 1e-6, 0.0, 0.0]], np.float32), (128, 1)),
        "oner": np.ones((1, 128), np.float32),
    }


def _tile_lhsT_down(w):
    return np.ascontiguousarray(
        w.reshape(NDC, 128, NHC, 128).transpose(2, 3, 0, 1).reshape(NHC, 128, NDC * 128))


def fm(x_core):
    return np.ascontiguousarray(x_core.T.reshape(NDC, 128, -1))


def run_ffn0(x1_full, inp):
    nc = build_ffn0()
    shared = ffn0_inputs(inp)
    in_maps = []
    for c in range(8):
        b, h = divmod(c, 2)
        m = dict(shared)
        m["x1"] = fm(x1_full[b, h * TLOC:(h + 1) * TLOC])
        in_maps.append(m)
    res = run_bass_kernel_spmd(nc, in_maps, core_ids=list(range(8)))
    out = np.zeros((4, 4096, D), np.float32)
    for c in range(8):
        b, h = divmod(c, 2)
        out[b, h * TLOC:(h + 1) * TLOC] = res.results[c]["x2"].reshape(D, TLOC).T
    return out



TCTX = 4096
TM = 256
NTI = TCTX // TM
RT0 = NTI // 2
NST = 16


def build_mamba(fused=None):
    nc = bass.Bass("TRN2", target_bir_lowering=False) if fused is None else fused.nc
    x0 = nc.dram_tensor("x0", [NDC, 128, TCTX], F32, kind="ExternalInput").ap()
    cw = nc.dram_tensor("cw", [128, NDC * 4], F32, kind="ExternalInput").ap()
    cb = nc.dram_tensor("cb", [128, NDC], F32, kind="ExternalInput").ap()
    xpw = nc.dram_tensor("xpw", [128, NDC * 160], F32, kind="ExternalInput").ap()
    dtw = nc.dram_tensor("dtw", [128, NDC * 128], F32, kind="ExternalInput").ap()
    dtb = nc.dram_tensor("dtb", [128, NDC], F32, kind="ExternalInput").ap()
    alog = nc.dram_tensor("alog", [128, NDC * NST], F32, kind="ExternalInput").ap()
    dpar = nc.dram_tensor("dpar", [128, NDC], F32, kind="ExternalInput").ap()
    bmaskd = nc.dram_tensor("bmask", [1, TCTX], F32, kind="ExternalInput").ap()
    onec = _shared_const(nc, fused, "onec", [128, 1])
    oner = _shared_const(nc, fused, "oner", [1, 128])
    cst = _shared_const(nc, fused, "cst", [128, 4])
    if fused is None:
        x1 = nc.dram_tensor("x1", [NDC, 128, TLOC], F32, kind="ExternalOutput").ap()
    else:
        x1 = nc.dram_tensor("x1", [NDC, 128, TLOC], F32).ap()
        fused.x1 = x1
    bcscr = nc.dram_tensor("bcscr", [2, 32, TM], F32).ap()

    HTM = TM + 3
    HT = TM // 2

    with ExitStack() as st:
        def sb(name, shape, dt):
            return st.enter_context(nc.sbuf_tensor(name, shape, dt,
                                                   allow_name_mangling=True))

        def psb(name, shape):
            return st.enter_context(nc.psum_tensor(name, shape, F32,
                                                   allow_name_mangling=True))

        x0t = sb("x0t", [128, NDC * TM], F32)
        x0rt = sb("x0rt", [128, NDC * TM], F32)
        sq = sb("sq", [128, NDC * TM], F32)
        rows = sb("rows", [1, 3 * TM], F32)
        xnh = sb("xnh", [128, 2 * NDC * HTM], BF16)
        xcv = sb("xcv", [128, 2 * NDC * TM], BF16)
        acc = sb("acc", [128, NDC * (TM // 2)], F32)
        dtin = sb("dtin", [128, 2 * TM], BF16)
        delta = sb("delta", [128, 2 * NDC * TM], BF16)
        sp1 = sb("sp1", [128, 2 * TM], F32)
        spA = sb("spA", [128, 2 * TM], F32)
        bcb = sb("bcb", [128, 32 * TM], BF16)
        bcrow = sb("bcrow", [32, TM], F32)
        bmask = sb("bmask_s", [1, TCTX], F32)
        abuf = sb("abuf", [128, 2 * NST * TM], BF16)
        bbuf = sb("bbuf", [128, NST * TM], BF16)
        hdc = sb("hdc", [128, 2 * NST * TM], BF16)
        ubuf = sb("ubuf", [128, TM], BF16)
        stt = sb("stt", [128, NDC * NST], F32)
        ytmp = sb("ytmp", [128, NST * TM], BF16)
        yred = sb("yred", [128, TM], F32)
        cw_s = sb("cw_s", [128, NDC * 4], F32)
        cb_s = sb("cb_s", [128, NDC], F32)
        xpw_s = sb("xpw_s", [128, NDC * 160], BF16)
        dtw_s = sb("dtw_s", [128, NDC * 128], BF16)
        dtb_s = sb("dtb_s", [128, NDC], F32)
        alog_s = sb("alog_s", [128, NDC * NST], F32)
        aneg = alog_s
        dpar_s = sb("dpar_s", [128, NDC], F32)
        onec_s = sb("onec_s", [128, 1], F32)
        oner_s = sb("oner_s", [1, 128], F32)
        cst_s = sb("cst_s", [128, 4], F32)

        ps_sum = psb("ps_sum", [1, TM])
        ps_ssq = psb("ps_ssq", [1, TM])
        ps_bc = [psb(f"ps_bc{i}", [128, TM]) for i in range(2)]
        ps_dt = psb("ps_dt", [128, TM])
        ps_misc = psb("ps_misc", [32, 2 * TM])
        ps_dl = [psb(f"ps_dl{i}", [128, TM]) for i in range(2)]

        def row(name):
            base = {"nmu": 0, "mrs": 0, "m2": 1, "var": 1, "t1": 2, "rstd": 2}[name]
            return rows[0:1, base * TM:(base + 1) * TM]

        def x0s(dc):
            return x0t[:, dc * TM:(dc + 1) * TM]

        def xnslot(i, dc, a, b):
            s = (i % 2) * NDC * HTM
            return xnh[:, s + dc * HTM + a:s + dc * HTM + b]

        def xcvs(i, dc):
            s = (i % 2) * NDC * TM
            return xcv[:, s + dc * TM:(s + dc * TM) + TM]

        def dls(i, dc):
            s = (i % 2) * NDC * TM
            return delta[:, s + dc * TM:(s + dc * TM) + TM]

        def e_sync(e):
            for i in range(NTI):
                if i > 0:
                    e.need(f"lnapply_{i-1}")
                    e.need(f"stats_{i-1}")
                    e.need(f"sq_{i-1}_{NDC-1}")
                for dc in range(NDC):
                    e.dma(lambda g, dc=dc, i=i: g.dma_start(
                        out=x0s(dc), in_=x0[dc, :, i * TM:(i + 1) * TM]))
                e.dflush(f"x0ld_{i}")
                if i >= RT0:
                    if i > RT0:
                        e.need(f"x1f_{i-1}")
                    for dc in range(NDC):
                        e.dma(lambda g, dc=dc, i=i: g.dma_start(
                            out=x0rt[:, dc * TM:(dc + 1) * TM],
                            in_=x0[dc, :, i * TM:(i + 1) * TM]))
                    e.dflush(f"x0rld_{i}")

        def e_gpsimd(e):
            e.dma(lambda g: g.dma_start(out=cw_s[:, :], in_=cw))
            e.dma(lambda g: g.dma_start(out=cb_s[:, :], in_=cb))
            e.dma(lambda g: g.dma_start(out=xpw_s[:, :], in_=xpw))
            e.dma(lambda g: g.dma_start(out=dtw_s[:, :], in_=dtw))
            e.dma(lambda g: g.dma_start(out=dtb_s[:, :], in_=dtb))
            e.dma(lambda g: g.dma_start(out=alog_s[:, :], in_=alog))
            e.dma(lambda g: g.dma_start(out=dpar_s[:, :], in_=dpar))
            e.dma(lambda g: g.dma_start(out=bmask[:, :], in_=bmaskd))
            e.dma(lambda g: g.dma_start(out=onec_s[:, :], in_=onec))
            e.dma(lambda g: g.dma_start(out=oner_s[:, :], in_=oner))
            e.dma(lambda g: g.dma_start(out=cst_s[:, :], in_=cst))
            e.dflush("const_ld")
            for i in range(NTI):
                e.need(f"bcmask_{i}")
                e.dma(lambda g, i=i: g.dma_start(out=bcscr[i % 2], in_=bcrow[:, :]))
                e.dflush(f"bcw_{i}")
                if i > 0:
                    e.need(f"bfin_{i-1}")
                e.dma(lambda g, i=i: g.dma_start(
                    out=bcb[:, :].rearrange("p (n t) -> p n t", n=32),
                    in_=bcscr[i % 2].partition_broadcast(128)))
                e.dflush(f"bcb_{i}")
                if i >= RT0:
                    for dc in range(NDC):
                        e.need(f"x1a_{i}_{dc}")
                        e.dma(lambda g, dc=dc, i=i: g.dma_start(
                            out=x1[dc, :, (i - RT0) * TM:(i - RT0 + 1) * TM],
                            in_=x0rt[:, dc * TM:(dc + 1) * TM]))
                    e.dflush(f"x1f_{i}")

        def e_tensor(e):
            e.need("const_ld")
            for i in range(NTI):
                e.need(f"x0ld_{i}")
                if i > 0:
                    e.need(f"rowrd_{i-1}")
                for dc in range(NDC):
                    e.run(lambda t, dc=dc: t.matmul(
                        ps_sum[:, :], lhsT=onec_s[:, :], rhs=x0s(dc),
                        start=(dc == 0), stop=(dc == NDC - 1)))
                e.need(f"sq_{i}_{NDC-1}")
                for dc in range(NDC):
                    e.sig(lambda t, dc=dc: t.matmul(
                        ps_ssq[:, :], lhsT=onec_s[:, :],
                        rhs=sq[:, dc * TM:(dc + 1) * TM],
                        start=(dc == 0), stop=(dc == NDC - 1)),
                        f"ssqmm_{i}_{dc}" if dc < NDC - 1 else f"stats_{i}")
                e.need(f"rstd_{i}")
                e.need(f"mrs_{i}")
                if i > 0:
                    e.need(f"lnapply_{i-1}")
                e.sig(lambda t: t.matmul(
                    ps_bc[0][:, :], lhsT=oner_s[:, :], rhs=row("rstd"),
                    start=True, stop=True), f"rstdb_{i}")
                e.sig(lambda t: t.matmul(
                    ps_bc[1][:, :], lhsT=oner_s[:, :], rhs=row("mrs"),
                    start=True, stop=True), f"mrsb_{i}")
                e.need(f"silucv_{i}_{NDC-1}")
                if i > 0:
                    e.need(f"dtin_{i-1}")
                for dc in range(NDC):
                    e.sig(lambda t, dc=dc, i=i: t.matmul(
                        ps_dt[:, :], lhsT=xpw_s[:, dc * 160:dc * 160 + 128],
                        rhs=xcvs(i, dc),
                        start=(dc == 0), stop=(dc == NDC - 1)),
                        f"dtp_{i}" if dc == NDC - 1 else None)
                if i > 0:
                    e.need(f"bcmask_{i-1}")
                for dc in range(NDC):
                    e.sig(lambda t, dc=dc, i=i: t.matmul(
                        ps_misc[0:32, 0:TM], lhsT=xpw_s[:, dc * 160 + 128:dc * 160 + 160],
                        rhs=xcvs(i, dc),
                        start=(dc == 0), stop=(dc == NDC - 1)),
                        f"bcp_{i}" if dc == NDC - 1 else None)
                e.sig(lambda t, i=i: t.matmul(
                    ps_misc[0:16, TM:2 * TM], lhsT=oner_s[0:1, 0:16],
                    rhs=bmask[0:1, i * TM:(i + 1) * TM],
                    start=True, stop=True), f"mbp_{i}")
                e.need(f"dtin_{i}")
                for dc in range(NDC):
                    k = i * NDC + dc
                    pd = k % 2
                    if k >= 2:
                        e.need(f"praw_{((k-2)//NDC)}_{(k-2)%NDC}")
                    e.sig(lambda t, dc=dc, i=i, pd=pd: t.matmul(
                        ps_dl[pd][:, :], lhsT=dtw_s[:, dc * 128:(dc + 1) * 128],
                        rhs=dtin[:, (i % 2) * TM:((i % 2) + 1) * TM],
                        start=True, stop=True), f"dtmm_{i}_{dc}")

        def e_scalar(e):
            e.need("const_ld")
            e.sig(lambda s: s.activation(
                out=aneg[:, :], in_=aneg[:, :], func=AF.Exp), "aexp_pre")
            for i in range(NTI):
                e.need(f"x0ld_{i}")
                if i > 0:
                    e.need(f"lnapply_{i-1}")
                e.sig(lambda s: s.activation(
                    out=sq[:, :], in_=x0t[:, :], func=AF.Square), f"sq_{i}_{NDC-1}")
                e.need(f"var_{i}")
                if i > 0:
                    e.need(f"rstdb_{i-1}")
                e.sig(lambda s: s.activation(
                    out=row("t1"), in_=row("var"), func=AF.Ln, bias=cst_s[0:1, 0:1]))
                e.fence()
                e.sig(lambda s: s.activation(
                    out=row("rstd"), in_=row("t1"), func=AF.Exp, scale=-0.5),
                    f"rstd_{i}")
                for hh in range(2):
                    t0 = hh * HT
                    e.need(f"convh_{i}_{hh}")
                    e.sig(lambda s, i=i, t0=t0: s.activation(
                        out=xcv[:, (i % 2) * NDC * TM:((i % 2) + 1) * NDC * TM]
                            .rearrange("p (c t) -> p c t", c=NDC)
                            [:, :, t0:t0 + HT],
                        in_=acc[:, :].rearrange("p (c t) -> p c t", c=NDC),
                        func=AF.Silu),
                        f"siluh_{i}_{hh}" if hh == 0 else f"silucv_{i}_{NDC-1}")
                for dc in range(NDC):
                    pd2 = dc % 2
                    e.need(f"praw_{i}_{dc}")
                    e.sig(lambda s, pd2=pd2: s.activation(
                        out=spA[:, pd2 * TM:(pd2 + 1) * TM],
                        in_=sp1[:, pd2 * TM:(pd2 + 1) * TM], func=AF.Abs))
                    e.fence()
                    e.sig(lambda s, pd2=pd2: s.activation(
                        out=spA[:, pd2 * TM:(pd2 + 1) * TM],
                        in_=spA[:, pd2 * TM:(pd2 + 1) * TM], func=AF.Exp, scale=-1.0))
                    e.fence()
                    e.sig(lambda s, pd2=pd2: s.activation(
                        out=spA[:, pd2 * TM:(pd2 + 1) * TM],
                        in_=spA[:, pd2 * TM:(pd2 + 1) * TM], func=AF.Ln, bias=1.0),
                        f"spl_{i}_{dc}")
                for dc in range(NDC):
                    k = i * NDC + dc
                    sa = (dc % 2) * NST * TM
                    e.need("aneg_neg")
                    e.need(f"delta_{i}_{dc}")
                    if k >= 2:
                        e.need(f"scn_{((k-2)//NDC)}_{(k-2)%NDC}")
                    for n in range(NST):
                        e.sig(lambda s, dc=dc, i=i, n=n, sa=sa: s.activation(
                            out=abuf[:, sa + n * TM:sa + (n + 1) * TM],
                            in_=dls(i, dc), func=AF.Exp,
                            scale=aneg[:, dc * NST + n:dc * NST + n + 1]),
                            f"aexp_{i}_{dc}" if n == NST - 1 else None)

        def e_vector(e):
            e.sig(lambda v: v.memset(xnh[:, :], 0.0), "xnh0")
            e.sig(lambda v: v.memset(stt[:, :], 0.0), "st0")
            e.need("aexp_pre")
            e.sig(lambda v: v.tensor_scalar(
                out=aneg[:, :], in0=aneg[:, :], scalar1=-1.0, scalar2=None,
                op0=Op.mult), "aneg_neg")
            for i in range(NTI):
                e.need(f"stats_{i}")
                e.fence()
                e.sig(lambda v: v.tensor_scalar(
                    out=row("nmu"), in0=ps_sum[:, :], scalar1=-1.0 / D,
                    scalar2=None, op0=Op.mult))
                e.sig(lambda v: v.tensor_scalar(
                    out=row("m2"), in0=ps_ssq[:, :], scalar1=1.0 / D,
                    scalar2=None, op0=Op.mult), f"rowrd_{i}")
                e.fence()
                e.sig(lambda v: v.tensor_tensor(
                    out=row("t1"), in0=row("nmu"), in1=row("nmu"), op=Op.mult))
                e.fence()
                e.sig(lambda v: v.tensor_tensor(
                    out=row("var"), in0=row("m2"), in1=row("t1"), op=Op.subtract),
                    f"var_{i}")
                e.need(f"rstd_{i}")
                if i > 0:
                    e.need(f"mrsb_{i-1}")
                e.fence()
                e.sig(lambda v: v.tensor_tensor(
                    out=row("mrs"), in0=row("nmu"), in1=row("rstd"), op=Op.mult),
                    f"mrs_{i}")
                e.need(f"rstdb_{i}")
                e.need(f"mrsb_{i}")
                if i == 0:
                    e.need("xnh0")
                if i >= 2:
                    e.need(f"convh_{i-2}_1")
                e.fence()
                e.sig(lambda v: v.tensor_tensor(
                    out=sq[:, :].rearrange("p (c t) -> p c t", c=NDC),
                    in0=x0t[:, :].rearrange("p (c t) -> p c t", c=NDC),
                    in1=ps_bc[0][:, :].rearrange("p t -> p () t").to_broadcast((128, NDC, TM)),
                    op=Op.mult))
                e.fence()
                e.sig(lambda v, i=i: v.tensor_tensor(
                    out=xnh[:, (i % 2) * NDC * HTM + 3:(i % 2) * NDC * HTM + 3 + NDC * HTM - 3]
                        .rearrange("p (c t) -> p c t", c=NDC, t=HTM)[:, :, 0:TM] if False else
                        xnh[:, (i % 2) * NDC * HTM:((i % 2) + 1) * NDC * HTM]
                        .rearrange("p (c t) -> p c t", c=NDC)[:, :, 3:3 + TM],
                    in0=sq[:, :].rearrange("p (c t) -> p c t", c=NDC),
                    in1=ps_bc[1][:, :].rearrange("p t -> p () t").to_broadcast((128, NDC, TM)),
                    op=Op.add), f"lnapply_{i}")
                if i < NTI - 1:
                    e.fence()
                    for dc in range(NDC):
                        e.sig(lambda v, dc=dc, i=i: v.tensor_copy(
                            xnslot(i + 1, dc, 0, 3), xnslot(i, dc, TM, TM + 3)),
                            f"halo_{i}" if dc == NDC - 1 else None)
                if i == 0:
                    e.need("const_ld")
                for hh in range(2):
                    if hh == 1:
                        e.need(f"siluh_{i}_0")
                    elif i >= 1:
                        e.need(f"silucv_{i-1}_{NDC-1}")
                    t0 = hh * HT
                    e.fence()
                    e.sig(lambda v: v.tensor_copy(
                        acc[:, :].rearrange("p (c t) -> p c t", c=NDC),
                        cb_s[:, :].rearrange("p c -> p c ()").to_broadcast(
                            (128, NDC, HT))))
                    for j in range(4):
                        e.fence()
                        e.sig(lambda v, i=i, j=j, t0=t0: v.tensor_tensor(
                            out=sq[:, 0:NDC * HT].rearrange(
                                "p (c t) -> p c t", c=NDC),
                            in0=xnh[:, (i % 2) * NDC * HTM:((i % 2) + 1) * NDC * HTM]
                                .rearrange("p (c t) -> p c t", c=NDC)
                                [:, :, j + t0:j + t0 + HT],
                            in1=cw_s[:, j:j + 4 * (NDC - 1) + 1:4].rearrange(
                                "p c -> p c ()").to_broadcast((128, NDC, HT)),
                            op=Op.mult))
                        e.fence()
                        e.sig(lambda v, j=j: v.tensor_tensor(
                            out=acc[:, :], in0=acc[:, :],
                            in1=sq[:, 0:NDC * HT], op=Op.add),
                            f"convh_{i}_{hh}" if j == 3 else None)
                e.need(f"dtp_{i}")
                e.fence()
                e.sig(lambda v, i=i: v.tensor_copy(
                    dtin[:, (i % 2) * TM:((i % 2) + 1) * TM], ps_dt[:, :]),
                    f"dtin_{i}")
                e.need(f"bcp_{i}")
                e.need(f"mbp_{i}")
                if i > 0:
                    e.need(f"bcw_{i-1}")
                e.fence()
                e.sig(lambda v: v.tensor_copy(bcrow[:, :], ps_misc[0:32, 0:TM]))
                e.fence()
                e.sig(lambda v: v.tensor_tensor(
                    out=bcrow[0:16, :], in0=bcrow[0:16, :],
                    in1=ps_misc[0:16, TM:2 * TM], op=Op.mult), f"bcmask_{i}")
                for dc in range(NDC):
                    k = i * NDC + dc
                    pd = k % 2
                    pd2 = dc % 2
                    e.need(f"dtmm_{i}_{dc}")
                    e.fence()
                    e.sig(lambda v, pd=pd, pd2=pd2, dc=dc: v.tensor_scalar(
                        out=sp1[:, pd2 * TM:(pd2 + 1) * TM], in0=ps_dl[pd][:, :],
                        scalar1=dtb_s[:, dc:dc + 1], scalar2=None, op0=Op.add),
                        f"praw_{i}_{dc}")
                    e.need(f"spl_{i}_{dc}")
                    e.fence()
                    e.sig(lambda v, pd2=pd2, i=i, dc=dc: v.scalar_tensor_tensor(
                        out=dls(i, dc), in0=sp1[:, pd2 * TM:(pd2 + 1) * TM],
                        scalar=0.0, in1=spA[:, pd2 * TM:(pd2 + 1) * TM],
                        op0=Op.max, op1=Op.add),
                        f"delta_{i}_{dc}")
                e.need(f"bcb_{i}")
                for dc in range(NDC):
                    sa = (dc % 2) * NST * TM
                    sh = (dc % 2) * NST * TM
                    e.fence()
                    e.sig(lambda v, i=i, dc=dc: v.tensor_tensor(
                        out=ubuf[:, :], in0=dls(i, dc), in1=xcvs(i, dc), op=Op.mult))
                    e.fence()
                    e.sig(lambda v: v.tensor_tensor(
                        out=bbuf[:, :].rearrange("p (n t) -> p n t", n=NST),
                        in0=ubuf[:, :].rearrange("p t -> p () t").to_broadcast((128, NST, TM)),
                        in1=bcb[:, 0:NST * TM].rearrange("p (n t) -> p n t", n=NST),
                        op=Op.mult),
                        f"bfin_{i}" if (dc == NDC - 1 and i < RT0) else None)
                    e.need(f"aexp_{i}_{dc}")
                    e.fence()
                    for n in range(NST):
                        e.sig(lambda v, n=n, sa=sa, sh=sh, dc=dc: v.tensor_tensor_scan(
                            out=hdc[:, sh + n * TM:sh + (n + 1) * TM],
                            data0=abuf[:, sa + n * TM:sa + (n + 1) * TM],
                            data1=bbuf[:, n * TM:(n + 1) * TM],
                            initial=stt[:, dc * NST + n:dc * NST + n + 1],
                            op0=Op.mult, op1=Op.add))
                    e.fence()
                    e.sig(lambda v, sh=sh, dc=dc: v.tensor_copy(
                        stt[:, dc * NST:(dc + 1) * NST],
                        hdc[:, sh:sh + NST * TM].rearrange(
                            "p (n t) -> p n t", n=NST)[:, :, TM - 1:TM].squeeze(2)),
                        f"scn_{i}_{dc}")
                    if i >= RT0:
                        e.fence()
                        e.sig(lambda v, sh=sh: v.tensor_tensor(
                            out=ytmp[:, :], in0=hdc[:, sh:sh + NST * TM],
                            in1=bcb[:, NST * TM:2 * NST * TM], op=Op.mult),
                            f"bfin_{i}" if dc == NDC - 1 else None)
                        e.fence()
                        e.sig(lambda v: v.tensor_reduce(
                            out=yred[:, :],
                            in_=ytmp[:, :].rearrange("p (n t) -> p t n", n=NST),
                            axis=mybir.AxisListType.X, op=Op.add))
                        e.fence()
                        e.sig(lambda v, i=i, dc=dc: v.scalar_tensor_tensor(
                            out=yred[:, :], in0=xnslot(i, dc, 3, 3 + TM),
                            scalar=dpar_s[:, dc:dc + 1], in1=yred[:, :],
                            op0=Op.mult, op1=Op.add))
                        if dc == 0:
                            e.need(f"x0rld_{i}")
                        e.fence()
                        e.sig(lambda v, dc=dc: v.tensor_tensor(
                            out=x0rt[:, dc * TM:(dc + 1) * TM], in0=yred[:, :],
                            in1=x0rt[:, dc * TM:(dc + 1) * TM], op=Op.add),
                            f"x1a_{i}_{dc}")

        ems = {"sync": e_sync, "tensor": e_tensor, "vector": e_vector,
               "scalar": e_scalar, "gpsimd": e_gpsimd}
        if fused is not None:
            return ems
        build_program(nc, ems)
    return nc


def mamba_inputs(inp):
    cw = inp["L0_attn_conv_w"][:, 0, :]
    xp = inp["L0_attn_xproj_w"]
    dtwm = inp["L0_attn_dtproj_w"]
    xpw = np.ascontiguousarray(xp.reshape(160, NDC, 128).transpose(1, 2, 0)).reshape(NDC, 128, 160)
    xpw = np.ascontiguousarray(xpw.transpose(1, 0, 2).reshape(128, NDC * 160))
    dtw = np.ascontiguousarray(dtwm.reshape(NDC, 128, 128).transpose(2, 0, 1).reshape(128, NDC * 128))
    return {
        "cw": np.ascontiguousarray(cw.reshape(NDC, 128, 4).transpose(1, 0, 2).reshape(128, NDC * 4)),
        "cb": np.ascontiguousarray(inp["L0_attn_conv_b"].reshape(NDC, 128).T),
        "xpw": xpw,
        "dtw": dtw,
        "dtb": np.ascontiguousarray(inp["L0_attn_dtproj_b"].reshape(NDC, 128).T),
        "alog": np.ascontiguousarray(
            inp["L0_attn_A_log"].reshape(NDC, 128, NST).transpose(1, 0, 2).reshape(128, NDC * NST)),
        "dpar": np.ascontiguousarray(inp["L0_attn_D_param"].reshape(NDC, 128).T),
        "onec": np.ones((128, 1), np.float32),
        "oner": np.ones((1, 128), np.float32),
        "cst": np.tile(np.array([[1e-5, 1e-6, 0.0, 0.0]], np.float32), (128, 1)),
    }


def mamba_core_inputs(x_full, b, h):
    if h == 0:
        ctx = np.concatenate([np.zeros((TLOC, D), np.float32), x_full[b, :TLOC]], 0)
        mask = np.concatenate([np.zeros(TLOC, np.float32), np.ones(TLOC, np.float32)])
    else:
        ctx = np.asarray(x_full[b], np.float32)
        mask = np.ones(TCTX, np.float32)
    return {"x0": fm(ctx), "bmask": mask.reshape(1, TCTX)}


def run_mamba(x_full, inp):
    nc = build_mamba()
    shared = mamba_inputs(inp)
    in_maps = []
    for c in range(8):
        b, h = divmod(c, 2)
        m = dict(shared)
        m.update(mamba_core_inputs(x_full, b, h))
        in_maps.append(m)
    res = run_bass_kernel_spmd(nc, in_maps, core_ids=list(range(8)))
    out = np.zeros((4, 4096, D), np.float32)
    for c in range(8):
        b, h = divmod(c, 2)
        out[b, h * TLOC:(h + 1) * TLOC] = res.results[c]["x1"].reshape(D, TLOC).T
    return out



NBLK = 16
BLK = 256
NHD = 8
CT = 512
NCT = TCTX // CT


def build_lightning(fused=None):
    nc = bass.Bass("TRN2", target_bir_lowering=False) if fused is None else fused.nc
    if fused is None:
        x2f = nc.dram_tensor("x2f", [NDC, 128, TCTX], F32, kind="ExternalInput").ap()
    else:
        x2f = None
    wq = nc.dram_tensor("wq", [4 * NHD, 128, NDC * 128], F32, kind="ExternalInput").ap()
    slopes = nc.dram_tensor("slopes", [1, NHD], F32, kind="ExternalInput").ap()
    onec = _shared_const(nc, fused, "onec", [128, 1])
    oner = _shared_const(nc, fused, "oner", [1, 128])
    cst = _shared_const(nc, fused, "cst", [128, 4])
    if fused is None:
        oo = nc.dram_tensor("oo", [NHD, 128, TCTX], F32, kind="ExternalOutput").ap()
        uo = nc.dram_tensor("uo", [NHD, 128, TCTX], F32, kind="ExternalOutput").ap()
        oub = None
    else:
        oo = uo = None
        oub = nc.dram_tensor("oub", [2 * 2048, TLOC], BF16).ap()
        fused.oub = oub
    qs = nc.dram_tensor("qs", [NHD, 128, TCTX], BF16).ap()
    xns = nc.dram_tensor("xns", [NDC, 128, TCTX], BF16).ap()
    ks = nc.dram_tensor("ks", [NHD, 128, TCTX], BF16).ap()
    vs = nc.dram_tensor("vs", [NHD, 128, TCTX], BF16).ap()

    with ExitStack() as st:
        def sb(name, shape, dt):
            return st.enter_context(nc.sbuf_tensor(name, shape, dt,
                                                   allow_name_mangling=True))

        def psb(name, shape, dt=F32):
            return st.enter_context(nc.psum_tensor(name, shape, dt,
                                                   allow_name_mangling=True))

        sq = sb("sq", [128, 2 * CT], F32)
        rows = sb("rows", [1, 6 * CT], F32)
        _ln_stack = ExitStack()
        x2t = _ln_stack.enter_context(nc.sbuf_tensor(
            "x2t", [128, NDC * CT], F32 if fused is None else BF16, allow_name_mangling=True))
        xn2 = _ln_stack.enter_context(nc.sbuf_tensor("xn2", [128, 2 * NDC * CT], BF16, allow_name_mangling=True))
        _ln_stack.close()
        wqs = sb("wqs", [128, 8 * NDC * 128], BF16)
        xnt = sb("xnt", [128, 2 * NDC * CT], BF16)
        evb = sb("evb", [128, 4 * CT], BF16)
        evf = sb("evf", [128, 4 * CT], F32)
        onec_s = sb("onec_s", [128, 1], F32)
        onec_b = sb("onec_b", [128, 1], BF16)
        oner_s = sb("oner_s", [1, 128], F32)
        cst_s = sb("cst_s", [128, 4], F32)
        slp = sb("slp", [1, NHD], F32)
        slpc = sb("slpc", [128, NHD], F32)
        nslpc = sb("nslpc", [128, NHD], F32)
        idxt = sb("idxt", [128, 2 * BLK], I32)
        idxc = sb("idxc", [128, 2 * BLK], F32)
        mk01 = sb("mk01", [128, 2 * BLK], F32)
        ddsc = sb("ddsc", [128, 2 * BLK], F32)
        ddt = sb("ddt", [128, NHD * 2 * BLK], BF16)
        qdb = sb("qdb", [128, NHD * BLK], BF16)
        qdi = sb("qdi", [128, BLK], I32)
        qdf = sb("qdf", [128, BLK], F32)
        kdi = sb("kdi", [128, 2], I32)
        kdc = sb("kdc", [128, 2 * NHD], F32)
        bdc = sb("bdc", [128, NHD], F32)
        qkvb = sb("qkvb", [128, 2 * NHD * 3 * BLK], BF16)
        kvsb = sb("kvsb", [128, NHD * 128], F32)
        kvbf = sb("kvbf", [128, 128], BF16)
        qpb = sb("qpb", [128, BLK], BF16)
        ktm = sb("ktm", [128, 2 * 128], BF16)
        vtm = sb("vtm", [128, 2 * 128], BF16)
        qkd = sb("qkd", [128, 2 * BLK], BF16)
        oev = sb("oev", [128, 2 * NHD * BLK], F32)

        ps_sum = psb("ps_sum", [1, CT])
        ps_ssq = psb("ps_ssq", [1, CT])
        ps_bc = [psb(f"ps_bc{i}", [128, CT]) for i in range(2)]
        ps_qk = [psb(f"ps_qk{i}", [128, CT]) for i in range(2)]
        ps_tpc = psb("ps_tpc", [128, 512], BF16)
        ps_tp = ps_tpc[:, 0:256]
        ps_tp2 = ps_tpc[:, 256:512]
        ps_kv = None
        ps_o = None

        def row(name):
            base = {"nmu": 0, "m2": 1, "t1": 2, "var": 3, "rstd": 4, "mrs": 5}[name]
            return rows[0:1, base * CT:(base + 1) * CT]

        def x2s(dc):
            return x2t[:, dc * CT:(dc + 1) * CT]

        def x2src(dc, tau):
            if fused is None:
                return x2f[dc, :, tau * CT:(tau + 1) * CT]
            slot, tp = divmod(tau * CT, TLOC)
            base = slot * 2048 + dc * 128
            return fused.xg[base:base + 128, tp:tp + CT]

        def osink(i, hd):
            if fused is None:
                return oo[hd, :, i * BLK:(i + 1) * BLK]
            r = (i // 8) * 2048 + hd * 128
            c = (i % 8) * BLK
            return oub[r:r + 128, c:c + BLK]

        def usink(hd, tau):
            if fused is None:
                return uo[hd, :, tau * CT:(tau + 1) * CT]
            r = (tau // 4) * 2048 + 1024 + hd * 128
            c = (tau % 4) * CT
            return oub[r:r + 128, c:c + CT]

        def xn2s(tau, dc):
            s = (tau % 2) * NDC * CT
            return xn2[:, s + dc * CT:s + dc * CT + CT]

        def qkvslot(i, hd, which):
            s = (i % 2) * NHD * 3 * BLK
            return qkvb[:, s + (hd * 3 + which) * BLK: s + (hd * 3 + which) * BLK + BLK]

        def e_sync(e):
            for tau in range(NCT):
                if tau > 0:
                    e.need(f"lnapply_{tau-1}")
                    e.need(f"stats_{tau-1}")
                    e.need(f"sq_{tau-1}_{NDC-1}")
                for dc in range(NDC):
                    e.dma(lambda g, dc=dc, tau=tau: g.dma_start(
                        out=x2s(dc), in_=x2src(dc, tau)))
                e.dflush(f"x2ld_{tau}")
            for fcg in range(4):
                for tau in range(NCT):
                    kk = fcg * NCT + tau
                    e.need(f"stats_{NCT-1}")
                    e.need(f"sq_{NCT-1}_{NDC-1}")
                    e.need(f"lnapply_{NCT-1}")
                    e.need(f"xnst_{NCT-1}")
                    if kk >= 2:
                        pkk = kk - 2
                        e.need(f"pmm_{(pkk//NCT)*8+7}_{pkk%NCT}")
                    for dc in range(NDC):
                        e.dma(lambda g, dc=dc, tau=tau, kk=kk: g.dma_start(
                            out=xnt[:, ((kk % 2) * NDC + dc) * CT:((kk % 2) * NDC + dc + 1) * CT],
                            in_=xns[dc, :, tau * CT:(tau + 1) * CT]))
                    e.dflush(f"xnld_{fcg}_{tau}")
            e.need("p1done")
            for i in range(NBLK):
                if i >= 2:
                    e.need(f"oev_{i-2}_{NHD-1}")
                for hd in range(NHD):
                    e.dma(lambda g, i=i, hd=hd: g.dma_start(
                        out=qkvslot(i, hd, 0), in_=qs[hd, :, i * BLK:(i + 1) * BLK]))
                    e.dma(lambda g, i=i, hd=hd: g.dma_start(
                        out=qkvslot(i, hd, 1), in_=ks[hd, :, i * BLK:(i + 1) * BLK]))
                    e.dma(lambda g, i=i, hd=hd: g.dma_start(
                        out=qkvslot(i, hd, 2), in_=vs[hd, :, i * BLK:(i + 1) * BLK]))
                e.dflush(f"qkvld_{i}")

        def e_gpsimd(e):
            e.dma(lambda g: g.dma_start(out=onec_s[:, :], in_=onec))
            e.dma(lambda g: g.dma_start(out=oner_s[:, :], in_=oner))
            e.dma(lambda g: g.dma_start(out=cst_s[:, :], in_=cst))
            e.dma(lambda g: g.dma_start(out=slp[:, :], in_=slopes))
            e.dflush("const_ld")
            for tau in range(NCT):
                e.need(f"lnapply_{tau}")
                for dc in range(NDC):
                    e.dma(lambda g, dc=dc, tau=tau: g.dma_start(
                        out=xns[dc, :, tau * CT:(tau + 1) * CT],
                        in_=xn2s(tau, dc)))
                e.dflush(f"xnst_{tau}")
            for fcg in range(4):
                e.need(f"stats_{NCT-1}")
                e.need(f"sq_{NCT-1}_{NDC-1}")
                e.need(f"lnapply_{NCT-1}")
                e.need(f"xnst_{NCT-1}")
                if fcg > 0:
                    e.need(f"pmm_{(fcg-1)*8+7}_{NCT-1}")
                for fc8 in range(8):
                    e.dma(lambda g, fcg=fcg, fc8=fc8: g.dma_start(
                        out=wqs[:, fc8 * NDC * 128:(fc8 + 1) * NDC * 128],
                        in_=wq[fcg * 8 + fc8]))
                e.dflush(f"wqg_{fcg}")
                for tau in range(NCT):
                    kk = fcg * NCT + tau
                    for fc8 in range(8):
                        fc = fcg * 8 + fc8
                        hd, quarter = divmod(fc, 4)
                        k = kk * 8 + fc8
                        sl = k % 4
                        e.need(f"pev_{fc}_{tau}")
                        tgt = [qs, ks, vs, None][quarter]
                        if quarter < 3:
                            e.dma(lambda g, hd=hd, tau=tau, sl=sl, tgt=tgt: g.dma_start(
                                out=tgt[hd, :, tau * CT:(tau + 1) * CT],
                                in_=evb[:, sl * CT:(sl + 1) * CT]))
                        else:
                            e.dma(lambda g, hd=hd, tau=tau, sl=sl: g.dma_start(
                                out=usink(hd, tau),
                                in_=evf[:, sl * CT:(sl + 1) * CT]))
                        if k % 2 == 1:
                            e.dflush(f"pst_{fc}_{tau}")
                        else:
                            e.dflush(f"pst2_{fc}_{tau}")
            e.dflush("p1done")
            for i in range(NBLK):
                for hd in range(NHD):
                    e.need(f"oev_{i}_{hd}")
                    e.dma(lambda g, i=i, hd=hd: g.dma_start(
                        out=osink(i, hd),
                        in_=oev[:, ((i % 2) * NHD + hd) * BLK:((i % 2) * NHD + hd + 1) * BLK]))
                e.dflush(f"ost_{i}")

        def e_tensor(e):
            e.need("const_ld")
            e.sig(lambda t: t.matmul(
                ps_bc[0][:, 0:NHD], lhsT=oner_s[:, :], rhs=slp[0:1, :],
                start=True, stop=True), "slpb")
            for tau in range(NCT):
                e.need(f"x2ld_{tau}")
                if tau > 0:
                    e.need(f"rowrd_{tau-1}")
                lhs_x = onec_s if fused is None else onec_b
                if fused is not None:
                    e.need("onecb")
                for dc in range(NDC):
                    e.run(lambda t, dc=dc, lhs_x=lhs_x: t.matmul(
                        ps_sum[:, :], lhsT=lhs_x[:, :], rhs=x2s(dc),
                        start=(dc == 0), stop=(dc == NDC - 1)))
                for dc in range(NDC):
                    e.need(f"sq_{tau}_{dc}")
                    e.sig(lambda t, dc=dc: t.matmul(
                        ps_ssq[:, :], lhsT=onec_s[:, :],
                        rhs=sq[:, (dc % 2) * CT:((dc % 2) + 1) * CT],
                        start=(dc == 0), stop=(dc == NDC - 1)),
                        f"ssqmm_{tau}_{dc}" if dc < NDC - 1 else f"stats_{tau}")
                e.need(f"rstd_{tau}")
                e.need(f"mrs_{tau}")
                if tau > 0:
                    e.need(f"lnapply_{tau-1}")
                if tau == 0:
                    e.need("ddtpre")
                e.sig(lambda t: t.matmul(
                    ps_bc[0][:, :], lhsT=oner_s[:, :], rhs=row("rstd"),
                    start=True, stop=True), f"rstdb_{tau}")
                e.sig(lambda t: t.matmul(
                    ps_bc[1][:, :], lhsT=oner_s[:, :], rhs=row("mrs"),
                    start=True, stop=True), f"mrsb_{tau}")
            for fcg in range(4):
                e.need(f"wqg_{fcg}")
                for tau in range(NCT):
                    kk = fcg * NCT + tau
                    e.need(f"xnld_{fcg}_{tau}")
                    for fc8 in range(8):
                        fc = fcg * 8 + fc8
                        k = kk * 8 + fc8
                        pg = k % 2
                        if k >= 2:
                            pk = k - 2
                            pfc = (pk // NCT // 8) * 8 + pk % 8 if False else None
                            e.need(f"pev_{(pk//8//NCT)*8+pk%8}_{(pk//8)%NCT}")
                        for dc in range(NDC):
                            e.sig(lambda t, dc=dc, kk=kk, pg=pg, fc8=fc8: t.matmul(
                                ps_qk[pg][:, :],
                                lhsT=wqs[:, (fc8 * NDC + dc) * 128:(fc8 * NDC + dc + 1) * 128],
                                rhs=xnt[:, ((kk % 2) * NDC + dc) * CT:((kk % 2) * NDC + dc + 1) * CT],
                                start=(dc == 0), stop=(dc == NDC - 1)),
                                f"pmm_{fc}_{tau}" if dc == NDC - 1 else None)
            for i in range(NBLK):
                e.need(f"qkvld_{i}")
                for hd in range(NHD):
                    if i * NHD + hd >= 1:
                        pk = i * NHD + hd - 1
                        e.need(f"ktmev_{pk//NHD}_{pk%NHD}")
                        e.need(f"vtmev_{pk//NHD}_{pk%NHD}")
                    for half in range(2):
                        e.sig(lambda t, i=i, hd=hd, half=half: t.transpose(
                            ps_tp[:, half * 128:(half + 1) * 128],
                            in_=qkvslot(i, hd, 1)[:, half * 128:(half + 1) * 128],
                            identity=kvbf[:, :]) if False else t.transpose(
                            ps_tp[:, half * 128:(half + 1) * 128],
                            in_=qkvslot(i, hd, 1)[:, half * 128:(half + 1) * 128],
                            identity=identb[:, :]),
                            f"ktp_{i}_{hd}" if half == 1 else None)
                    for half in range(2):
                        e.sig(lambda t, i=i, hd=hd, half=half: t.transpose(
                            ps_tp2[:, half * 128:(half + 1) * 128],
                            in_=qkvslot(i, hd, 2)[:, half * 128:(half + 1) * 128],
                            identity=identb[:, :]),
                            f"vtp_{i}_{hd}" if half == 1 else None)
                    if i * NHD + hd >= 2:
                        pk = i * NHD + hd - 2
                        e.need(f"qkdm_{pk//NHD}_{pk%NHD}")
                    for half in range(2):
                        e.sig(lambda t, i=i, hd=hd, half=half: t.matmul(
                            ps_qk[half][:, 0:BLK],
                            lhsT=qkvslot(i, hd, 1)[:, half * 128:(half + 1) * 128],
                            rhs=qkvslot(i, hd, 0),
                            start=True, stop=True),
                            f"qkmm_{i}_{hd}" if half == 1 else None)
                    e.need(f"ktmev_{i}_{hd}")
                    e.need(f"vtmev_{i}_{hd}")
                    if i * NHD + hd >= 1:
                        pk = i * NHD + hd - 1
                        e.need(f"kvup_{pk//NHD}_{pk%NHD}")
                    for half in range(2):
                        e.sig(lambda t, half=half: t.matmul(
                            ps_bc[1][:, 0:128], lhsT=ktm[:, half * 128:(half + 1) * 128],
                            rhs=vtm[:, half * 128:(half + 1) * 128],
                            start=(half == 0), stop=(half == 1)),
                            f"kvmm_{i}_{hd}" if half == 1 else None)
                    e.need(f"kvbf_{i}_{hd}")
                    e.need(f"qpb_{i}_{hd}")
                    if i * NHD + hd >= 1:
                        pk = i * NHD + hd - 1
                        e.need(f"oev_{pk//NHD}_{pk%NHD}")
                    e.sig(lambda t: t.matmul(
                        ps_bc[0][:, 0:BLK], lhsT=kvbf[:, :], rhs=qpb[:, :],
                        start=True, stop=False))
                    e.need(f"qkdm_{i}_{hd}")
                    for half in range(2):
                        e.sig(lambda t, half=half: t.matmul(
                            ps_bc[0][:, 0:BLK], lhsT=vtm[:, half * 128:(half + 1) * 128],
                            rhs=qkd[:, half * BLK:(half + 1) * BLK],
                            start=False, stop=(half == 1)),
                            f"odone_{i}_{hd}" if half == 1 else None)

        def e_scalar(e):
            e.need("const_ld")
            e.need("idx_pre")
            e.need("slpneg")
            for hd in range(NHD):
                if hd >= 1:
                    e.need(f"ddtw_{hd-1}_1")
                for half in range(2):
                    e.sig(lambda s, hd=hd, half=half: s.activation(
                        out=ddsc[:, half * BLK:(half + 1) * BLK],
                        in_=idxc[:, half * BLK:(half + 1) * BLK], func=AF.Exp,
                        scale=nslpc[:, hd:hd + 1]),
                        f"ddexp_{hd}_{half}")
            e.need("qdi_pre")
            for hd in range(NHD):
                e.sig(lambda s, hd=hd: s.activation(
                    out=qdf[:, :], in_=qdi[:, :], func=AF.Exp,
                    scale=nslpc[:, hd:hd + 1]), f"qdexp_{hd}")
                e.fence()
                e.need(f"qdbw_{hd}")
            e.need("kdi_pre")
            for hd in range(NHD):
                e.sig(lambda s, hd=hd: s.activation(
                    out=kdc[:, hd * 2:hd * 2 + 2], in_=kdi[:, :], func=AF.Exp,
                    scale=nslpc[:, hd:hd + 1]), f"kdexp_{hd}")
            e.sig(lambda s: s.activation(
                out=bdc[:, :], in_=slpc[:, :], func=AF.Exp, scale=-256.0), "bdexp")
            for tau in range(NCT):
                e.need(f"x2ld_{tau}")
                for dc in range(NDC):
                    if dc >= 2:
                        e.need(f"ssqmm_{tau}_{dc-2}")
                    elif tau > 0:
                        e.need(f"stats_{tau-1}")
                    e.sig(lambda s, dc=dc: s.activation(
                        out=sq[:, (dc % 2) * CT:((dc % 2) + 1) * CT],
                        in_=x2s(dc), func=AF.Square), f"sq_{tau}_{dc}")
                e.need(f"var_{tau}")
                if tau > 0:
                    e.need(f"rstdb_{tau-1}")
                e.sig(lambda s: s.activation(
                    out=row("t1"), in_=row("var"), func=AF.Ln, bias=cst_s[0:1, 1:2]))
                e.fence()
                e.sig(lambda s: s.activation(
                    out=row("rstd"), in_=row("t1"), func=AF.Exp, scale=-0.5),
                    f"rstd_{tau}")
            for fcg in range(4):
                for tau in range(NCT):
                    kk = fcg * NCT + tau
                    for fc8 in range(8):
                        fc = fcg * 8 + fc8
                        quarter = fc % 4
                        k = kk * 8 + fc8
                        pg = k % 2
                        sl = k % 4
                        e.need(f"pmm_{fc}_{tau}")
                        if k >= 4:
                            pk = k - 4
                            pfc = (pk // 8 // NCT) * 8 + pk % 8
                            ptau = (pk // 8) % NCT
                            if (pk % 2) == 1:
                                e.need(f"pst_{pfc}_{ptau}")
                            else:
                                e.need(f"pst2_{pfc}_{ptau}")
                        if quarter in (0, 1):
                            e.sig(lambda s, pg=pg, sl=sl: s.activation(
                                out=evb[:, sl * CT:(sl + 1) * CT], in_=ps_qk[pg][:, :],
                                func=AF.Silu), f"pev_{fc}_{tau}")
                        elif quarter == 2:
                            e.sig(lambda s, pg=pg, sl=sl: s.activation(
                                out=evb[:, sl * CT:(sl + 1) * CT], in_=ps_qk[pg][:, :],
                                func=AF.Identity), f"pev_{fc}_{tau}")
                        else:
                            e.sig(lambda s, pg=pg, sl=sl: s.activation(
                                out=evf[:, sl * CT:(sl + 1) * CT], in_=ps_qk[pg][:, :],
                                func=AF.Identity), f"pev_{fc}_{tau}")

        def e_vector(e):
            e.need("const_ld")
            e.sig(lambda v: v.tensor_copy(onec_b[:, :], onec_s[:, :]), "onecb")
            e.need("identload")
            e.need("slpb")
            e.fence()
            e.sig(lambda v: v.tensor_copy(slpc[:, :], ps_bc[0][:, 0:NHD]), "slpcp")
            e.fence()
            e.sig(lambda v: v.tensor_scalar(
                out=nslpc[:, :], in0=slpc[:, :], scalar1=-1.0, scalar2=None,
                op0=Op.mult), "slpneg")
            e.sig(lambda v: v.iota(idxt[:, 0:BLK], pattern=[[1, BLK]], base=0,
                                   channel_multiplier=-1) if False else v.engine_nop(), "iota_nop")
            e.need("iota_done")
            e.fence()
            e.sig(lambda v: v.tensor_scalar(
                out=idxc[:, :], in0=idxt[:, :], scalar1=0.0, scalar2=None,
                op0=Op.max), "idxclamp")
            e.sig(lambda v: v.tensor_scalar(
                out=mk01[:, :], in0=idxt[:, :], scalar1=0.0, scalar2=None,
                op0=Op.is_ge), "mk01_done")
            e.sig(lambda v: v.engine_nop(), "idx_pre")
            for hd in range(NHD):
                for half in range(2):
                    e.need(f"ddexp_{hd}_{half}")
                    e.fence()
                    e.sig(lambda v, hd=hd, half=half: v.tensor_tensor(
                        out=ddt[:, (hd * 2 + half) * BLK:(hd * 2 + half + 1) * BLK],
                        in0=ddsc[:, half * BLK:(half + 1) * BLK],
                        in1=mk01[:, half * BLK:(half + 1) * BLK], op=Op.mult),
                        f"ddtw_{hd}_{half}" if half == 1 else None)
            e.sig(lambda v: v.engine_nop(), "ddtpre")
            for hd in range(NHD):
                e.need(f"qdexp_{hd}")
                e.fence()
                e.sig(lambda v, hd=hd: v.tensor_copy(
                    qdb[:, hd * BLK:(hd + 1) * BLK], qdf[:, :]), f"qdbw_{hd}")
            e.sig(lambda v: v.memset(kvsb[:, :], 0.0), "kv0")
            for tau in range(NCT):
                e.need(f"stats_{tau}")
                e.fence()
                e.sig(lambda v: v.tensor_scalar(
                    out=row("nmu"), in0=ps_sum[:, :], scalar1=-1.0 / D,
                    scalar2=None, op0=Op.mult))
                e.sig(lambda v: v.tensor_scalar(
                    out=row("m2"), in0=ps_ssq[:, :], scalar1=1.0 / D,
                    scalar2=None, op0=Op.mult), f"rowrd_{tau}")
                e.fence()
                e.sig(lambda v: v.tensor_tensor(
                    out=row("t1"), in0=row("nmu"), in1=row("nmu"), op=Op.mult))
                e.fence()
                e.sig(lambda v: v.tensor_tensor(
                    out=row("var"), in0=row("m2"), in1=row("t1"), op=Op.subtract),
                    f"var_{tau}")
                e.need(f"rstd_{tau}")
                if tau > 0:
                    e.need(f"mrsb_{tau-1}")
                e.fence()
                e.sig(lambda v: v.tensor_tensor(
                    out=row("mrs"), in0=row("nmu"), in1=row("rstd"), op=Op.mult),
                    f"mrs_{tau}")
                e.need(f"rstdb_{tau}")
                e.need(f"mrsb_{tau}")
                if tau >= 2:
                    e.need(f"xnst_{tau-2}")
                for dc in range(NDC):
                    e.fence()
                    e.sig(lambda v, dc=dc: v.tensor_tensor(
                        out=sq[:, 0:CT], in0=x2s(dc), in1=ps_bc[0][:, :], op=Op.mult))
                    e.fence()
                    e.sig(lambda v, dc=dc, tau=tau: v.tensor_tensor(
                        out=xn2s(tau, dc), in0=sq[:, 0:CT], in1=ps_bc[1][:, :],
                        op=Op.add), f"lnapply_{tau}" if dc == NDC - 1 else None)
            for i in range(NBLK):
                for hd in range(NHD):
                    e.need(f"qkvld_{i}")
                    e.fence()
                    e.sig(lambda v, i=i, hd=hd: v.tensor_tensor(
                        out=qpb[:, :], in0=qkvslot(i, hd, 0),
                        in1=qdb[:, hd * BLK:(hd + 1) * BLK], op=Op.mult),
                        f"qpb_{i}_{hd}")
                    e.need(f"ktp_{i}_{hd}")
                    e.fence()
                    for half in range(2):
                        e.sig(lambda v, hd=hd, half=half: v.tensor_scalar(
                            out=ktm[:, half * 128:(half + 1) * 128],
                            in0=ps_tp[:, half * 128:(half + 1) * 128],
                            scalar1=kdc[:, hd * 2 + half:hd * 2 + half + 1],
                            scalar2=None, op0=Op.mult),
                            f"ktmev_{i}_{hd}" if half == 1 else None)
                    e.need(f"vtp_{i}_{hd}")
                    e.fence()
                    for half in range(2):
                        e.sig(lambda v, half=half: v.tensor_copy(
                            vtm[:, half * 128:(half + 1) * 128],
                            ps_tp2[:, half * 128:(half + 1) * 128]),
                            f"vtmev_{i}_{hd}" if half == 1 else None)
                    e.need(f"qkmm_{i}_{hd}")
                    e.fence()
                    for half in range(2):
                        e.sig(lambda v, hd=hd, half=half: v.tensor_tensor(
                            out=qkd[:, half * BLK:(half + 1) * BLK],
                            in0=ps_qk[half][:, 0:BLK],
                            in1=ddt[:, (hd * 2 + half) * BLK:(hd * 2 + half + 1) * BLK],
                            op=Op.mult), f"qkdm_{i}_{hd}" if half == 1 else None)
                    if i == 0 and hd == 0:
                        e.need("kv0")
                    e.fence()
                    e.sig(lambda v, hd=hd: v.tensor_copy(
                        kvbf[:, :], kvsb[:, hd * 128:(hd + 1) * 128]), f"kvbf_{i}_{hd}")
                    e.need(f"kvmm_{i}_{hd}")
                    e.fence()
                    e.sig(lambda v, hd=hd: v.scalar_tensor_tensor(
                        out=kvsb[:, hd * 128:(hd + 1) * 128],
                        in0=kvsb[:, hd * 128:(hd + 1) * 128],
                        scalar=bdc[:, hd:hd + 1], in1=ps_bc[1][:, 0:128],
                        op0=Op.mult, op1=Op.add), f"kvup_{i}_{hd}")
                    e.need(f"odone_{i}_{hd}")
                    if i >= 2:
                        e.need(f"ost_{i-2}")
                    e.fence()
                    e.sig(lambda v, i=i, hd=hd: v.tensor_copy(
                        oev[:, ((i % 2) * NHD + hd) * BLK:((i % 2) * NHD + hd + 1) * BLK],
                        ps_bc[0][:, 0:BLK]), f"oev_{i}_{hd}")

        identb = sb("identb", [128, 128], BF16)
        identd = nc.dram_tensor("identd", [128, 128], F32, kind="ExternalInput").ap()

        def e_gpsimd2(e):
            e.dma(lambda g: g.dma_start(out=identb[:, :], in_=identd))
            e.dflush("identload")
            e.sig(lambda g: g.iota(idxt[:, 0:BLK], pattern=[[1, BLK]], base=0,
                                   channel_multiplier=-1))
            e.sig(lambda g: g.iota(idxt[:, BLK:2 * BLK], pattern=[[1, BLK]], base=-128,
                                   channel_multiplier=-1))
            e.sig(lambda g: g.iota(qdi[:, :], pattern=[[1, BLK]], base=1,
                                   channel_multiplier=0))
            e.sig(lambda g: g.engine_nop(), "qdi_pre")
            e.sig(lambda g: g.iota(kdi[:, 0:1], pattern=[[0, 1]], base=255,
                                   channel_multiplier=-1))
            e.sig(lambda g: g.iota(kdi[:, 1:2], pattern=[[0, 1]], base=127,
                                   channel_multiplier=-1))
            e.sig(lambda g: g.engine_nop(), "kdi_pre")
            e.sig(lambda g: g.engine_nop(), "iota_done")
            e_gpsimd(e)

        ems = {"sync": e_sync, "tensor": e_tensor, "vector": e_vector,
               "scalar": e_scalar, "gpsimd": e_gpsimd2}
        if fused is not None:
            return ems
        build_program(nc, ems)
    return nc


def lightning_inputs(inp, hh):
    W = inp["L1_attn_W_qkvu"]
    rowsel = []
    for hd in range(NHD):
        g = hh * NHD + hd
        for quarter in range(4):
            base = quarter * 2048 + g * 128
            rowsel.append(W[base:base + 128])
    Wsel = np.stack(rowsel, 0)
    strips = np.ascontiguousarray(
        Wsel.reshape(4 * NHD, 128, NDC, 128).transpose(0, 3, 2, 1).reshape(4 * NHD, 128, NDC * 128))
    slopes = inp["L1_attn_slope_rate"].reshape(16)[hh * NHD:(hh + 1) * NHD]
    return {
        "wq": strips,
        "slopes": slopes.reshape(1, NHD).astype(np.float32),
        "onec": np.ones((128, 1), np.float32),
        "oner": np.ones((1, 128), np.float32),
        "cst": np.tile(np.array([[1e-5, 1e-6, 0.0, 0.0]], np.float32), (128, 1)),
        "identd": np.eye(128, dtype=np.float32),
    }


def run_lightning(x2_full, inp):
    nc = build_lightning()
    shared = [lightning_inputs(inp, hh) for hh in range(2)]
    in_maps = []
    for c in range(8):
        b, hh = divmod(c, 2)
        m = dict(shared[hh])
        m["x2f"] = fm(x2_full[b])
        in_maps.append(m)
    res = run_bass_kernel_spmd(nc, in_maps, core_ids=list(range(8)))
    o = np.zeros((4, 4096, D), np.float32)
    u = np.zeros((4, 4096, D), np.float32)
    for c in range(8):
        b, hh = divmod(c, 2)
        ofm = res.results[c]["oo"].reshape(NHD * 128, TCTX)
        ufm = res.results[c]["uo"].reshape(NHD * 128, TCTX)
        o[b, :, hh * 1024:(hh + 1) * 1024] = ofm.T
        u[b, :, hh * 1024:(hh + 1) * 1024] = ufm.T
    return o, u



NEX = 10
EGHC = 11


def build_tail(fused=None):
    nc = bass.Bass("TRN2", target_bir_lowering=False) if fused is None else fused.nc
    if fused is None:
        x2 = nc.dram_tensor("x2", [NDC, 128, TLOC], F32, kind="ExternalInput").ap()
        oo = nc.dram_tensor("oo", [NDC, 128, TLOC], F32, kind="ExternalInput").ap()
        uo = nc.dram_tensor("uo", [NDC, 128, TLOC], F32, kind="ExternalInput").ap()
        oidx = uidx = None
    else:
        x2 = fused.x2
        oo = uo = None
        oidx = nc.dram_tensor("oidx", [128, NDC], I32, kind="ExternalInput").ap()
        uidx = nc.dram_tensor("uidx", [128, NDC], I32, kind="ExternalInput").ap()
    wo = nc.dram_tensor("wo", [NDC, 128, NDC * 128], F32, kind="ExternalInput").ap()
    gw = nc.dram_tensor("gw", [128, NDC * 8], F32, kind="ExternalInput").ap()
    w13 = nc.dram_tensor("w13", [NEX, EGHC, 2, 128, NDC * 128], F32, kind="ExternalInput").ap()
    w2s = nc.dram_tensor("w2s", [NEX, EGHC, 128, NDC * 128], F32, kind="ExternalInput").ap()
    onec = _shared_const(nc, fused, "onec", [128, 1])
    oner = _shared_const(nc, fused, "oner", [1, 128])
    cst = _shared_const(nc, fused, "cst", [128, 4])
    x4 = nc.dram_tensor("x4", [NDC, 128, TLOC], F32, kind="ExternalOutput").ap()
    oug = fused.oug if fused is not None else None
    x3s = nc.dram_tensor("x3s", [NDC, 128, TLOC], F32).ap()
    ogd = nc.dram_tensor("ogd", [NDC, 128, TLOC], BF16).ap()
    hscr = nc.dram_tensor("hscr_d", [EGHC, NT, 128, TT], BF16).ap()

    with ExitStack() as st:
        def sb(name, shape, dt):
            return st.enter_context(nc.sbuf_tensor(name, shape, dt,
                                                   allow_name_mangling=True))

        def psb(name, shape):
            return st.enter_context(nc.psum_tensor(name, shape, F32,
                                                   allow_name_mangling=True))

        xn3 = sb("xn3", [128, NDC * TLOC], BF16)
        x3t = sb("x3t", [128, NDC * TT], F32)
        dwsb = sb("dwsb", [8, TLOC], F32)
        gw_s = sb("gw_s", [128, NDC * 8], F32)
        gwsumc = sb("gwsumc", [8, 1], F32)
        onec_s = sb("onec_s", [128, 1], F32)
        oner_s = sb("oner_s", [1, 128], F32)
        cst_s = sb("cst_s", [128, 4], F32)
        OUDT = F32 if fused is None else BF16
        idx_s = sb("idx_s", [128, 2 * NDC], I32)
        _p1 = ExitStack()
        rawsb = _p1.enter_context(nc.sbuf_tensor("rawsb", [8, TT], F32, allow_name_mangling=True))
        lgt = _p1.enter_context(nc.sbuf_tensor("lgt", [8, TT], F32, allow_name_mangling=True))
        exs = _p1.enter_context(nc.sbuf_tensor("exs", [8, TT], F32, allow_name_mangling=True))
        srow = _p1.enter_context(nc.sbuf_tensor("srow", [1, 3 * TT], F32, allow_name_mangling=True))
        scod = _p1.enter_context(nc.sbuf_tensor("scod", [8, TT], F32, allow_name_mangling=True))
        eqt = _p1.enter_context(nc.sbuf_tensor("eqt", [8, TT], F32, allow_name_mangling=True))
        rows = _p1.enter_context(nc.sbuf_tensor("rows", [1, 6 * TT], F32, allow_name_mangling=True))
        sq = _p1.enter_context(nc.sbuf_tensor("sq", [128, 2 * TT], F32, allow_name_mangling=True))
        old = _p1.enter_context(nc.sbuf_tensor("old", [128, 2 * TT], OUDT, allow_name_mangling=True))
        uld = _p1.enter_context(nc.sbuf_tensor("uld", [128, 2 * TT], OUDT, allow_name_mangling=True))
        og = _p1.enter_context(nc.sbuf_tensor("og", [128, 2 * TT], BF16, allow_name_mangling=True))
        wos = _p1.enter_context(nc.sbuf_tensor("wos", [128, 8 * NDC * 128], BF16, allow_name_mangling=True))
        oglt = _p1.enter_context(nc.sbuf_tensor("oglt", [128, 2 * 8 * TT], BF16, allow_name_mangling=True))
        x2t = _p1.enter_context(nc.sbuf_tensor("x2t", [128, 2 * TT], F32, allow_name_mangling=True))
        x3lo = _p1.enter_context(nc.sbuf_tensor("x3lo", [128, 2 * TT], F32, allow_name_mangling=True))
        _p1.close()
        wgbf = sb("wgbf", [128, 2 * NDC * 128], BF16)
        wubf = sb("wubf", [128, 2 * NDC * 128], BF16)
        wdbf = sb("wdbf", [128, EGHC * NDC * 128], BF16)
        hrd = sb("hrd", [128, EGHC * TT], BF16)
        gt = sb("gt", [128, 2 * TT], BF16)
        ut = sb("ut", [128, 2 * TT], BF16)
        hb = sb("hb", [128, 2 * TT], BF16)
        dwrow = sb("dwrow", [1, NT * TT], F32)
        dwbs = sb("dwbs", [128, NT * TT], BF16)

        ps_sum = psb("ps_sum", [128, TT])
        ps_ssq = psb("ps_ssq", [128, TT])
        ps_bc = [psb(f"ps_bc{i}", [128, TT]) for i in range(2)]
        ps_w = [psb(f"ps_w{i}", [128, TT]) for i in range(2)]
        ps_r8 = psb("ps_r8", [128, TT])
        ps_rms = psb("ps_rms", [128, TT])

        def row(name):
            base = {"nmu": 0, "m2": 1, "t1": 2, "var": 3, "rstd": 4, "mrs": 5}[name]
            return rows[0:1, base * TT:(base + 1) * TT]

        def x3sl(dc):
            return x3t[:, dc * TT:(dc + 1) * TT]

        def xn3s(dc, tau):
            return xn3[:, (dc * NT + tau) * TT:(dc * NT + tau + 1) * TT]

        def ogs(k):
            return og[:, (k % 2) * TT:((k % 2) + 1) * TT]

        def emit_ou_loads_tau(e, tau, store_cb=None):
            if True:
                if tau > 0:
                    e.need(f"ogm_{tau-1}_{NDC-1}")
                for ec in range(NDC):
                    k = tau * NDC + ec
                    if k >= 2:
                        e.need(f"osq_{(k-2)//NDC}_{(k-2)%NDC}")
                    if fused is None:
                        e.dma(lambda g, ec=ec, tau=tau, k=k: g.dma_start(
                            out=old[:, (k % 2) * TT:((k % 2) + 1) * TT],
                            in_=oo[ec, :, tau * TT:(tau + 1) * TT]))
                    else:
                        e.dma(lambda g, ec=ec, tau=tau, k=k: g.indirect_dma_start(
                            out=old[:, (k % 2) * TT:((k % 2) + 1) * TT],
                            out_offset=None,
                            in_=oug, element_offset=tau * TT,
                            in_offset=bass.IndirectOffsetOnAxis(
                                ap=idx_s[:, ec:ec + 1], axis=0)))
                    e.dflush(f"old_{tau}_{ec}")
                e.need(f"osq_{tau}_{NDC-1}")
                for ec in range(NDC):
                    k = tau * NDC + ec
                    if k >= 1:
                        e.need(f"ogm_{(k-1)//NDC}_{(k-1)%NDC}")
                    if fused is None:
                        e.dma(lambda g, ec=ec, tau=tau, k=k: g.dma_start(
                            out=old[:, (k % 2) * TT:((k % 2) + 1) * TT],
                            in_=oo[ec, :, tau * TT:(tau + 1) * TT]))
                        e.dma(lambda g, ec=ec, tau=tau, k=k: g.dma_start(
                            out=uld[:, (k % 2) * TT:((k % 2) + 1) * TT],
                            in_=uo[ec, :, tau * TT:(tau + 1) * TT]))
                    else:
                        e.dma(lambda g, ec=ec, tau=tau, k=k: g.indirect_dma_start(
                            out=old[:, (k % 2) * TT:((k % 2) + 1) * TT],
                            out_offset=None,
                            in_=oug, element_offset=tau * TT,
                            in_offset=bass.IndirectOffsetOnAxis(
                                ap=idx_s[:, ec:ec + 1], axis=0)))
                        e.dma(lambda g, ec=ec, tau=tau, k=k: g.indirect_dma_start(
                            out=uld[:, (k % 2) * TT:((k % 2) + 1) * TT],
                            out_offset=None,
                            in_=oug, element_offset=tau * TT,
                            in_offset=bass.IndirectOffsetOnAxis(
                                ap=idx_s[:, NDC + ec:NDC + ec + 1], axis=0)))
                    e.dflush(f"ould_{tau}_{ec}")
                    if store_cb is not None:
                        store_cb(e, tau, ec)

        def emit_ou_loads(e):
            for tau in range(NT):
                emit_ou_loads_tau(e, tau)

        def e_sync(e):
            if fused is None:
                emit_ou_loads(e)
            for tau in range(NT):
                e.need(f"ogfin_{tau}")
                if tau >= 2:
                    e.need(f"wolomm_{tau-2}_{NDC-1}")
                for e8 in range(8):
                    e.dma(lambda g, e8=e8, tau=tau: g.dma_start(
                        out=oglt[:, ((tau % 2) * 8 + e8) * TT:((tau % 2) * 8 + e8 + 1) * TT],
                        in_=ogd[e8, :, tau * TT:(tau + 1) * TT]))
                e.dflush(f"oglod_{tau}")
                for dc in range(NDC):
                    k = tau * NDC + dc
                    if k >= 2:
                        e.need(f"x3evlo_{(k-2)//NDC}_{(k-2)%NDC}")
                    e.dma(lambda g, dc=dc, tau=tau, k=k: g.dma_start(
                        out=x2t[:, (k % 2) * TT:((k % 2) + 1) * TT],
                        in_=x2[dc, :, tau * TT:(tau + 1) * TT]))
                    e.dflush(f"x2ld_{tau}_{dc}")
            for tau in range(NT):
                e.need(f"x3lost_{tau}")
                if tau >= 2:
                    e.need(f"wohimm_{tau-2}_{NDC-1}")
                for e8 in range(8):
                    e.dma(lambda g, e8=e8, tau=tau: g.dma_start(
                        out=oglt[:, ((tau % 2) * 8 + e8) * TT:((tau % 2) * 8 + e8 + 1) * TT],
                        in_=ogd[8 + e8, :, tau * TT:(tau + 1) * TT]))
                e.dflush(f"oghid_{tau}")
                for dc in range(NDC):
                    k = tau * NDC + dc
                    if k >= 2:
                        e.need(f"x3ev_{(k-2)//NDC}_{(k-2)%NDC}")
                    e.dma(lambda g, dc=dc, tau=tau, k=k: g.dma_start(
                        out=x3lo[:, (k % 2) * TT:((k % 2) + 1) * TT],
                        in_=x3s[dc, :, tau * TT:(tau + 1) * TT]))
                    e.dflush(f"x3lold_{tau}_{dc}")
            for ex in range(NEX):
                for tau in range(NT):
                    k = ex * NT + tau
                    if k >= 1:
                        pe, ptau = divmod(k - 1, NT)
                        e.need(f"dmm_{pe}_{ptau}_{NDC-1}")
                    e.need(f"hstf_{ex}")
                    for ic in range(EGHC):
                        e.dma(lambda g, ex=ex, tau=tau, ic=ic: g.dma_start(
                            out=hrd[:, ic * TT:(ic + 1) * TT],
                            in_=hscr[ic, tau]))
                    e.dflush(f"hld_{ex}_{tau}")
                    if ex == 0:
                        if tau == 0:
                            e.need(f"lnapply_{NT-1}")
                            e.need(f"rawmm_{NT-1}")
                            e.need(f"x3st_{NT-1}")
                        else:
                            e.need(f"x4f_{0}_{tau-1}")
                        for dc in range(NDC):
                            e.dma(lambda g, dc=dc, tau=tau: g.dma_start(
                                out=x3sl(dc), in_=x3s[dc, :, tau * TT:(tau + 1) * TT]))
                        e.dflush(f"x3r_{tau}")

        def e_gpsimd(e):
            e.dma(lambda g: g.dma_start(out=onec_s[:, :], in_=onec))
            e.dma(lambda g: g.dma_start(out=oner_s[:, :], in_=oner))
            e.dma(lambda g: g.dma_start(out=cst_s[:, :], in_=cst))
            e.dma(lambda g: g.dma_start(out=gw_s[:, :], in_=gw))
            if fused is not None:
                e.dma(lambda g: g.dma_start(out=idx_s[:, 0:NDC], in_=oidx))
                e.dma(lambda g: g.dma_start(out=idx_s[:, NDC:2 * NDC], in_=uidx))
            e.dflush("const_ld")
            if fused is not None:
                e.need("const_ld")
            def og_store(e2, tau, ec):
                k = tau * NDC + ec
                e2.need(f"ogm_{tau}_{ec}")
                e2.dma(lambda g, ec=ec, tau=tau, k=k: g.dma_start(
                    out=ogd[ec, :, tau * TT:(tau + 1) * TT],
                    in_=og[:, (k % 2) * TT:((k % 2) + 1) * TT]))
                e2.dflush(f"ogstq_{tau}_{ec}")

            for tau in range(NT):
                if fused is not None:
                    emit_ou_loads_tau(e, tau, store_cb=og_store)
                else:
                    for ec in range(NDC):
                        og_store(e, tau, ec)
                e.dflush(f"ogfin_{tau}")
            for e8 in range(8):
                e.dma(lambda g, e8=e8: g.dma_start(
                    out=wos[:, e8 * NDC * 128:(e8 + 1) * NDC * 128], in_=wo[e8]))
            e.dflush("wolo")
            for tau in range(NT):
                for dc in range(NDC):
                    e.need(f"x3evlo_{tau}_{dc}")
                    e.dma(lambda g, dc=dc, tau=tau: g.dma_start(
                        out=x3s[dc, :, tau * TT:(tau + 1) * TT],
                        in_=x3t[:, dc * TT:(dc + 1) * TT]))
                e.dflush(f"x3lost_{tau}")
            e.need(f"wolomm_{NT-1}_{NDC-1}")
            for e8 in range(8):
                e.dma(lambda g, e8=e8: g.dma_start(
                    out=wos[:, e8 * NDC * 128:(e8 + 1) * NDC * 128], in_=wo[8 + e8]))
            e.dflush("wohi")
            for tau in range(NT):
                for dc in range(NDC):
                    e.need(f"x3ev_{tau}_{dc}")
                    e.need(f"x3sq_{tau}_{dc}")
                    e.dma(lambda g, dc=dc, tau=tau: g.dma_start(
                        out=x3s[dc, :, tau * TT:(tau + 1) * TT],
                        in_=x3t[:, dc * TT:(dc + 1) * TT]))
                e.dflush(f"x3st_{tau}")
                e.need(f"lgt_{tau}")
                e.sig(lambda g: g.tensor_reduce(
                    out=srow[0:1, 0:TT], in_=lgt[:, :], axis=mybir.AxisListType.C,
                    op=Op.max), f"m1_{tau}")
                e.need(f"exs_{tau}")
                e.sig(lambda g: g.tensor_reduce(
                    out=srow[0:1, TT:2 * TT], in_=exs[:, :], axis=mybir.AxisListType.C,
                    op=Op.add), f"sume_{tau}")
                e.need(f"mskt_{tau}")
                e.sig(lambda g: g.tensor_reduce(
                    out=srow[0:1, 2 * TT:3 * TT], in_=eqt[:, :], axis=mybir.AxisListType.C,
                    op=Op.max), f"m2_{tau}")
            for ex in range(NEX):
                if ex < 8:
                    if ex > 0:
                        e.need(f"dwbs_{ex-1}_{NT-1}")
                    for tau in range(NT):
                        e.need(f"dw_{tau}")
                        e.dma(lambda g, ex=ex, tau=tau: g.dma_start(
                            out=dwrow[:, tau * TT:(tau + 1) * TT],
                            in_=dwsb[ex:ex + 1, tau * TT:(tau + 1) * TT]))
                    e.dflush(f"dwr_{ex}")
                for ihc in range(EGHC):
                    hc = ex * EGHC + ihc
                    ws = (hc % 2) * NDC * 128
                    if hc >= 2:
                        e.need(f"gmm_{hc-2}_{NT-1}")
                    e.dma(lambda g, ex=ex, ihc=ihc, ws=ws: g.dma_start(
                        out=wgbf[:, ws:ws + NDC * 128], in_=w13[ex, ihc, 0]))
                    e.dflush(f"wgld_{hc}")
                    if hc >= 2:
                        e.need(f"umm_{hc-2}_{NT-1}")
                    e.dma(lambda g, ex=ex, ihc=ihc, ws=ws: g.dma_start(
                        out=wubf[:, ws:ws + NDC * 128], in_=w13[ex, ihc, 1]))
                    e.dflush(f"wuld_{hc}")
                    for tau in range(NT):
                        pg = (hc * NT + tau) % 2
                        e.need(f"hmul_{hc}_{tau}")
                        if ex > 0:
                            e.need(f"hld_{ex-1}_{tau}")
                        e.dma(lambda g, ihc=ihc, tau=tau, pg=pg: g.dma_start(
                            out=hscr[ihc, tau], in_=hb[:, pg * TT:(pg + 1) * TT]))
                        e.dflush(f"hstq_{hc}_{tau}")
                e.dflush(f"hstf_{ex}")
                if ex > 0:
                    e.need(f"dmm_{ex-1}_{NT-1}_{NDC-1}")
                for ic in range(EGHC):
                    e.dma(lambda g, ex=ex, ic=ic: g.dma_start(
                        out=wdbf[:, ic * NDC * 128:(ic + 1) * NDC * 128],
                        in_=w2s[ex, ic]))
                    e.dflush(f"wdld_{ex}_{ic}")
                for tau in range(NT):
                    for dc in range(NDC):
                        e.need(f"devac_{ex}_{tau}_{dc}")
                        if ex == 0:
                            e.dma(lambda g, dc=dc, tau=tau: g.dma_start(
                                out=x4[dc, :, tau * TT:(tau + 1) * TT],
                                in_=x3sl(dc)))
                        else:
                            e.dma(lambda g, dc=dc, tau=tau: g.dma_start(
                                out=x4[dc, :, tau * TT:(tau + 1) * TT],
                                in_=x3sl(dc), accum_op=Op.add))
                    e.dflush(f"x4f_{ex}_{tau}")

        def e_tensor(e):
            e.need("const_ld")
            for dc in range(NDC):
                e.sig(lambda t, dc=dc: t.matmul(
                    ps_r8[0:8, 0:1], lhsT=gw_s[:, dc * 8:(dc + 1) * 8], rhs=onec_s[:, :],
                    start=(dc == 0), stop=(dc == NDC - 1)),
                    "gwsum_mm" if dc == NDC - 1 else None)
            for tau in range(NT):
                if tau > 0:
                    e.need(f"rmsrd_{tau-1}")
                for ec in range(NDC):
                    e.need(f"osq_{tau}_{ec}")
                    e.sig(lambda t, ec=ec: t.matmul(
                        ps_rms[0:1, :], lhsT=onec_s[:, :],
                        rhs=sq[:, (ec % 2) * TT:((ec % 2) + 1) * TT],
                        start=(ec == 0), stop=(ec == NDC - 1)),
                        f"rmsmm_{tau}_{ec}" if ec < NDC - 1 else f"rmsstat_{tau}")
                e.need(f"rrow_{tau}")
                if tau > 0:
                    e.need(f"ogm_{tau-1}_{NDC-1}")
                e.sig(lambda t: t.matmul(
                    ps_bc[0][:, :], lhsT=oner_s[:, :], rhs=row("rstd"),
                    start=True, stop=True), f"rbb_{tau}")
            e.need("wolo")
            for tau in range(NT):
                e.need(f"oglod_{tau}")
                for dc in range(NDC):
                    k = tau * NDC + dc
                    pw = k % 2
                    if k >= 2:
                        e.need(f"x3evlo_{(k-2)//NDC}_{(k-2)%NDC}")
                    for e8 in range(8):
                        e.sig(lambda t, dc=dc, tau=tau, e8=e8, pw=pw: t.matmul(
                            ps_w[pw][:, :],
                            lhsT=wos[:, (e8 * NDC + dc) * 128:(e8 * NDC + dc + 1) * 128],
                            rhs=oglt[:, ((tau % 2) * 8 + e8) * TT:((tau % 2) * 8 + e8 + 1) * TT],
                            start=(e8 == 0), stop=(e8 == 7)),
                            f"wolomm_{tau}_{dc}" if e8 == 7 else None)
            e.need("wohi")
            for tau in range(NT):
                e.need(f"oghid_{tau}")
                if tau > 0:
                    e.need(f"rowrd_{tau-1}")
                    e.need(f"rawrd_{tau-1}")
                for dc in range(NDC):
                    k = tau * NDC + dc
                    pw = k % 2
                    if k >= 2:
                        e.need(f"x3ev_{(k-2)//NDC}_{(k-2)%NDC}")
                    for e8 in range(8):
                        e.sig(lambda t, dc=dc, tau=tau, e8=e8, pw=pw: t.matmul(
                            ps_w[pw][:, :],
                            lhsT=wos[:, (e8 * NDC + dc) * 128:(e8 * NDC + dc + 1) * 128],
                            rhs=oglt[:, ((tau % 2) * 8 + e8) * TT:((tau % 2) * 8 + e8 + 1) * TT],
                            start=(e8 == 0), stop=(e8 == 7)),
                            f"wohimm_{tau}_{dc}" if e8 == 7 else None)
                for dc in range(NDC):
                    e.need(f"x3ev_{tau}_{dc}")
                    e.run(lambda t, dc=dc: t.matmul(
                        ps_sum[0:1, :], lhsT=onec_s[:, :], rhs=x3sl(dc),
                        start=(dc == 0), stop=(dc == NDC - 1)))
                    e.need(f"x3sq_{tau}_{dc}")
                    e.sig(lambda t, dc=dc: t.matmul(
                        ps_ssq[0:1, :], lhsT=onec_s[:, :],
                        rhs=sq[:, (dc % 2) * TT:((dc % 2) + 1) * TT],
                        start=(dc == 0), stop=(dc == NDC - 1)),
                        f"ssqmm_{tau}_{dc}" if dc < NDC - 1 else f"stats_{tau}")
                    e.sig(lambda t, dc=dc: t.matmul(
                        ps_r8[0:8, 0:TT], lhsT=gw_s[:, dc * 8:(dc + 1) * 8], rhs=x3sl(dc),
                        start=(dc == 0), stop=(dc == NDC - 1)),
                        f"rawmm_{tau}" if dc == NDC - 1 else None)
                e.need(f"rstd_{tau}")
                e.need(f"mrs_{tau}")
                if tau > 0:
                    e.need(f"lnapply_{tau-1}")
                e.sig(lambda t: t.matmul(
                    ps_bc[0][:, :], lhsT=oner_s[:, :], rhs=row("rstd"),
                    start=True, stop=True), f"rstdb_{tau}")
                e.sig(lambda t: t.matmul(
                    ps_bc[1][:, :], lhsT=oner_s[:, :], rhs=row("mrs"),
                    start=True, stop=True), f"mrsb_{tau}")
                e.need(f"rawrd_{tau}")
                e.need(f"m1_{tau}")
                e.sig(lambda t: t.matmul(
                    ps_r8[0:8, 0:TT], lhsT=oner_s[0:1, 0:8], rhs=srow[0:1, 0:TT],
                    start=True, stop=True), f"m1b_{tau}")
                e.need(f"recip_{tau}")
                e.need(f"exsub_{tau}")
                e.sig(lambda t: t.matmul(
                    ps_r8[0:8, 0:TT], lhsT=oner_s[0:1, 0:8], rhs=srow[0:1, TT:2 * TT],
                    start=True, stop=True), f"recb_{tau}")
                e.need(f"scomul_{tau}")
                e.sig(lambda t: t.matmul(
                    ps_r8[0:8, 0:TT], lhsT=oner_s[0:1, 0:8], rhs=srow[0:1, 0:TT],
                    start=True, stop=True), f"m1sb_{tau}")
                e.need(f"eqm_{tau}")
                e.need(f"m2_{tau}")
                e.sig(lambda t: t.matmul(
                    ps_r8[0:8, 0:TT], lhsT=oner_s[0:1, 0:8], rhs=srow[0:1, 2 * TT:3 * TT],
                    start=True, stop=True), f"m2b_{tau}")
            for ex in range(NEX):
                if ex < 8:
                    e.need(f"dwr_{ex}")
                    for tau in range(NT):
                        if ex > 0:
                            e.need(f"hmul_{(ex-1)*EGHC+EGHC-1}_{NT-1}")
                        if tau > 0:
                            e.need(f"dwbs_{ex}_{tau-1}")
                        elif ex > 0:
                            e.need(f"dwbs_{ex-1}_{NT-1}")
                        e.sig(lambda t, tau=tau: t.matmul(
                            ps_r8[:, 0:TT], lhsT=oner_s[:, :],
                            rhs=dwrow[0:1, tau * TT:(tau + 1) * TT],
                            start=True, stop=True), f"dwbmm_{ex}_{tau}")
                for ihc in range(EGHC):
                    hc = ex * EGHC + ihc
                    ws = (hc % 2) * NDC * 128
                    e.need(f"wgld_{hc}")
                    e.need(f"wuld_{hc}")
                    for tau in range(NT):
                        k = hc * NT + tau
                        e.need(f"lnapply_{tau}")
                        if k >= 2:
                            pk = k - 2
                            e.need(f"gevac_{pk//NT}_{pk%NT}")
                        for dc in range(NDC):
                            e.sig(lambda t, dc=dc, tau=tau, ws=ws, k=k: t.matmul(
                                (ps_bc[0] if k % 2 == 0 else ps_sum)[:, :],
                                lhsT=wgbf[:, ws + dc * 128:ws + (dc + 1) * 128],
                                rhs=xn3s(dc, tau),
                                start=(dc == 0), stop=(dc == NDC - 1)),
                                f"gmm_{hc}_{tau}" if dc == NDC - 1 else None)
                        if k >= 2:
                            pk = k - 2
                            e.need(f"uevac_{pk//NT}_{pk%NT}")
                        for dc in range(NDC):
                            e.sig(lambda t, dc=dc, tau=tau, ws=ws, k=k: t.matmul(
                                (ps_bc[1] if k % 2 == 0 else ps_ssq)[:, :],
                                lhsT=wubf[:, ws + dc * 128:ws + (dc + 1) * 128],
                                rhs=xn3s(dc, tau),
                                start=(dc == 0), stop=(dc == NDC - 1)),
                                f"umm_{hc}_{tau}" if dc == NDC - 1 else None)
                for tau in range(NT):
                    e.need(f"hld_{ex}_{tau}")
                    for dc in range(NDC):
                        k = ex * NT * NDC + tau * NDC + dc
                        pd = k % 2
                        if k >= 2:
                            pk = k - 2
                            e.need(f"devac_{pk//(NT*NDC)}_{(pk%(NT*NDC))//NDC}_{pk%NDC}")
                        for ic in range(EGHC):
                            e.need(f"wdld_{ex}_{ic}")
                            e.sig(lambda t, dc=dc, ic=ic, pd=pd: t.matmul(
                                ps_w[pd][:, :],
                                lhsT=wdbf[:, (ic * NDC + dc) * 128:(ic * NDC + dc + 1) * 128],
                                rhs=hrd[:, ic * TT:(ic + 1) * TT],
                                start=(ic == 0), stop=(ic == EGHC - 1)),
                                f"dmm_{ex}_{tau}_{dc}" if ic == EGHC - 1 else None)

        def e_scalar(e):
            e.need("const_ld")
            for tau in range(NT):
                for ec in range(NDC):
                    k = tau * NDC + ec
                    e.need(f"old_{tau}_{ec}")
                    if k >= 2:
                        pk = k - 2
                        e.need(f"rmsmm_{pk//NDC}_{pk%NDC}" if pk % NDC < NDC - 1
                               else f"rmsstat_{pk//NDC}")
                    e.sig(lambda s, k=k: s.activation(
                        out=sq[:, (k % 2) * TT:((k % 2) + 1) * TT],
                        in_=old[:, (k % 2) * TT:((k % 2) + 1) * TT], func=AF.Square),
                        f"osq_{tau}_{ec}")
                e.need(f"rmsrow_{tau}")
                if tau > 0:
                    e.need(f"rbb_{tau-1}")
                e.sig(lambda s: s.activation(
                    out=row("t1"), in_=row("var"), func=AF.Ln, bias=cst_s[0:1, 1:2]))
                e.fence()
                e.sig(lambda s: s.activation(
                    out=row("rstd"), in_=row("t1"), func=AF.Exp, scale=-0.5),
                    f"rrow_{tau}")
            for tau in range(NT):
                for dc in range(NDC):
                    k = tau * NDC + dc
                    e.need(f"x3ev_{tau}_{dc}")
                    if k >= 2:
                        pk = k - 2
                        e.need(f"ssqmm_{pk//NDC}_{pk%NDC}" if pk % NDC < NDC - 1
                               else f"stats_{pk//NDC}")
                    e.sig(lambda s, dc=dc, k=k: s.activation(
                        out=sq[:, (k % 2) * TT:((k % 2) + 1) * TT],
                        in_=x3sl(dc), func=AF.Square), f"x3sq_{tau}_{dc}")
                e.need(f"var_{tau}")
                if tau > 0:
                    e.need(f"rstdb_{tau-1}")
                e.sig(lambda s: s.activation(
                    out=row("t1"), in_=row("var"), func=AF.Ln, bias=cst_s[0:1, 0:1]))
                e.fence()
                e.sig(lambda s: s.activation(
                    out=row("rstd"), in_=row("t1"), func=AF.Exp, scale=-0.5),
                    f"rstd_{tau}")
                e.need(f"exsub_{tau}")
                e.fence()
                e.sig(lambda s: s.activation(
                    out=exs[:, :], in_=scod[:, :], func=AF.Exp), f"exs_{tau}")
            for ex in range(NEX):
                if ex < 8:
                    for tau in range(NT):
                        e.need(f"dwbmm_{ex}_{tau}")
                        e.sig(lambda s, tau=tau: s.activation(
                            out=dwbs[:, tau * TT:(tau + 1) * TT], in_=ps_r8[:, 0:TT],
                            func=AF.Identity), f"dwbs_{ex}_{tau}")
                for ihc in range(EGHC):
                    hc = ex * EGHC + ihc
                    for tau in range(NT):
                        k = hc * NT + tau
                        pg = k % 2
                        e.need(f"gmm_{hc}_{tau}")
                        if k >= 2:
                            e.need(f"hmul_{(k-2)//NT}_{(k-2)%NT}")
                        e.sig(lambda s, pg=pg, k=k: s.activation(
                            out=gt[:, pg * TT:(pg + 1) * TT],
                            in_=(ps_bc[0] if k % 2 == 0 else ps_sum)[:, :],
                            func=AF.Silu), f"gevac_{hc}_{tau}")
                        e.need(f"umm_{hc}_{tau}")
                        e.sig(lambda s, pg=pg, k=k: s.activation(
                            out=ut[:, pg * TT:(pg + 1) * TT],
                            in_=(ps_bc[1] if k % 2 == 0 else ps_ssq)[:, :],
                            func=AF.Identity), f"uevac_{hc}_{tau}")

        def e_vector(e):
            e.need("gwsum_mm")
            e.fence()
            e.sig(lambda v: v.tensor_copy(gwsumc[:, :], ps_r8[0:8, 0:1]), "gwsum_rd")
            for tau in range(NT):
                e.need(f"rmsstat_{tau}")
                e.fence()
                e.sig(lambda v: v.tensor_scalar(
                    out=row("var"), in0=ps_rms[0:1, :], scalar1=1.0 / D, scalar2=None,
                    op0=Op.mult), f"rmsrow_{tau}")
                e.sig(lambda v: v.engine_nop(), f"rmsrd_{tau}")
                e.need(f"rbb_{tau}")
                for ec in range(NDC):
                    k = tau * NDC + ec
                    e.need(f"ould_{tau}_{ec}")
                    if k >= 2:
                        e.need(f"ogstq_{(k-2)//NDC}_{(k-2)%NDC}")
                    e.fence()
                    e.sig(lambda v, k=k: v.tensor_tensor(
                        out=uld[:, (k % 2) * TT:((k % 2) + 1) * TT],
                        in0=old[:, (k % 2) * TT:((k % 2) + 1) * TT],
                        in1=uld[:, (k % 2) * TT:((k % 2) + 1) * TT], op=Op.mult))
                    e.fence()
                    e.sig(lambda v, k=k: v.tensor_tensor(
                        out=og[:, (k % 2) * TT:((k % 2) + 1) * TT],
                        in0=uld[:, (k % 2) * TT:((k % 2) + 1) * TT],
                        in1=ps_bc[0][:, :], op=Op.mult), f"ogm_{tau}_{ec}")
            for tau in range(NT):
                if tau > 0:
                    e.need(f"x3lost_{tau-1}")
                for dc in range(NDC):
                    k = tau * NDC + dc
                    pw = k % 2
                    e.need(f"wolomm_{tau}_{dc}")
                    e.need(f"x2ld_{tau}_{dc}")
                    e.fence()
                    e.sig(lambda v, dc=dc, pw=pw, k=k: v.tensor_tensor(
                        out=x3sl(dc), in0=ps_w[pw][:, :],
                        in1=x2t[:, (k % 2) * TT:((k % 2) + 1) * TT], op=Op.add),
                        f"x3evlo_{tau}_{dc}")
            for tau in range(NT):
                if tau == 0:
                    e.need(f"x3lost_{NT-1}")
                else:
                    e.need(f"x3st_{tau-1}")
                for dc in range(NDC):
                    k = tau * NDC + dc
                    pw = k % 2
                    e.need(f"wohimm_{tau}_{dc}")
                    e.need(f"x3lold_{tau}_{dc}")
                    e.fence()
                    e.sig(lambda v, dc=dc, pw=pw, k=k: v.tensor_tensor(
                        out=x3sl(dc), in0=ps_w[pw][:, :],
                        in1=x3lo[:, (k % 2) * TT:((k % 2) + 1) * TT], op=Op.add),
                        f"x3ev_{tau}_{dc}")
                e.need(f"stats_{tau}")
                e.fence()
                e.sig(lambda v: v.tensor_scalar(
                    out=row("nmu"), in0=ps_sum[0:1, :], scalar1=-1.0 / D,
                    scalar2=None, op0=Op.mult))
                e.sig(lambda v: v.tensor_scalar(
                    out=row("m2"), in0=ps_ssq[0:1, :], scalar1=1.0 / D,
                    scalar2=None, op0=Op.mult), f"rowrd_{tau}")
                e.fence()
                e.sig(lambda v: v.tensor_tensor(
                    out=row("t1"), in0=row("nmu"), in1=row("nmu"), op=Op.mult))
                e.fence()
                e.sig(lambda v: v.tensor_tensor(
                    out=row("var"), in0=row("m2"), in1=row("t1"), op=Op.subtract),
                    f"var_{tau}")
                e.need(f"rstd_{tau}")
                if tau > 0:
                    e.need(f"mrsb_{tau-1}")
                e.fence()
                e.sig(lambda v: v.tensor_tensor(
                    out=row("mrs"), in0=row("nmu"), in1=row("rstd"), op=Op.mult),
                    f"mrs_{tau}")
                e.need(f"rstdb_{tau}")
                e.need(f"mrsb_{tau}")
                for dc in range(NDC):
                    e.fence()
                    e.sig(lambda v, dc=dc: v.tensor_tensor(
                        out=sq[:, 0:TT], in0=x3sl(dc), in1=ps_bc[0][:, :], op=Op.mult))
                    e.fence()
                    e.sig(lambda v, dc=dc, tau=tau: v.tensor_tensor(
                        out=xn3s(dc, tau), in0=sq[:, 0:TT], in1=ps_bc[1][:, :],
                        op=Op.add), f"lnapply_{tau}" if dc == NDC - 1 else None)
                e.need(f"rawmm_{tau}")
                e.fence()
                e.sig(lambda v: v.tensor_copy(rawsb[:, :], ps_r8[0:8, 0:TT]),
                      f"rawrd_{tau}")
                e.fence()
                e.sig(lambda v: v.tensor_tensor(
                    out=rawsb[:, :], in0=rawsb[:, :], in1=ps_bc[0][0:8, :], op=Op.mult))
                e.fence()
                e.sig(lambda v: v.scalar_tensor_tensor(
                    out=lgt[:, :], in0=ps_bc[1][0:8, :], scalar=gwsumc[:, :],
                    in1=rawsb[:, :], op0=Op.mult, op1=Op.add), f"lgt_{tau}")
                e.need(f"m1b_{tau}")
                e.fence()
                e.sig(lambda v: v.tensor_tensor(
                    out=scod[:, :], in0=lgt[:, :], in1=ps_r8[0:8, 0:TT], op=Op.subtract),
                    f"exsub_{tau}")
                e.need(f"sume_{tau}")
                e.fence()
                e.sig(lambda v: v.reciprocal(
                    out=srow[0:1, TT:2 * TT], in_=srow[0:1, TT:2 * TT]), f"recip_{tau}")
                e.need(f"recb_{tau}")
                e.need(f"exs_{tau}")
                e.fence()
                e.sig(lambda v: v.tensor_tensor(
                    out=scod[:, :], in0=exs[:, :], in1=ps_r8[0:8, 0:TT], op=Op.mult),
                    f"scomul_{tau}")
                e.need(f"m1sb_{tau}")
                e.fence()
                e.sig(lambda v: v.tensor_tensor(
                    out=eqt[:, :], in0=lgt[:, :], in1=ps_r8[0:8, 0:TT], op=Op.is_equal),
                    f"eqm_{tau}")
                e.fence()
                e.sig(lambda v: v.scalar_tensor_tensor(
                    out=eqt[:, :], in0=eqt[:, :], scalar=-1e30, in1=scod[:, :],
                    op0=Op.mult, op1=Op.add), f"mskt_{tau}")
                e.need(f"m2b_{tau}")
                e.fence()
                e.sig(lambda v: v.tensor_tensor(
                    out=eqt[:, :], in0=scod[:, :], in1=ps_r8[0:8, 0:TT], op=Op.is_ge))
                e.fence()
                e.sig(lambda v, tau=tau: v.tensor_tensor(
                    out=dwsb[:, tau * TT:(tau + 1) * TT], in0=scod[:, :], in1=eqt[:, :],
                    op=Op.mult), f"dw_{tau}")
            for ex in range(NEX):
                for ihc in range(EGHC):
                    hc = ex * EGHC + ihc
                    for tau in range(NT):
                        k = hc * NT + tau
                        pg = k % 2
                        e.need(f"gevac_{hc}_{tau}")
                        e.need(f"uevac_{hc}_{tau}")
                        if k >= 2:
                            e.need(f"hstq_{(k-2)//NT}_{(k-2)%NT}")
                        if ex < 8:
                            e.need(f"dwbs_{ex}_{tau}")
                            e.fence()
                            e.sig(lambda v, pg=pg: v.tensor_tensor(
                                out=gt[:, pg * TT:(pg + 1) * TT],
                                in0=gt[:, pg * TT:(pg + 1) * TT],
                                in1=ut[:, pg * TT:(pg + 1) * TT], op=Op.mult))
                            e.fence()
                            e.sig(lambda v, pg=pg, tau=tau: v.tensor_tensor(
                                out=hb[:, pg * TT:(pg + 1) * TT],
                                in0=gt[:, pg * TT:(pg + 1) * TT],
                                in1=dwbs[:, tau * TT:(tau + 1) * TT],
                                op=Op.mult), f"hmul_{hc}_{tau}")
                        else:
                            e.fence()
                            e.sig(lambda v, pg=pg: v.tensor_tensor(
                                out=hb[:, pg * TT:(pg + 1) * TT],
                                in0=gt[:, pg * TT:(pg + 1) * TT],
                                in1=ut[:, pg * TT:(pg + 1) * TT], op=Op.mult),
                                f"hmul_{hc}_{tau}")
                for tau in range(NT):
                    for dc in range(NDC):
                        k = ex * NT * NDC + tau * NDC + dc
                        pd = k % 2
                        e.need(f"dmm_{ex}_{tau}_{dc}")
                        if ex == 0:
                            e.need(f"x3r_{tau}")
                            e.fence()
                            e.sig(lambda v, pd=pd, dc=dc: v.tensor_tensor(
                                out=x3sl(dc), in0=ps_w[pd][:, :], in1=x3sl(dc),
                                op=Op.add), f"devac_{ex}_{tau}_{dc}")
                        else:
                            if tau > 0:
                                e.need(f"x4f_{ex}_{tau-1}")
                            else:
                                e.need(f"x4f_{ex-1}_{NT-1}")
                            e.fence()
                            e.sig(lambda v, pd=pd, dc=dc: v.tensor_copy(
                                x3sl(dc), ps_w[pd][:, :]), f"devac_{ex}_{tau}_{dc}")

        ems = {"sync": e_sync, "tensor": e_tensor, "vector": e_vector,
               "scalar": e_scalar, "gpsimd": e_gpsimd}
        if fused is not None:
            return ems
        build_program(nc, ems)
    return nc


def tail_inputs(inp):
    Wout = inp["L1_attn_W_out"]
    wo = np.ascontiguousarray(
        Wout.reshape(NDC, 128, NDC, 128).transpose(2, 3, 0, 1).reshape(NDC, 128, NDC * 128))
    gwm = inp["L1_ffn_gate_w"]
    gws = np.ascontiguousarray(gwm.reshape(8, NDC, 128).transpose(1, 2, 0).reshape(128 * NDC, 8))
    gws = np.ascontiguousarray(gwm.T.reshape(NDC, 128, 8).transpose(1, 0, 2).reshape(128, NDC * 8))
    def lhsT_strips(w, nmc):
        return np.ascontiguousarray(
            w.reshape(nmc, 128, NDC, 128).transpose(0, 3, 2, 1).reshape(nmc, 128, NDC * 128))
    w13 = np.zeros((NEX, EGHC, 2, 128, NDC * 128), np.float32)
    w2 = np.zeros((NEX, EGHC, 128, NDC * 128), np.float32)
    for ex in range(8):
        w13[ex, :, 0] = lhsT_strips(inp["L1_ffn_w1"][ex], EGHC)
        w13[ex, :, 1] = lhsT_strips(inp["L1_ffn_w3"][ex], EGHC)
        w2[ex] = np.ascontiguousarray(
            inp["L1_ffn_w2"][ex].reshape(NDC, 128, EGHC, 128).transpose(2, 3, 0, 1)
            .reshape(EGHC, 128, NDC * 128))
    for half in range(2):
        sl = slice(half * 1408, (half + 1) * 1408)
        w13[8 + half, :, 0] = lhsT_strips(inp["L1_ffn_sw1"][sl], EGHC)
        w13[8 + half, :, 1] = lhsT_strips(inp["L1_ffn_sw3"][sl], EGHC)
        w2[8 + half] = np.ascontiguousarray(
            inp["L1_ffn_sw2"][:, sl].reshape(NDC, 128, EGHC, 128).transpose(2, 3, 0, 1)
            .reshape(EGHC, 128, NDC * 128))
    return {
        "wo": wo, "gw": gws, "w13": w13, "w2s": w2,
        "onec": np.ones((128, 1), np.float32),
        "oner": np.ones((1, 128), np.float32),
        "cst": np.tile(np.array([[1e-5, 1e-6, 0.0, 0.0]], np.float32), (128, 1)),
    }


def run_tail(x2_full, o_full, u_full, inp):
    nc = build_tail()
    shared = tail_inputs(inp)
    in_maps = []
    for c in range(8):
        b, h = divmod(c, 2)
        m = dict(shared)
        m["x2"] = fm(x2_full[b, h * TLOC:(h + 1) * TLOC])
        m["oo"] = fm(o_full[b, h * TLOC:(h + 1) * TLOC])
        m["uo"] = fm(u_full[b, h * TLOC:(h + 1) * TLOC])
        in_maps.append(m)
    res = run_bass_kernel_spmd(nc, in_maps, core_ids=list(range(8)))
    out = np.zeros((4, 4096, D), np.float32)
    for c in range(8):
        b, h = divmod(c, 2)
        out[b, h * TLOC:(h + 1) * TLOC] = res.results[c]["x4"].reshape(D, TLOC).T
    return out



PAIRS = [[0, 1], [2, 3], [4, 5], [6, 7]]


def build_fused():
    from concourse.aot_env import allow_ici_all_to_all
    nc = bass.Bass("TRN2", target_bir_lowering=False)
    fio = FusedIO()
    fio.nc = nc
    A = build_mamba(fused=fio)
    B = build_ffn0(fused=fio)
    x2b = nc.dram_tensor("x2b", [NDC * 128, TLOC], BF16).ap()
    xg = nc.dram_tensor("xg", [2 * NDC * 128, TLOC], BF16).ap()
    fio.xg = xg
    C = build_lightning(fused=fio)
    oug = nc.dram_tensor("oug", [2 * 2 * TLOC, TLOC], BF16).ap()
    fio.oug = oug
    D = build_tail(fused=fio)
    stages = {"a": A, "b": B, "c": C, "d": D}

    def bar_end(e, s):
        e.fence()
        e.flush_all()
        e.sig(lambda x: x.nop(), f"{s}fin_{e.name}")

    def bar_start(e, s):
        for eng2 in ENGS:
            e.need(f"{s}fin_{eng2}")

    def glue_x2(e):
        e.dma(lambda g: g.dma_start(
            out=x2b.rearrange("(c p) t -> c p t", p=128), in_=fio.x2))
        e.dflush("x2cast")
        e.coll(lambda g: g.collective_compute(
            "AllGather", Op.bypass, replica_groups=PAIRS,
            ins=[x2b.opt()], outs=[xg.opt()]), "agx2")

    def glue_ou(e):
        e.coll(lambda g: g.collective_compute(
            "AllGather", Op.bypass, replica_groups=PAIRS,
            ins=[fio.oub.opt()], outs=[oug.opt()]), "agou")

    def mk(eng):
        def em(e):
            stages["a"][eng](PrefCtx(e, "a_"))
            bar_end(e, "a")
            bar_start(e, "a")
            stages["b"][eng](PrefCtx(e, "b_"))
            bar_end(e, "b")
            if eng == "gpsimd":
                glue_x2(e)
            bar_start(e, "b")
            e.need("agx2")
            stages["c"][eng](PrefCtx(e, "c_"))
            bar_end(e, "c")
            if eng == "gpsimd":
                glue_ou(e)
            bar_start(e, "c")
            e.need("agou")
            stages["d"][eng](PrefCtx(e, "d_"))
        return em

    build_program(nc, {eng: mk(eng) for eng in ENGS})
    return nc


def fused_inputs(inp):
    shared = {}
    shared.update(mamba_inputs(inp))
    shared.update(ffn0_inputs(inp))
    shared.update(tail_inputs(inp))
    lw = [lightning_inputs(inp, hh) for hh in range(2)]
    return shared, lw


def ou_indices(h):
    oidx = np.zeros((128, NDC), np.int32)
    uidx = np.zeros((128, NDC), np.int32)
    p = np.arange(128)
    for ec in range(NDC):
        slot, hdl = divmod(ec, 8)
        base = slot * 4096 + h * 2048 + hdl * 128
        oidx[:, ec] = base + p
        uidx[:, ec] = base + 1024 + p
    return oidx, uidx


def run_fused(inputs):
    nc = build_fused()
    shared, lw = fused_inputs(inputs)
    x0 = np.asarray(inputs["L0_attn_x"], np.float32)
    in_maps = []
    for c in range(8):
        b, h = divmod(c, 2)
        m = dict(shared)
        m.update(lw[h])
        m.update(mamba_core_inputs(x0, b, h))
        oidx, uidx = ou_indices(h)
        m["oidx"] = oidx
        m["uidx"] = uidx
        in_maps.append(m)
    res = run_bass_kernel_spmd(nc, in_maps, core_ids=list(range(8)))
    out = np.zeros((4, 4096, D), np.float32)
    for c in range(8):
        b, h = divmod(c, 2)
        out[b, h * TLOC:(h + 1) * TLOC] = res.results[c]["x4"].reshape(D, TLOC).T
    return out


def kernel(**inputs):
    return run_fused(inputs)
